# version 01
# baseline (speedup 1.0000x reference)
import contextlib
import numpy as np

import concourse.bass as bass
import concourse.mybir as mybir
from concourse.bass_utils import run_bass_kernel_spmd

F32 = mybir.dt.float32
BF16 = mybir.dt.bfloat16
I32 = mybir.dt.int32
AF = mybir.ActivationFunctionType
ALU = mybir.AluOpType
AX = mybir.AxisListType

T = 4096
C = 1024
NCC = 8
TCH = 512
NTC = T // TCH
ENGINES = ("tensor", "scalar", "vector", "gpsimd", "sync")
BAR = {"tensor": "b_t", "scalar": "b_s", "vector": "b_v", "gpsimd": "b_g", "sync": "b_y"}
RSQ128 = float(1.0 / np.sqrt(128.0))

DRAM_REG = {
    "x_in":      ([C, T], F32),
    "out_f":     ([C, T], F32),
    "xs":        ([C, T], BF16),
    "mod_rows":  ([2, 6144], F32),
    "mod_cols":  ([2, 2, 128, 8], F32),
    "aq":        ([3 * C, T], BF16),
    "l1f_h":     ([C, T], BF16),
    "l1f_q":     ([C, T], BF16),
    "l1f_qh":    ([C, T], BF16),
    "l1f_kh":    ([C, T], BF16),
    "l1f_vh":    ([C, T], BF16),
    "l1f_ctx":   ([C, T], BF16),
    "n0": ([C, T], BF16), "n1": ([C, T], BF16), "n2": ([C, T], BF16),
    "n3": ([C, T], BF16), "n4": ([C, T], BF16), "n5": ([C, T], BF16),
    "dbg_a":     ([C, T], F32),
    "dbg_b":     ([2816, T], BF16),
    "wgq":       ([C, 2816], BF16),
    "wuq":       ([C, 2816], BF16),
    "wdq":       ([2816, C], BF16),
    "w_qkv0":    ([C, 3072], F32),
    "w_proj0":   ([C, C], F32),
    "bproj0":    ([C], F32),
    "wg_in":     ([2816, C], F32),
    "wu_in":     ([2816, C], F32),
    "wd_in":     ([C, 2816], F32),
    "c0_in":     ([C], F32),
    "wmod0":     ([C, 6144], F32),
    "w_qkv1":    ([C, 3072], F32),
    "w_out1":    ([C, C], F32),
    "c1_in":     ([C], F32),
    "wmod1":     ([C, 6144], F32),
    "we_in":     ([20, C, C], F32),
    "be_in":     ([20, C], F32),
    "wq_in":     ([6, C, C], F32),
    "wk_in":     ([6, C, C], F32),
    "wv_in":     ([6, C, C], F32),
    "wo_in":     ([6, C, C], F32),
    "ew_in":     ([18], F32),
    "actw_in":   ([6], F32),
    "lng_in":    ([C], F32),
    "lnb_in":    ([C], F32),
    "ident_in":  ([128, 128], F32),
}
EXTERNAL_IN = {"x_in", "w_qkv0", "w_proj0", "bproj0", "wg_in", "wu_in", "wd_in",
               "c0_in", "wmod0", "w_qkv1", "w_out1", "c1_in", "wmod1", "we_in",
               "be_in", "wq_in", "wk_in", "wv_in", "wo_in", "ew_in", "actw_in",
               "lng_in", "lnb_in", "ident_in"}

DAG = []
_lind = 0
for _c in range(6):
    _nsrc = min(_c + 1, 5)
    _snode = _c - _nsrc
    DAG.append((_lind + _nsrc - 1, _lind, _lind + _nsrc // 2,
                _c - 1, _snode, _snode + _nsrc // 2))
    _lind += _nsrc
NODE_NAME = {-1: "l1f_h", 0: "n0", 1: "n1", 2: "n2", 3: "n3", 4: "n4", 5: "n5"}


def stage(reads, writes):
    def deco(fn):
        fn.reads = reads
        fn.writes = writes
        return fn
    return deco


class Ctx:
    def __init__(self, nc, stack, rd_aps, wr_aps):
        self.nc = nc
        self.stack = stack
        self._rd = rd_aps
        self._wr = wr_aps
        self.sems = {}
        self.cnt = {}
        for name in ("wl", "al", "st", "pe", "act", "dve", "g2",
                     "b_t", "b_s", "b_v", "b_g", "b_y"):
            self.sems[name] = stack.enter_context(nc.semaphore(name))
            self.cnt[name] = 0
        self.bar_i = 0
        self.ones_row = self.sb(stack, "ones_row", [1, 512], BF16)
        self.ones_row_f = self.sb(stack, "ones_row_f", [1, 512], F32)
        self.negones_row_f = self.sb(stack, "negones_row_f", [1, 128], F32)
        self.ones_col = self.sb(stack, "ones_col", [128, 1], BF16)
        self.ident = self.sb(stack, "ident", [128, 128], BF16)
        self.ident_f = self.sb(stack, "ident_f", [128, 128], F32)
        self.cvals = {}
        for v in (1e-5, 1e-6, 1e-20):
            self.cvals[v] = self.sb(stack, "cval", [128, 1], F32)

    def cb(self, val):
        return self.cvals[val][:, :]

    _uid = 0

    def sb(self, stack, name, shape, dt):
        Ctx._uid += 1
        return stack.enter_context(self.nc.sbuf_tensor(f"{name}_{Ctx._uid}", shape, dt))

    def ps(self, stack, name, shape, dt):
        Ctx._uid += 1
        return stack.enter_context(self.nc.psum_tensor(f"{name}_{Ctx._uid}", shape, dt))

    def rd(self, name):
        return self._rd[name]

    def wr(self, name):
        return self._wr[name]

    def s(self, name):
        return self.sems[name]

    def bump(self, name, n=1):
        self.cnt[name] += n
        return self.cnt[name]

    def barrier(self):
        nc = self.nc
        self.bar_i += 1
        i = self.bar_i
        dma_targets = [(self.s(n), self.cnt[n]) for n in ("wl", "al", "st", "g2")]
        with nc.Block(name=f"bar{i}") as block:
            def mk(engname):
                def body(eng):
                    if engname == "sync":
                        for sem, v in dma_targets:
                            if v:
                                eng.wait_ge(sem, v)
                    eng.nop().then_inc(self.s(BAR[engname]), 1)
                    for other in ENGINES:
                        if other != engname:
                            eng.wait_ge(self.s(BAR[other]), i)
                return body
            block.tensor(mk("tensor"))
            block.scalar(mk("scalar"))
            block.vector(mk("vector"))
            block.gpsimd(mk("gpsimd"))
            block.sync(mk("sync"))
        for n in BAR.values():
            self.cnt[n] += 1


class Prog:

    def __init__(self, ctx):
        self.ctx = ctx
        self.ops = {e: [] for e in ENGINES}

    def add(self, eng, fn, waits=(), inc=None, inc_n=1):
        v = None
        incs = []
        if inc is not None:
            v = self.ctx.bump(inc, inc_n)
            incs = [(inc, inc_n)]
        self.ops[eng].append((fn, list(waits), incs))
        return v

    def dma(self, eng, out, in_, sem, waits=()):
        return self.add(eng, lambda h, o=out, i=in_: h.dma_start(out=o, in_=i),
                        waits=waits, inc=sem, inc_n=16)

    def play(self, name):
        ctx = self.ctx
        with ctx.nc.Block(name=name) as block:
            for eng in ENGINES:
                ops = self.ops[eng]
                if not ops:
                    continue
                def body(h, ops=ops):
                    for (fn, waits, incs) in ops:
                        for (s, v) in waits:
                            h.wait_ge(ctx.s(s), v)
                        inst = fn(h)
                        for (s, n) in incs:
                            inst.then_inc(ctx.s(s), n)
                getattr(block, eng)(body)
        self.ops = {e: [] for e in ENGINES}



def emit_consts(ctx):
    p = Prog(ctx)
    wl = p.dma("gpsimd", ctx.ident[:], ctx.rd("ident_in")[:, :], "wl")
    al = p.dma("sync", ctx.ident_f[:], ctx.rd("ident_in")[:, :], "al")
    p.add("vector", lambda v: v.memset(ctx.ones_row[:], 1.0))
    p.add("vector", lambda v: v.memset(ctx.ones_row_f[:], 1.0))
    p.add("vector", lambda v: v.memset(ctx.negones_row_f[:], -1.0))
    p.add("vector", lambda v: v.memset(ctx.ones_col[:], 1.0))
    for val, tile in ctx.cvals.items():
        p.add("vector", lambda v, tile=tile, val=val: v.memset(tile[:], float(val)))
    p.play("consts")
    ctx.barrier()


def rows_to_cols(p, specs, ps_t, extra_waits=()):
    ctx = p.ctx
    evac_v = {}
    for i, (row, col) in enumerate(specs):
        waits = list(extra_waits) if i == 0 else []
        if i >= 4:
            waits.append(("dve", evac_v[i - 4]))
        p.add("tensor", lambda t, r=row, b=ps_t[i % 4]: t.matmul(
            b[:, 0:1], r, ctx.ones_row_f[:, 0:1], start=True, stop=True),
            waits=waits, inc="pe")
        pev = ctx.cnt["pe"]
        evac_v[i] = p.add("vector", lambda v, c=col, b=ps_t[i % 4]: v.tensor_copy(c, b[:, 0:1]),
                          waits=[("pe", pev)], inc="dve")


def scalar_bcast_col(p, src11, dst_col, ps, scale=1.0, negate=False, waits=()):
    ctx = p.ctx
    lhs = ctx.negones_row_f if negate else ctx.ones_row_f
    p.add("tensor", lambda t: t.matmul(ps[:, 0:1], lhs[:, 0:128], src11, start=True, stop=True),
          waits=list(waits), inc="pe")
    pev = ctx.cnt["pe"]
    if scale == 1.0:
        return p.add("vector", lambda v: v.tensor_copy(dst_col, ps[:, 0:1]),
                     waits=[("pe", pev)], inc="dve")
    return p.add("vector", lambda v: v.tensor_scalar_mul(dst_col, ps[:, 0:1], float(scale)),
                 waits=[("pe", pev)], inc="dve")



@stage(reads=["x_in"], writes=["xs"])
def emit_xcast(ctx):
    nc = ctx.nc
    with contextlib.ExitStack() as st:
        buf = ctx.sb(st, "xc", [128, 2 * T], BF16)
        p = Prog(ctx)
        stv = {}
        for cc in range(NCC):
            waits = []
            if cc >= 2:
                waits.append(("st", stv[cc - 2]))
            alv = p.dma("gpsimd", buf[:, (cc % 2) * T:(cc % 2) * T + T],
                        ctx.rd("x_in")[cc * 128:(cc + 1) * 128, :], "al", waits=waits)
            stv[cc] = p.dma("sync", ctx.wr("xs")[cc * 128:(cc + 1) * 128, :],
                            buf[:, (cc % 2) * T:(cc % 2) * T + T], "st",
                            waits=[("al", alv)])
        p.play("xcast")
    ctx.barrier()


def emit_mod(ctx, layer):
    nc = ctx.nc
    c_in = ctx.rd("c0_in" if layer == 0 else "c1_in")
    wmod = ctx.rd("wmod0" if layer == 0 else "wmod1")
    with contextlib.ExitStack() as st:
        c_row = ctx.sb(st, "c_row", [1, C], F32)
        c_col = ctx.sb(st, "c_col", [128, 8], F32)
        sc_col = ctx.sb(st, "sc_col", [128, 8], BF16)
        wm = ctx.sb(st, "wm", [128, 8 * 6144], BF16)
        mod = ctx.sb(st, "mod", [1, 6144], F32)
        rows = ctx.sb(st, "rows", [1, 6144], F32)
        cols = ctx.sb(st, "cols", [128, 16], F32)
        tmp = ctx.sb(st, "tmpm", [1, 1024], F32)
        ps_t = [ctx.ps(st, f"ps_t{i}", [128, 512], F32) for i in range(4)]
        ps_m = [ctx.ps(st, f"ps_m{i}", [128, 512], F32) for i in range(2)]

        p = Prog(ctx)
        al_c = p.dma("sync", c_row[:, :], c_in[None, :], "al")
        wlv = {}
        for cc in range(NCC):
            wlv[cc] = p.dma("gpsimd", wm[:, cc * 6144:(cc + 1) * 6144],
                            wmod[cc * 128:(cc + 1) * 128, :], "wl")
        p.ops["tensor"].append((lambda t: t.nop(), [("al", al_c)], []))
        rows_to_cols(p, [(c_row[:, a * 128:(a + 1) * 128], c_col[:, a:a + 1]) for a in range(8)],
                     ps_t)
        dve_c = p.add("vector", lambda v: v.drain(), inc="dve")
        p.add("scalar", lambda s: s.activation(sc_col[:, :], c_col[:, :], AF.Silu),
              waits=[("dve", dve_c)])
        actv = p.add("scalar", lambda s: s.drain(), inc="act")
        evac = {}
        for e in range(12):
            for cc in range(NCC):
                waits = [("act", actv), ("wl", wlv[cc])]
                if cc == 0 and e >= 2:
                    waits.append(("dve", evac[e - 2]))
                p.add("tensor",
                      lambda t, e=e, cc=cc: t.matmul(
                          ps_m[e % 2][0:1, :], sc_col[:, cc:cc + 1],
                          wm[:, cc * 6144 + e * 512: cc * 6144 + (e + 1) * 512],
                          start=(cc == 0), stop=(cc == NCC - 1)),
                      waits=waits, inc="pe" if cc == NCC - 1 else None)
            pev = ctx.cnt["pe"]
            evac[e] = p.add("vector",
                            lambda v, e=e: v.tensor_copy(mod[:, e * 512:(e + 1) * 512], ps_m[e % 2][0:1, :]),
                            waits=[("pe", pev)], inc="dve")
        p.play(f"mod{layer}_mm")

        p = Prog(ctx)
        dve_pp_w = [("dve", ctx.cnt["dve"])]
        p.add("vector", lambda v: v.tensor_copy(rows[:, 0:1024], mod[:, 0:1024]), waits=dve_pp_w)
        p.add("vector", lambda v: v.tensor_scalar_add(rows[:, 1024:2048], mod[:, 1024:2048], 1.0))
        p.add("vector", lambda v: v.tensor_copy(rows[:, 3 * 1024:4 * 1024], mod[:, 3 * 1024:4 * 1024]))
        p.add("vector", lambda v: v.tensor_scalar_add(rows[:, 4 * 1024:5 * 1024], mod[:, 4 * 1024:5 * 1024], 1.0))
        for src in (2, 5):
            g_ = mod[:, src * 1024:(src + 1) * 1024]
            r_ = rows[:, src * 1024:(src + 1) * 1024]
            p.add("vector", lambda v, g_=g_: v.tensor_scalar(tmp[:, :], g_, 0.0, 2.0, ALU.is_ge, ALU.mult))
            p.add("vector", lambda v: v.drain())
            p.add("vector", lambda v: v.tensor_scalar_sub(tmp[:, :], tmp[:, :], 1.0))
            p.add("vector", lambda v, g_=g_, r_=r_: v.tensor_scalar_mul(r_, g_, -1.0))
            p.add("vector", lambda v: v.drain())
            p.add("vector", lambda v, g_=g_, r_=r_: v.tensor_tensor(r_, r_, g_, ALU.max))
            p.add("vector", lambda v, r_=r_: v.tensor_scalar_max(r_, r_, 0.25))
            p.add("vector", lambda v: v.drain())
            p.add("vector", lambda v, r_=r_: v.tensor_tensor(r_, r_, tmp[:, :], ALU.mult))
        dve_rows = p.add("vector", lambda v: v.drain(), inc="dve")
        p.ops["tensor"].append((lambda t: t.nop(), [("dve", dve_rows)], []))
        specs = []
        for j, src in enumerate((2, 5)):
            for a in range(8):
                specs.append((rows[:, src * 1024 + a * 128: src * 1024 + (a + 1) * 128],
                              cols[:, j * 8 + a: j * 8 + a + 1]))
        rows_to_cols(p, specs, ps_t)
        dve_fin = p.add("vector", lambda v: v.drain(), inc="dve")
        p.dma("sync", ctx.wr("mod_rows")[layer][None, :], rows[:, :], "st",
              waits=[("dve", dve_fin)])
        p.dma("sync", ctx.wr("mod_cols")[layer, 0], cols[:, 0:8], "st",
              waits=[("dve", dve_fin)])
        p.dma("sync", ctx.wr("mod_cols")[layer, 1], cols[:, 8:16], "st",
              waits=[("dve", dve_fin)])
        p.play(f"mod{layer}_pp")
    ctx.barrier()


@stage(reads=["c0_in", "wmod0"], writes=["mod_rows", "mod_cols"])
def emit_mod0(ctx):
    emit_mod(ctx, 0)


@stage(reads=["c1_in", "wmod1"], writes=["mod_rows", "mod_cols"])
def emit_mod1(ctx):
    emit_mod(ctx, 1)


def _emit_stats_tc(p, ctx, tc, xv, xwaits, sq, ps_sum, ps_sq, r_row, negmr_row, srow,
                   eps, prev_sdve=None):
    sqmm = {}
    actv = {}
    for cc in range(NCC):
        w = [xwaits[cc]]
        if cc == 0 and prev_sdve is not None:
            w.append(("dve", prev_sdve))
        if cc >= 2:
            w.append(("pe", sqmm[cc - 2]))
        p.add("scalar", lambda s, cc=cc: s.activation(
            sq[:, (cc % 2) * 512:(cc % 2) * 512 + 512], xv[cc], AF.Square),
            waits=w)
        actv[cc] = p.add("scalar", lambda s: s.drain(), inc="act")
        p.add("tensor", lambda t, cc=cc: t.matmul(
            ps_sum[0:1, 0:512], ctx.ones_col[:, :], xv[cc],
            start=(cc == 0), stop=(cc == NCC - 1)), waits=[xwaits[cc]])
        sqmm[cc] = p.add("tensor", lambda t, cc=cc: t.matmul(
            ps_sq[0:1, 0:512], ctx.ones_col[:, :], sq[:, (cc % 2) * 512:(cc % 2) * 512 + 512],
            start=(cc == 0), stop=(cc == NCC - 1)),
            waits=[("act", actv[cc])], inc="pe")
    pev = sqmm[NCC - 1]
    m_ = srow[:, 0:512]
    es = srow[:, 512:1024]
    var = srow[:, 1024:1536]
    mr = srow[:, 1536:2048]
    p.add("vector", lambda v: v.tensor_scalar_mul(m_, ps_sum[0:1, 0:512], 1.0 / C),
          waits=[("pe", pev)])
    p.add("vector", lambda v: v.tensor_scalar_mul(es, ps_sq[0:1, 0:512], 1.0 / C))
    p.add("vector", lambda v: v.drain())
    p.add("vector", lambda v: v.tensor_tensor(var, m_, m_, ALU.mult))
    p.add("vector", lambda v: v.drain())
    p.add("vector", lambda v: v.tensor_tensor(var, es, var, ALU.subtract))
    dvev = p.add("vector", lambda v: v.drain(), inc="dve")
    p.add("scalar", lambda s: s.activation(var, var, AF.Ln, bias=ctx.cb(eps)[0:1, :]),
          waits=[("dve", dvev)])
    p.add("scalar", lambda s: s.drain())
    p.add("scalar", lambda s: s.activation(
        r_row[:, tc * 512:(tc + 1) * 512], var, AF.Exp, scale=-0.5))
    actv2 = p.add("scalar", lambda s: s.drain(), inc="act")
    p.add("vector", lambda v: v.tensor_tensor(mr, m_, r_row[:, tc * 512:(tc + 1) * 512], ALU.mult),
          waits=[("act", actv2)])
    p.add("vector", lambda v: v.drain())
    p.add("vector", lambda v: v.tensor_scalar_mul(
        negmr_row[:, tc * 512:(tc + 1) * 512], mr, -1.0))
    p.add("vector", lambda v: v.drain())
    return p.add("vector", lambda v: v.engine_nop(), inc="dve")


def _emit_apply_tc(p, ctx, tc, cc, xv_cc, stats_dve, sa_row, capa_row, r_row, negmr_row,
                   ps_pair, h_dst, t1, extra_waits=(), out_f32=False):
    rslab = r_row[:, tc * 512:(tc + 1) * 512]
    nmrslab = negmr_row[:, tc * 512:(tc + 1) * 512]
    waits = [("dve", stats_dve)] + list(extra_waits)
    p.add("tensor", lambda t: t.matmul(ps_pair[:, 0:512], capa_row[:, cc * 128:(cc + 1) * 128],
                                       rslab, start=True, stop=True), waits=waits)
    p.add("tensor", lambda t: t.matmul(ps_pair[:, 512:1024], sa_row[:, cc * 128:(cc + 1) * 128],
                                       ctx.ones_row[:, :], start=True, stop=False))
    p.add("tensor", lambda t: t.matmul(ps_pair[:, 512:1024], capa_row[:, cc * 128:(cc + 1) * 128],
                                       nmrslab, start=False, stop=True), inc="pe")
    pev = ctx.cnt["pe"]
    p.add("vector", lambda v: v.tensor_tensor(t1, xv_cc, ps_pair[:, 0:512], ALU.mult),
          waits=[("pe", pev)])
    p.add("vector", lambda v: v.tensor_tensor(h_dst, t1, ps_pair[:, 512:1024], ALU.add))
    p.add("vector", lambda v: v.drain())
    return p.add("vector", lambda v: v.engine_nop(), inc="dve")


def make_ln_qkv_stage(layer, phi, qscale, wname):
    @stage(reads=["xs", wname, "mod_rows"], writes=["aq"])
    def fn(ctx):
        _ln_qkv_impl(ctx, layer, phi, qscale, wname)
    fn.__name__ = f"l{layer}_qkv"
    return fn


def _ln_qkv_impl(ctx, layer, phi, qscale, wname):
    nc = ctx.nc
    eps = 1e-5
    NE = 24
    with contextlib.ExitStack() as stk:
        w = ctx.sb(stk, "wqkv", [128, NCC * 3072], BF16)
        xt = ctx.sb(stk, "xt", [128, 2 * T], BF16)
        hb = ctx.sb(stk, "hb", [128, T], BF16)
        sq = ctx.sb(stk, "sqt", [128, 1024], BF16)
        srow = ctx.sb(stk, "srow", [1, 2048], F32)
        r_row = ctx.sb(stk, "r_row", [1, T], BF16)
        negmr_row = ctx.sb(stk, "negmr_row", [1, T], BF16)
        sa_row = ctx.sb(stk, "sa_row", [1, 1024], BF16)
        capa_row = ctx.sb(stk, "capa_row", [1, 1024], BF16)
        t1 = ctx.sb(stk, "t1q", [128, 512], F32)
        e1 = ctx.sb(stk, "e1", [128, 512], F32)
        e2 = ctx.sb(stk, "e2", [128, 512], F32)
        ob = ctx.sb(stk, "ob", [128, 2 * 512], BF16)
        ps_sum = ctx.ps(stk, "ps_sum", [128, 512], F32)
        ps_sq = ctx.ps(stk, "ps_sq", [128, 512], F32)
        ps_sb = [ctx.ps(stk, f"ps_sb{i}", [128, 1024], F32) for i in range(2)]
        ps_q = [ctx.ps(stk, f"ps_q{i}", [128, 512], F32) for i in range(2)]

        p = Prog(ctx)
        for cc in range(NCC):
            p.dma("gpsimd", w[:, cc * 3072:(cc + 1) * 3072],
                  ctx.rd(wname)[cc * 128:(cc + 1) * 128, :], "wl")
        wl_all = ctx.cnt["wl"]
        mr = ctx.rd("mod_rows")
        al_sa = p.dma("gpsimd", sa_row[:, :], mr[layer][None, 0:1024], "al")
        al_ca = p.dma("gpsimd", capa_row[:, :], mr[layer][None, 1024:2048], "al")

        sdve = None
        h_done = {}
        ep_done = {}
        st_done = {}
        pe_of_e = {}
        for tc in range(NTC):
            xbase = (tc % 2) * T
            xwaits = []
            for cc in range(NCC):
                waits = []
                if tc >= 2 and (tc - 2, NE - 1) in ep_done:
                    waits.append(ep_done[(tc - 2, NE - 1)])
                    waits.append(("dve", h_done[(tc - 2, NCC - 1)]))
                v = p.dma("sync", xt[:, xbase + cc * 512: xbase + (cc + 1) * 512],
                          ctx.rd("xs")[cc * 128:(cc + 1) * 128, tc * 512:(tc + 1) * 512],
                          "al", waits=waits)
                xwaits.append(("al", v))
            xv = [xt[:, xbase + cc * 512: xbase + (cc + 1) * 512] for cc in range(NCC)]
            sdve = _emit_stats_tc(p, ctx, tc, xv, xwaits, sq, ps_sum, ps_sq,
                                  r_row, negmr_row, srow, eps,
                                  prev_sdve=(sdve if tc > 0 else None))
            for cc in range(NCC):
                ew = [("al", al_sa), ("al", al_ca)]
                if (tc * NCC + cc) >= 2:
                    pi = tc * NCC + cc - 2
                    ew.append(("dve", h_done[(pi // NCC, pi % NCC)]))
                h_done[(tc, cc)] = _emit_apply_tc(
                    p, ctx, tc, cc, xv[cc], sdve, sa_row[:, :], capa_row[:, :],
                    r_row[:, :], negmr_row[:, :],
                    ps_sb[cc % 2], hb[:, cc * 512:(cc + 1) * 512], t1[:, :], extra_waits=ew)
            h_all = h_done[(tc, NCC - 1)]
            for e in range(NE):
                for cc in range(NCC):
                    waits = [("dve", h_all)]
                    if cc == 0:
                        waits.append(("wl", wl_all))
                        key = (tc, e - 2) if e >= 2 else (tc - 1, NE - 2 + e)
                        if key in ep_done:
                            waits.append(ep_done[key])
                    p.add("tensor", lambda t, e=e, cc=cc, xb=xbase: t.matmul(
                        ps_q[e % 2][:, :],
                        w[:, cc * 3072 + e * 128: cc * 3072 + (e + 1) * 128],
                        hb[:, cc * 512:(cc + 1) * 512],
                        start=(cc == 0), stop=(cc == NCC - 1)),
                        waits=waits, inc="pe" if cc == NCC - 1 else None)
                pe_of_e[(tc, e)] = ctx.cnt["pe"]
                pw = [("pe", pe_of_e[(tc, e)])]
                slot = e % 2
                o_ap = ob[:, slot * 512:(slot + 1) * 512]
                sw = st_done.get((tc, e - 2)) or st_done.get((tc - 1, NE - 2 + e))
                sw = [("st", sw)] if sw else []
                if phi and e < 16:
                    p.add("vector", lambda v, e=e: v.tensor_scalar_min(e1[:, :], ps_q[e % 2][:, :], 0.0),
                          waits=pw + sw)
                    d1 = p.add("vector", lambda v: v.drain(), inc="dve")
                    a1 = p.add("scalar", lambda s: s.activation(e2[:, :], e1[:, :], AF.Exp),
                               waits=[("dve", d1)])
                    a1d = p.add("scalar", lambda s: s.drain(), inc="act")
                    p.add("vector", lambda v, e=e, o=o_ap: v.scalar_tensor_tensor(
                        o, ps_q[e % 2][:, :], 0.0, e2[:, :], ALU.max, ALU.add),
                        waits=[("act", a1d)])
                    epv = p.add("vector", lambda v: v.drain(), inc="dve")
                    ep_done[(tc, e)] = ("dve", epv)
                else:
                    sc_ = float(qscale) if (qscale is not None and e < 8) else 1.0
                    p.add("scalar", lambda s, e=e, o=o_ap, sc_=sc_: s.activation(
                        o, ps_q[e % 2][:, :], AF.Copy, scale=sc_),
                        waits=pw + sw)
                    epv = p.add("scalar", lambda s: s.drain(), inc="act")
                    ep_done[(tc, e)] = ("act", epv)
                st_done[(tc, e)] = p.dma(
                    "gpsimd", ctx.wr("aq")[e * 128:(e + 1) * 128, tc * 512:(tc + 1) * 512],
                    o_ap, "st", waits=[ep_done[(tc, e)]])
        p.play(f"l{layer}a_qkv" if phi else f"l{layer}_qkv")
    ctx.barrier()


@stage(reads=["aq", "w_proj0", "bproj0", "mod_cols", "xs"], writes=["xs"])
def emit_l0a_attn(ctx):
    nc = ctx.nc
    H = 16
    with contextlib.ExitStack() as stk:
        kf = ctx.sb(stk, "kf", [64, T], BF16)
        vf = ctx.sb(stk, "vf", [64, T], BF16)
        qf = ctx.sb(stk, "qf", [64, T], BF16)
        ktm = ctx.sb(stk, "ktm", [128, 32 * 64], BF16)
        vtm = ctx.sb(stk, "vtm", [128, 32 * 65], BF16)
        kvb = ctx.sb(stk, "kvb", [64, 16 * 65], BF16)
        ctx_sb = ctx.sb(stk, "ctx_sb", [128, NCC * T], BF16)
        rbuf = ctx.sb(stk, "rbuf", [1, 512], F32)
        craw = ctx.sb(stk, "craw", [64, 2 * 512], BF16)
        wpr = ctx.sb(stk, "wpr", [128, NCC * 1024], BF16)
        bp_row = ctx.sb(stk, "bp_row", [1, 1024], F32)
        ga_col = ctx.sb(stk, "ga_col", [128, 8], F32)
        bias_col = ctx.sb(stk, "bias_col", [128, 8], F32)
        xt = ctx.sb(stk, "xtp", [128, 2 * 512], BF16)
        t1 = ctx.sb(stk, "t1p", [128, 2 * 512], F32)
        ob = ctx.sb(stk, "obp", [128, 2 * 512], BF16)
        ps_tr = [ctx.ps(stk, f"ptr{i}", [128, 1024], BF16) for i in range(3)]
        ps_kv = ctx.ps(stk, "ps_kv", [128, 512], F32)
        ps_o = [ctx.ps(stk, f"ps_o{i}", [128, 512], F32) for i in range(2)]
        ps_b = [ctx.ps(stk, f"ps_b{i}", [128, 512], F32) for i in range(2)]

        p = Prog(ctx)
        p.add("vector", lambda v: v.memset(vtm[:, :], 1.0))
        vtm_init = p.add("vector", lambda v: v.engine_nop(), inc="dve")

        kv_done = {}
        kv_evac = {}
        tr_evac = {}
        tr_idx = 0
        aq = ctx.rd("aq")
        for h in range(H):
            al_k = p.dma("sync", kf[:, :], aq[C + h * 64: C + (h + 1) * 64, :], "al",
                         waits=([("pe", kv_done[h - 1])] if h > 0 else []))
            al_v = p.dma("sync", vf[:, :], aq[2 * C + h * 64: 2 * C + (h + 1) * 64, :], "al")
            ev_k, ev_v = {}, {}
            for i in range(32):
                for (src, dst_evmap, dst) in ((kf, ev_k, ktm[:, i * 64:(i + 1) * 64]),
                                              (vf, ev_v, vtm[:, i * 65:i * 65 + 64])):
                    waits = [("al", al_k if src is kf else al_v)]
                    if src is vf:
                        waits.append(("dve", vtm_init))
                    if tr_idx >= 3:
                        waits.append(("dve", tr_evac[tr_idx - 3]))
                    b = ps_tr[tr_idx % 3]
                    p.add("tensor", lambda t, src=src, i=i, b=b: t.transpose(
                        b[:, 0:64], src[:, i * 128:(i + 1) * 128], ctx.ident[0:64, 0:64]),
                        waits=waits, inc="pe")
                    pev = ctx.cnt["pe"]
                    tr_evac[tr_idx] = p.add("vector", lambda v, d=dst, b=b: v.tensor_copy(d, b[:, 0:64]),
                                            waits=[("pe", pev)], inc="dve")
                    dst_evmap[i] = tr_evac[tr_idx]
                    tr_idx += 1
            p.add("vector", lambda v: v.drain())
            trm = p.add("vector", lambda v: v.engine_nop(), inc="dve")
            for i in range(32):
                waits = [("dve", trm)]
                if i == 0 and h > 0:
                    waits.append(("dve", kv_evac[h - 1]))
                p.add("tensor", lambda t, i=i: t.matmul(
                    ps_kv[0:64, 0:65], ktm[:, i * 64:(i + 1) * 64], vtm[:, i * 65:(i + 1) * 65],
                    start=(i == 0), stop=(i == 31)),
                    waits=waits, inc="pe" if i == 31 else None)
            kv_done[h] = ctx.cnt["pe"]
            p.add("vector", lambda v, h=h: v.tensor_copy(
                kvb[:, h * 65:(h + 1) * 65], ps_kv[0:64, 0:65]),
                waits=[("pe", kv_done[h])])
            p.add("vector", lambda v: v.drain())
            kv_evac[h] = p.add("vector", lambda v: v.engine_nop(), inc="dve")

        ev_prev = {}
        for h in range(H):
            al_q = p.dma("sync", qf[:, :], aq[h * 64:(h + 1) * 64, :], "al",
                         waits=[("dve", kv_evac[h])])
            for tc in range(NTC):
                sl = tc % 2
                waits = [("al", al_q), ("dve", kv_evac[h])]
                key = (h, tc - 2) if tc >= 2 else (h - 1, NTC - 2 + tc)
                if key in ev_prev:
                    waits.append(("dve", ev_prev[key]))
                p.add("tensor", lambda t, h=h, tc=tc, sl=sl: t.matmul(
                    ps_o[sl][0:65, :], kvb[:, h * 65:(h + 1) * 65],
                    qf[:, tc * 512:(tc + 1) * 512], start=True, stop=True),
                    waits=waits, inc="pe")
                pev = ctx.cnt["pe"]
                p.add("vector", lambda v, sl=sl: v.tensor_scalar_add(rbuf[:, :], ps_o[sl][64:65, :], 1e-6),
                      waits=[("pe", pev)])
                d1 = p.add("vector", lambda v: v.reciprocal(rbuf[:, :], rbuf[:, :]), inc="dve")
                p.add("tensor", lambda t, sl=sl: t.matmul(
                    ps_b[sl][0:64, :], ctx.ones_row_f[:, 0:64], rbuf[:, :], start=True, stop=True),
                    waits=[("dve", d1)], inc="pe")
                pbv = ctx.cnt["pe"]
                cw = [("pe", pbv)]
                if key in ev_prev:
                    cw.append(("dve", ev_prev[key]))
                p.add("scalar", lambda s, sl=sl: s.activation(
                    craw[:, sl * 512:(sl + 1) * 512], ps_o[sl][0:64, :], AF.Copy),
                    waits=cw)
                a1 = p.add("scalar", lambda s: s.drain(), inc="act")
                dst = ctx_sb[(h % 2) * 64:(h % 2) * 64 + 64,
                             (h // 2) * T + tc * 512:(h // 2) * T + (tc + 1) * 512]
                ev_prev[(h, tc)] = p.add("vector", lambda v, sl=sl, dst=dst: v.tensor_tensor(
                    dst, craw[:, sl * 512:(sl + 1) * 512], ps_b[sl][0:64, :], ALU.mult),
                    waits=[("act", a1)], inc="dve")
        p.add("vector", lambda v: v.drain())
        ctx_all = p.add("vector", lambda v: v.engine_nop(), inc="dve")

        for cc in range(NCC):
            p.dma("gpsimd", wpr[:, cc * 1024:(cc + 1) * 1024],
                  ctx.rd("w_proj0")[cc * 128:(cc + 1) * 128, :], "wl")
        wl_all = ctx.cnt["wl"]
        al_bp = p.dma("sync", bp_row[:, :], ctx.rd("bproj0")[None, :], "al")
        al_ga = p.dma("sync", ga_col[:, :], ctx.rd("mod_cols")[0, 0], "al")
        rows_to_cols(p, [(bp_row[:, a * 128:(a + 1) * 128], bias_col[:, a:a + 1]) for a in range(8)],
                     [ps_o[0], ps_o[1], ps_b[0], ps_b[1]],
                     extra_waits=[("al", al_bp), ("dve", ctx_all)])
        bias_dve = ctx.cnt["dve"]
        p.add("vector", lambda v: v.tensor_tensor(bias_col[:, :], bias_col[:, :], ga_col[:, :], ALU.mult),
              waits=[("dve", bias_dve), ("al", al_ga)])
        p.add("vector", lambda v: v.drain())
        bdv = p.add("vector", lambda v: v.engine_nop(), inc="dve")
        ep = {}
        stv = {}
        for tc in range(NTC):
            for ec in range(NCC):
                for cc in range(NCC):
                    waits = [("dve", ctx_all), ("dve", bdv)]
                    if cc == 0:
                        waits.append(("wl", wl_all))
                        key = (tc, ec - 2) if ec >= 2 else (tc - 1, NCC - 2 + ec)
                        if key in ep:
                            waits.append(("dve", ep[key]))
                    p.add("tensor", lambda t, ec=ec, cc=cc: t.matmul(
                        ps_o[ec % 2][:, :], wpr[:, cc * 1024 + ec * 128: cc * 1024 + (ec + 1) * 128],
                        ctx_sb[:, cc * T + tc * 512: cc * T + (tc + 1) * 512],
                        start=(cc == 0), stop=(cc == NCC - 1)),
                        waits=waits, inc="pe" if cc == NCC - 1 else None)
                pev = ctx.cnt["pe"]
                slot = ec % 2
                pkey = (tc, ec - 2) if ec >= 2 else (tc - 1, NCC - 2 + ec)
                alx = p.dma("sync", xt[:, slot * 512:(slot + 1) * 512],
                            ctx.rd("xs")[ec * 128:(ec + 1) * 128, tc * 512:(tc + 1) * 512],
                            "al", waits=[("dve", ep[pkey])] if pkey in ep else [])
                aw = [("pe", pev), ("al", al_ga)]
                if pkey in ep:
                    aw.append(("dve", ep[pkey]))
                p.add("scalar", lambda s, ec=ec, slot=slot: s.activation(
                    t1[:, slot * 512:(slot + 1) * 512], ps_o[ec % 2][:, :], AF.Identity,
                    bias=bias_col[:, ec:ec + 1], scale=ga_col[:, ec:ec + 1]),
                    waits=aw)
                a1 = p.add("scalar", lambda s: s.drain(), inc="act")
                o_ap = ob[:, slot * 512:(slot + 1) * 512]
                sw = []
                if pkey in stv:
                    sw.append(("st", stv[pkey]))
                p.add("vector", lambda v, o=o_ap, slot=slot: v.tensor_tensor(
                    o, t1[:, slot * 512:(slot + 1) * 512], xt[:, slot * 512:(slot + 1) * 512], ALU.add),
                    waits=[("act", a1), ("al", alx)] + sw)
                ep[(tc, ec)] = p.add("vector", lambda v: v.drain(), inc="dve")
                stv[(tc, ec)] = p.dma(
                    "gpsimd", ctx.wr("xs")[ec * 128:(ec + 1) * 128, tc * 512:(tc + 1) * 512],
                    o_ap, "st", waits=[("dve", ep[(tc, ec)])])
        p.play("l0a_attn")
    ctx.barrier()



QUANT_DRAIN = True


def emit_quant_tile(p, ctx, x_ap, out_ap, F, scr, waits=()):
    nb = F // 16
    A = scr["A"][:, 0:F]
    B = scr["B"][:, 0:F]
    Cc = scr["C"][:, 0:F]
    D = scr["D"][:, 0:F]
    I = scr["I"][:, 0:F]
    SC = scr["SC"][:, 0:nb]
    RS = scr["RS"][:, 0:nb]
    SM = scr["SM"]
    x3 = x_ap.rearrange("p (b k) -> p b k", k=16)
    a3 = A.rearrange("p (b k) -> p b k", k=16)
    v3 = lambda ap, k: ap.rearrange("p (b j) -> p b j", j=k)
    _add = p.add

    def padd(eng, fn, **kw):
        r = _add(eng, fn, **kw)
        if QUANT_DRAIN and eng == "vector":
            _add("vector", lambda v: v.drain())
        return r
    p = type("P", (), {"add": staticmethod(padd), "dma": p.dma, "ctx": ctx})()
    _add("vector", lambda v: v.drain())
    p.add("vector", lambda v: v.reduce_max(
        B[:, 0:nb].unsqueeze(2), x3, AX.X, apply_absolute_value=True),
        waits=list(waits))
    p.add("vector", lambda v: v.tensor_scalar(SC, B[:, 0:nb], 1.0 / 6.0, 1e-12, ALU.mult, ALU.max))
    p.add("vector", lambda v: v.reciprocal(RS, SC))
    rs3 = RS.unsqueeze(2).broadcast_to([128, nb, 16])
    sc3 = SC.unsqueeze(2).broadcast_to([128, nb, 16])
    c3v = v3(Cc, 16)
    p.add("vector", lambda v: v.tensor_tensor(c3v, x3, rs3, ALU.mult))
    p.add("vector", lambda v: v.tensor_scalar_mul(A, Cc, -1.0))
    p.add("vector", lambda v: v.tensor_tensor(A, Cc, A, ALU.max))
    p.add("vector", lambda v: v.tensor_scalar_mul(B, A, 2.0))
    p.add("vector", lambda v: v.tensor_copy(I, B))
    p.add("vector", lambda v: v.tensor_scalar(B, I, 0.5, 2.0, ALU.mult, ALU.min))
    p.add("vector", lambda v: v.tensor_copy(I, A))
    p.add("vector", lambda v: v.tensor_scalar(D, I, 2.0, 0.0, ALU.subtract, ALU.max))
    p.add("vector", lambda v: v.tensor_scalar_min(D, D, 2.0))
    p.add("vector", lambda v: v.tensor_tensor(B, B, D, ALU.add))
    p.add("vector", lambda v: v.tensor_scalar_mul(D, A, 0.5))
    p.add("vector", lambda v: v.tensor_copy(I, D))
    p.add("vector", lambda v: v.tensor_scalar(D, I, 2.0, 4.0, ALU.mult, ALU.subtract))
    p.add("vector", lambda v: v.scalar_tensor_tensor(B, D, 0.0, B, ALU.max, ALU.add))
    smb = SM.broadcast_to([128, F])
    p.add("vector", lambda v: v.tensor_tensor(A.bitcast(I32), x_ap.bitcast(I32), smb, ALU.bitwise_and))
    p.add("vector", lambda v: v.tensor_tensor(Cc.bitcast(I32), B.bitcast(I32), A.bitcast(I32), ALU.bitwise_or))
    o3 = out_ap.rearrange("p (b k) -> p b k", k=16)
    p.add("vector", lambda v: v.tensor_tensor(o3, v3(Cc, 16), sc3, ALU.mult))
    p.add("vector", lambda v: v.drain())
    return p.add("vector", lambda v: v.engine_nop(), inc="dve")


def make_quant_scratch(ctx, stk, F):
    return {
        "A": ctx.sb(stk, "qA", [128, F], F32)[:, :],
        "B": ctx.sb(stk, "qB", [128, F], F32)[:, :],
        "C": ctx.sb(stk, "qC", [128, F], F32)[:, :],
        "D": ctx.sb(stk, "qD", [128, F], F32)[:, :],
        "I": ctx.sb(stk, "qI", [128, F], I32)[:, :],
        "SC": ctx.sb(stk, "qSC", [128, F // 16], F32)[:, :],
        "RS": ctx.sb(stk, "qRS", [128, F // 16], F32)[:, :],
        "SM": ctx.sb(stk, "qSM", [128, 1], I32)[:, :],
    }


@stage(reads=["wg_in", "wu_in", "wd_in"], writes=["wgq", "wuq", "wdq"])
def emit_l0f_wq(ctx):
    nc = ctx.nc
    with contextlib.ExitStack() as stk:
        xin = ctx.sb(stk, "wqx", [128, 2 * 2816], F32)
        qt = ctx.sb(stk, "wqq", [128, 2 * 2816], BF16)
        tb = ctx.sb(stk, "wqt", [128, 4 * 128], BF16)
        scr = make_quant_scratch(ctx, stk, 2816)
        ps_tr = [ctx.ps(stk, f"wqp{i}", [128, 1024], BF16) for i in range(4)]
        p = Prog(ctx)
        p.add("vector", lambda v: v.memset(scr["SM"], -2147483648))
        jobs = [("wg_in", "wgq", 22, 1024), ("wu_in", "wuq", 22, 1024),
                ("wd_in", "wdq", 8, 2816)]
        tr_done = {}
        ev_done = {}
        st_done = {}
        gidx = 0
        ridx = 0
        for (src_n, dst_n, NR, F) in jobs:
            nt = F // 128
            for r in range(NR):
                sl = ridx % 2
                waits = []
                if ridx >= 2:
                    waits.append(("dve", q_done[ridx - 2]))
                alv = p.dma("gpsimd", xin[:, sl * 2816: sl * 2816 + F],
                            ctx.rd(src_n)[r * 128:(r + 1) * 128, :], "wl", waits=waits)
                qw = [("wl", alv)]
                if ridx >= 2:
                    qw.append(("pe", tr_done[ridx - 2]))
                if ridx == 0:
                    qw.append(("dve", ctx.cnt["dve"]))
                if 'q_done' not in dir():
                    q_done = {}
                q_done[ridx] = emit_quant_tile(
                    p, ctx, xin[:, sl * 2816: sl * 2816 + F],
                    qt[:, sl * 2816: sl * 2816 + F], F, scr, waits=qw)
                for j in range(nt):
                    w2 = [("dve", q_done[ridx])]
                    if gidx >= 4:
                        w2.append(("dve", ev_done[gidx - 4]))
                    p.add("tensor", lambda t, sl=sl, j=j, b=ps_tr[gidx % 4]: t.transpose(
                        b[:, 0:128], qt[:, sl * 2816 + j * 128: sl * 2816 + (j + 1) * 128],
                        ctx.ident[:]), waits=w2, inc="pe")
                    pev = ctx.cnt["pe"]
                    tsl = gidx % 4
                    w3 = [("pe", pev)]
                    if gidx >= 4 and (gidx - 4) in st_done:
                        w3.append(("st", st_done[gidx - 4]))
                    ev_done[gidx] = p.add("vector", lambda v, tsl=tsl, b=ps_tr[gidx % 4]: v.tensor_copy(
                        tb[:, tsl * 128:(tsl + 1) * 128], b[:, 0:128]),
                        waits=w3, inc="dve")
                    st_done[gidx] = p.dma(
                        "sync", ctx.wr(dst_n)[j * 128:(j + 1) * 128, r * 128:(r + 1) * 128],
                        tb[:, tsl * 128:(tsl + 1) * 128], "st",
                        waits=[("dve", ev_done[gidx])])
                    gidx += 1
                tr_done[ridx] = ctx.cnt["pe"]
                ridx += 1
        p.play("l0f_wq")
    ctx.barrier()


DEBUG_L0F = False
DEBUG_L0F_H = False
DEBUG_L0F_G = False
DEBUG_NP = False
DEBUG_L0F_S = False
DEBUG_FIN = False


@stage(reads=["xs", "wgq", "wuq", "wdq", "mod_rows", "mod_cols"], writes=["xs", "dbg_a", "dbg_b"])
def emit_l0f_main(ctx):
    nc = ctx.nc
    NI = 22
    with contextlib.ExitStack() as stk:
        wg = ctx.sb(stk, "wg", [128, NCC * 2816], BF16)
        wu = ctx.sb(stk, "wu", [128, NCC * 2816], BF16)
        wd = ctx.sb(stk, "wd", [128, 1024], BF16)
        xt = ctx.sb(stk, "xt0f", [128, T], BF16)
        hf = ctx.sb(stk, "hf", [128, NCC * 512], F32)
        uni = ctx.sb(stk, "uni0f", [128, 6656], F32)
        htm = uni[:, 0:4096]
        hqtm = ctx.sb(stk, "hqtm0f", [128, 4096], BF16)[:, :]
        h2qf = uni[:, 0:5632].bitcast(BF16)
        hqf = ctx.sb(stk, "hqf", [128, NCC * 512], BF16)
        h2 = ctx.sb(stk, "h2", [128, 512], F32)
        h2tm = ctx.sb(stk, "h2tm", [128, 4 * 128], F32)
        h2qtm = ctx.sb(stk, "h2qtm", [128, 4 * 128], BF16)
        gt = ctx.sb(stk, "gt", [128, 512], F32)
        ut = ctx.sb(stk, "ut", [128, 512], F32)
        sq = ctx.sb(stk, "sq0f", [128, 1024], BF16)
        srow = ctx.sb(stk, "srow0f", [1, 2048], F32)
        r_row = ctx.sb(stk, "r_row0f", [1, T], BF16)
        negmr_row = ctx.sb(stk, "negmr0f", [1, T], BF16)
        sb_row = ctx.sb(stk, "sb_row", [1, 1024], BF16)
        cbpa_row = ctx.sb(stk, "cbpa_row", [1, 1024], BF16)
        gb_col = ctx.sb(stk, "gb_col", [128, 8], F32)
        t1 = gt
        t1e = ctx.sb(stk, "t1e", [128, 2 * 512], F32)
        xr = ctx.sb(stk, "xr0f", [128, 2 * 512], BF16)
        ob = ctx.sb(stk, "ob0f", [128, 2 * 512], BF16)
        scr = {
            "A": hf[:, 0:1024], "B": hf[:, 1024:2048],
            "C": hf[:, 2048:3072], "D": hf[:, 3072:4096],
            "I": uni[:, 5632:6656].bitcast(I32),
            "SC": h2[:, 0:64],
            "RS": h2[:, 64:128],
            "SM": ctx.sb(stk, "qSM0f", [128, 1], I32)[:, :],
        }
        PA = ctx.ps(stk, "PA", [128, 1024], F32)
        PB = ctx.ps(stk, "PB", [128, 1024], F32)
        PG = ctx.ps(stk, "PG", [128, 512], F32)
        PU = ctx.ps(stk, "PU", [128, 512], F32)
        PS1 = ctx.ps(stk, "PS1", [128, 512], F32)
        PS2 = ctx.ps(stk, "PS2", [128, 512], F32)
        trf = [PB[:, 0:128], PB[:, 512:640], PG[:, 0:128]]
        trf2 = [PS1[:, 0:128], PS2[:, 0:128], PA[:, 0:128]]
        trb = [PB[:, 0:64].bitcast(BF16), PB[:, 512:576].bitcast(BF16),
               PG[:, 0:64].bitcast(BF16)]
        trb2 = [PA[:, 512:576].bitcast(BF16), PB[:, 0:64].bitcast(BF16),
                PB[:, 512:576].bitcast(BF16)]
        down_banks = [PA[:, 0:512], PA[:, 512:1024], PB[:, 0:512], PB[:, 512:1024]]

        p = Prog(ctx)
        p.add("vector", lambda v: v.memset(scr["SM"], -2147483648))
        for cc in range(NCC):
            p.dma("gpsimd", wg[:, cc * 2816:(cc + 1) * 2816],
                  ctx.rd("wgq")[cc * 128:(cc + 1) * 128, :], "wl")
            p.dma("gpsimd", wu[:, cc * 2816:(cc + 1) * 2816],
                  ctx.rd("wuq")[cc * 128:(cc + 1) * 128, :], "wl")
        wl_all = ctx.cnt["wl"]
        mr = ctx.rd("mod_rows")
        al_sb = p.dma("gpsimd", sb_row[:, :], mr[0][None, 3 * 1024:4 * 1024], "al")
        al_cb = p.dma("gpsimd", cbpa_row[:, :], mr[0][None, 4 * 1024:5 * 1024], "al")
        al_gb = p.dma("sync", gb_col[:, :], ctx.rd("mod_cols")[0, 1], "al")

        sdve = None
        h_done = {}
        res_done = {}
        st_done = {}
        act_guard = {}
        for tc in range(NTC):
            xbase = 0
            xwaits = []
            for cc in range(NCC):
                waits = []
                if tc >= 1 and (tc - 1) in act_guard:
                    waits.append(("act", act_guard[tc - 1]))
                    waits.append(("dve", h_done[(tc - 1, NCC - 1)]))
                if tc >= 2:
                    waits.append(("dve", res_done[(tc - 2, NCC - 1)]))
                v = p.dma("sync", xt[:, xbase + cc * 512: xbase + (cc + 1) * 512],
                          ctx.rd("xs")[cc * 128:(cc + 1) * 128, tc * 512:(tc + 1) * 512],
                          "al", waits=waits)
                xwaits.append(("al", v))
            xv = [xt[:, xbase + cc * 512: xbase + (cc + 1) * 512] for cc in range(NCC)]
            sdve = _emit_stats_tc(p, ctx, tc, xv, xwaits, sq, PS1, PS2,
                                  r_row, negmr_row, srow, 1e-5,
                                  prev_sdve=(sdve if tc > 0 else None))
            for cc in range(NCC):
                ew = [("al", al_sb), ("al", al_cb)]
                if (tc, cc - 1) in h_done:
                    ew.append(("dve", h_done[(tc, cc - 1)]))
                elif (tc - 1, NCC - 1) in h_done:
                    ew.append(("dve", h_done[(tc - 1, NCC - 1)]))
                h_done[(tc, cc)] = _emit_apply_tc(
                    p, ctx, tc, cc, xv[cc], sdve, sb_row[:, :], cbpa_row[:, :],
                    r_row[:, :], negmr_row[:, :], PA,
                    hf[:, cc * 512:(cc + 1) * 512], t1[:, :], extra_waits=ew)
            h_all = h_done[(tc, NCC - 1)]
            p.play(f"l0f_m{tc}pre")
            ctx.barrier()
            p = Prog(ctx)
            dump_st = None
            if DEBUG_L0F_H:
                for cc in range(NCC):
                    dump_st = p.dma("gpsimd",
                                    ctx.wr("dbg_a")[cc * 128:(cc + 1) * 128,
                                                    tc * 512:(tc + 1) * 512],
                                    hf[:, cc * 512:(cc + 1) * 512], "st",
                                    waits=[("dve", h_all)])
            gi = 0
            ev = {}
            for tt in range(4):
                for cc in range(NCC):
                    w2 = [("dve", h_all)]
                    if gi >= 3:
                        w2.append(("dve", ev[gi - 3]))
                    b = trf[gi % 3]
                    p.add("tensor", lambda t, tt=tt, cc=cc, b=b: t.transpose(
                        b, hf[:, cc * 512 + tt * 128: cc * 512 + (tt + 1) * 128],
                        ctx.ident_f[:]), waits=w2, inc="pe")
                    pev = ctx.cnt["pe"]
                    ev[gi] = p.add("vector", lambda v, tt=tt, cc=cc, b=b: v.tensor_copy(
                        htm[:, tt * 1024 + cc * 128: tt * 1024 + (cc + 1) * 128], b),
                        waits=[("pe", pev)], inc="dve")
                    gi += 1
            if DEBUG_L0F_H:
                for tt in range(4):
                    dump_st = p.dma("gpsimd",
                                    ctx.wr("dbg_b")[tt * 128:(tt + 1) * 128, tc * 512:(tc + 1) * 512],
                                    htm[:, tt * 1024: tt * 1024 + 512], "st",
                                    waits=[("dve", ev[31])])
                    dump_st = p.dma("gpsimd",
                                    ctx.wr("dbg_b")[512 + tt * 128: 512 + (tt + 1) * 128, tc * 512:(tc + 1) * 512],
                                    htm[:, tt * 1024 + 512: tt * 1024 + 1024], "st",
                                    waits=[("dve", ev[31])])
            qdv = {}
            for tt in range(4):
                qw = [("dve", ev[31])]
                if DEBUG_L0F_H and tt == 0:
                    qw.append(("st", dump_st))
                qdv[tt] = emit_quant_tile(p, ctx, htm[:, tt * 1024:(tt + 1) * 1024],
                                          hqtm[:, tt * 1024:(tt + 1) * 1024], 1024, scr,
                                          waits=qw)
            gi = 0
            ev2 = {}
            for cc in range(NCC):
                for tt in range(4):
                    w2 = [("dve", qdv[tt])]
                    if gi >= 3:
                        w2.append(("dve", ev2[gi - 3]))
                    else:
                        w2.append(("dve", ev[31]))
                    b = trb[gi % 3]
                    p.add("tensor", lambda t, tt=tt, cc=cc, b=b: t.transpose(
                        b, hqtm[:, tt * 1024 + cc * 128: tt * 1024 + (cc + 1) * 128],
                        ctx.ident[:]), waits=w2, inc="pe")
                    pev = ctx.cnt["pe"]
                    ev2[gi] = p.add("vector", lambda v, tt=tt, cc=cc, b=b: v.tensor_copy(
                        hqf[:, cc * 512 + tt * 128: cc * 512 + (tt + 1) * 128], b),
                        waits=[("pe", pev)], inc="dve")
                    gi += 1
            p.add("vector", lambda v: v.drain())
            hq_all = p.add("vector", lambda v: v.engine_nop(), inc="dve")
            p.play(f"l0f_m{tc}a")
            ctx.barrier()
            p = Prog(ctx)
            if DEBUG_L0F_S == "hq" and tc == 0:
                for cc in range(NCC):
                    p.dma("gpsimd", ctx.wr("dbg_a")[cc * 128:(cc + 1) * 128, 0:512],
                          hqf[:, cc * 512:(cc + 1) * 512], "st", waits=[("dve", hq_all)])
            if DEBUG_L0F_H:
                for tt in range(4):
                    p.dma("sync", ctx.wr("dbg_b")[1024 + tt * 128: 1024 + (tt + 1) * 128,
                                                  tc * 512:(tc + 1) * 512],
                          hqtm[:, tt * 1024: tt * 1024 + 512], "st", waits=[("dve", hq_all)])
                    p.dma("sync", ctx.wr("dbg_b")[1536 + tt * 128: 1536 + (tt + 1) * 128,
                                                  tc * 512:(tc + 1) * 512],
                          hqtm[:, tt * 1024 + 512: tt * 1024 + 1024], "st", waits=[("dve", hq_all)])
                for cc in range(4):
                    p.dma("sync", ctx.wr("dbg_b")[2048 + cc * 128: 2048 + (cc + 1) * 128,
                                                  tc * 512:(tc + 1) * 512],
                          hqf[:, cc * 512:(cc + 1) * 512], "st", waits=[("dve", hq_all)])
            if DEBUG_L0F and not DEBUG_L0F_H:
                for cc in range(NCC):
                    p.dma("sync", ctx.wr("dbg_a")[cc * 128:(cc + 1) * 128, tc * 512:(tc + 1) * 512],
                          hqf[:, cc * 512:(cc + 1) * 512], "st", waits=[("dve", hq_all)])
            h2q_done = {}
            gu_ev = {}
            for ec in range(NI):
                for cc in range(NCC):
                    w = [("dve", hq_all), ("wl", wl_all)]
                    if cc == 0 and ec >= 1:
                        w.append(("dve", gu_ev[ec - 1]))
                    p.add("tensor", lambda t, ec=ec, cc=cc: t.matmul(
                        PG[:, :], wg[:, cc * 2816 + ec * 128: cc * 2816 + (ec + 1) * 128],
                        hqf[:, cc * 512:(cc + 1) * 512],
                        start=(cc == 0), stop=(cc == NCC - 1)), waits=w)
                    p.add("tensor", lambda t, ec=ec, cc=cc: t.matmul(
                        PU[:, :], wu[:, cc * 2816 + ec * 128: cc * 2816 + (ec + 1) * 128],
                        hqf[:, cc * 512:(cc + 1) * 512],
                        start=(cc == 0), stop=(cc == NCC - 1)),
                        inc="pe" if cc == NCC - 1 else None)
                pev = ctx.cnt["pe"]
                aw = [("pe", pev)]
                if ec >= 1:
                    aw.append(("dve", gu_ev[ec - 1]))
                p.add("scalar", lambda s: s.activation(gt[:, :], PG[:, :], AF.Silu),
                      waits=aw)
                a1 = p.add("scalar", lambda s: s.drain(), inc="act")
                p.add("vector", lambda v: v.tensor_copy(ut[:, :], PU[:, :]),
                      waits=[("pe", pev), ("act", a1)])
                p.add("vector", lambda v: v.tensor_tensor(
                    h2[:, :], gt[:, :], ut[:, :], ALU.mult))
                p.add("vector", lambda v: v.drain())
                gu_ev[ec] = p.add("vector", lambda v: v.engine_nop(), inc="dve")
                if DEBUG_L0F_G:
                    p.dma("gpsimd", ctx.wr("dbg_b")[ec * 128:(ec + 1) * 128, tc * 512:(tc + 1) * 512],
                          h2[:, :], "st", waits=[("dve", gu_ev[ec])])
                    gu_ev[ec] = p.add("vector", lambda v: v.engine_nop(),
                                      waits=[("st", ctx.cnt["st"])], inc="dve")
                ev3 = {}
                for tt in range(4):
                    w2 = [("dve", gu_ev[ec])]
                    if tt >= 3:
                        w2.append(("dve", ev3[tt - 3]))
                    b = trf2[tt % 3]
                    p.add("tensor", lambda t, tt=tt, b=b: t.transpose(
                        b, h2[:, tt * 128:(tt + 1) * 128], ctx.ident_f[:]),
                        waits=w2, inc="pe")
                    pev = ctx.cnt["pe"]
                    ev3[tt] = p.add("vector", lambda v, tt=tt, b=b: v.tensor_copy(
                        h2tm[:, tt * 128:(tt + 1) * 128], b),
                        waits=[("pe", pev)], inc="dve")
                q2 = emit_quant_tile(p, ctx, h2tm[:, :], h2qtm[:, :], 512, scr,
                                     waits=[("dve", ev3[3])])
                ev4 = {}
                for tt in range(4):
                    w2 = [("dve", q2)]
                    if tt >= 3:
                        w2.append(("dve", ev4[tt - 3]))
                    b = trb2[tt % 3]
                    p.add("tensor", lambda t, tt=tt, b=b: t.transpose(
                        b, h2qtm[:, tt * 128:(tt + 1) * 128], ctx.ident[:]),
                        waits=w2, inc="pe")
                    pev = ctx.cnt["pe"]
                    ev4[tt] = p.add("vector", lambda v, tt=tt, ec=ec, b=b: v.tensor_copy(
                        h2qf[:, ec * 512 + tt * 128: ec * 512 + (tt + 1) * 128], b),
                        waits=[("pe", pev)], inc="dve")
                p.add("vector", lambda v: v.drain())
                h2q_done[ec] = p.add("vector", lambda v: v.engine_nop(), inc="dve")
                if DEBUG_L0F_S == "h2q" and tc == 0:
                    p.dma("sync", ctx.wr("dbg_b")[ec * 128:(ec + 1) * 128, 0:512],
                          h2qf[:, ec * 512:(ec + 1) * 512], "st", waits=[("dve", h2q_done[ec])])
                if DEBUG_L0F:
                    p.dma("sync", ctx.wr("dbg_b")[ec * 128:(ec + 1) * 128, tc * 512:(tc + 1) * 512],
                          h2qf[:, ec * 512:(ec + 1) * 512], "st", waits=[("dve", h2q_done[ec])])
            p.play(f"l0f_m{tc}b")
            ctx.barrier()
            p = Prog(ctx)
            eplist = []
            prev_dpe = None
            for half in range(2):
                wd_pe = {}
                wdl = {}
                for ec in range(NI):
                    w = [("dve", h2q_done[NI - 1])]
                    if ec >= 1:
                        w.append(("pe", wd_pe[ec - 1]))
                    elif prev_dpe is not None:
                        w.append(("pe", prev_dpe))
                    wdl[ec] = p.dma("gpsimd", wd[:, :],
                                    ctx.rd("wdq")[ec * 128:(ec + 1) * 128, :], "wl",
                                    waits=w)
                    for hj in range(4):
                        hc = half * 4 + hj
                        w2 = [("wl", wdl[ec])]
                        if ec == 0:
                            w2.append(("dve", h2q_done[NI - 1]))
                            if half == 1:
                                w2.append(("act", eplist[-1]))
                        p.add("tensor", lambda t, ec=ec, hj=hj, hc=hc: t.matmul(
                            down_banks[hj], wd[:, hc * 128:(hc + 1) * 128],
                            h2qf[:, ec * 512:(ec + 1) * 512],
                            start=(ec == 0), stop=(ec == NI - 1)),
                            inc="pe" if hj == 3 else None, waits=w2)
                    wd_pe[ec] = ctx.cnt["pe"]
                dpe = ctx.cnt["pe"]
                prev_dpe = dpe
                for hj in range(4):
                    hc = half * 4 + hj
                    sl = hc % 2
                    pkey = (tc, hc - 2) if hc >= 2 else (tc - 1, NCC - 2 + hc)
                    alx = p.dma("sync", xr[:, sl * 512:(sl + 1) * 512],
                                ctx.rd("xs")[hc * 128:(hc + 1) * 128, tc * 512:(tc + 1) * 512],
                                "al", waits=[("dve", res_done[pkey])] if pkey in res_done else [])
                    aw = [("pe", dpe), ("al", al_gb)]
                    if pkey in res_done:
                        aw.append(("dve", res_done[pkey]))
                    p.add("scalar", lambda s, hj=hj, hc=hc, sl=sl: s.activation(
                        t1e[:, sl * 512:(sl + 1) * 512], down_banks[hj], AF.Identity,
                        scale=gb_col[:, hc:hc + 1]), waits=aw)
                    a1 = p.add("scalar", lambda s: s.drain(), inc="act")
                    eplist.append(a1)
                    sw = [("st", st_done[pkey])] if pkey in st_done else []
                    p.add("vector", lambda v, sl=sl: v.tensor_tensor(
                        ob[:, sl * 512:(sl + 1) * 512], t1e[:, sl * 512:(sl + 1) * 512],
                        xr[:, sl * 512:(sl + 1) * 512], ALU.add),
                        waits=[("act", a1), ("al", alx)] + sw)
                    res_done[(tc, hc)] = p.add("vector", lambda v: v.drain(), inc="dve")
                    st_done[(tc, hc)] = p.dma(
                        "gpsimd", ctx.wr("xs")[hc * 128:(hc + 1) * 128, tc * 512:(tc + 1) * 512],
                        ob[:, sl * 512:(sl + 1) * 512], "st",
                        waits=[("dve", res_done[(tc, hc)])])
            act_guard[tc] = eplist[-1]
            p.play(f"l0f_m{tc}c")
            ctx.barrier()
            p = Prog(ctx)
        if DEBUG_L0F_S == "rows":
            p.dma("gpsimd", ctx.wr("dbg_a")[0:1, 0:T], r_row[:, :], "st",
                  waits=[("act", act_guard[NTC - 1])])
            p.dma("gpsimd", ctx.wr("dbg_a")[1:2, 0:T], negmr_row[:, :], "st")
        p.play("l0f_main")
    ctx.barrier()


@stage(reads=["aq", "w_out1", "mod_cols", "xs"], writes=["xs"])
def emit_l1a_attn(ctx):
    nc = ctx.nc
    H = 16
    NQS = 8
    with contextlib.ExitStack() as stk:
        qf = ctx.sb(stk, "qf1", [64, T], BF16)
        kf = ctx.sb(stk, "kf1", [64, T], BF16)
        vf = ctx.sb(stk, "vf1", [64, T], BF16)
        vtm = ctx.sb(stk, "vtm1", [128, 32 * 65], BF16)
        sqq = ctx.sb(stk, "sqq1", [64, T], BF16)
        nq2 = ctx.sb(stk, "nq2r", [1, T], F32)
        nk2 = ctx.sb(stk, "nk2r", [1, T], F32)
        m_row = ctx.sb(stk, "m_row1", [1, T], F32)
        k2m = ctx.sb(stk, "k2m", [1, 8], F32)
        msr = ctx.sb(stk, "msr", [1, 8], F32)
        negm = ctx.sb(stk, "negm", [128, 8], F32)
        pt = [ctx.sb(stk, f"pt1{i}", [128, 1024], BF16) for i in range(2)]
        rbuf = ctx.sb(stk, "rbuf1", [1, 512], F32)
        craw = ctx.sb(stk, "craw1", [64, 2 * 512], BF16)
        ctx_sb = ctx.sb(stk, "ctx1", [128, NCC * T], BF16)
        wout = ctx.sb(stk, "wout", [128, NCC * 1024], BF16)
        ga_col = ctx.sb(stk, "ga_col1", [128, 8], F32)
        xr = ctx.sb(stk, "xr1", [128, 2 * 512], BF16)
        t1 = ctx.sb(stk, "t1x", [128, 2 * 512], F32)
        ob = ctx.sb(stk, "ob1", [128, 2 * 512], BF16)
        PSS = [ctx.ps(stk, f"PSS{i}", [128, 1024], F32) for i in range(2)]
        PSO = [ctx.ps(stk, f"PSO{i}", [128, 512], F32) for i in range(2)]
        PSB = ctx.ps(stk, "PSB", [128, 512], F32)
        trb = [PSS[0][:, 0:64].bitcast(BF16), PSS[0][:, 512:576].bitcast(BF16),
               PSS[1][:, 0:64].bitcast(BF16)]

        p = Prog(ctx)
        p.add("vector", lambda v: v.memset(vtm[:, :], 1.0))
        vtm_init = p.add("vector", lambda v: v.drain(), inc="dve")
        aq = ctx.rd("aq")
        head_done = {}
        ev_prev = {}
        exp_done = {}
        for h in range(H):
            lw = [("pe", head_done[h - 1]), ("dve", ev_prev[(h - 1, NQS - 1)])] if h > 0 else []
            al_q = p.dma("sync", qf[:, :], aq[h * 64:(h + 1) * 64, :], "al", waits=lw)
            al_k = p.dma("sync", kf[:, :], aq[C + h * 64: C + (h + 1) * 64, :], "al")
            al_v = p.dma("sync", vf[:, :], aq[2 * C + h * 64: 2 * C + (h + 1) * 64, :], "al")
            tr_ev = {}
            for i in range(32):
                w = [("al", al_v), ("dve", vtm_init)]
                if i >= 3:
                    w.append(("dve", tr_ev[i - 3]))
                if i == 0 and h > 0:
                    w.append(("dve", ev_prev[(h - 1, NQS - 1)]))
                b = trb[i % 3][:, 0:64]
                p.add("tensor", lambda t, i=i, b=b: t.transpose(
                    b, vf[:, i * 128:(i + 1) * 128], ctx.ident[0:64, 0:64]),
                    waits=w, inc="pe")
                pev = ctx.cnt["pe"]
                tr_ev[i] = p.add("vector", lambda v, i=i, b=b: v.tensor_copy(
                    vtm[:, i * 65:i * 65 + 64], b), waits=[("pe", pev)], inc="dve")
            p.add("scalar", lambda s: s.activation(sqq[:, :], qf[:, :], AF.Square),
                  waits=[("al", al_q)])
            a_q = p.add("scalar", lambda s: s.drain(), inc="act")
            nev = {}
            for tc in range(NTC):
                w = [("act", a_q)]
                if tc > 0:
                    w.append(("dve", nev[tc - 1]))
                p.add("tensor", lambda t, tc=tc: t.matmul(
                    PSB[0:1, :], ctx.ones_col[0:64, :], sqq[:, tc * 512:(tc + 1) * 512],
                    start=True, stop=True), waits=w, inc="pe")
                pev = ctx.cnt["pe"]
                nev[tc] = p.add("vector", lambda v, tc=tc: v.tensor_copy(
                    nq2[:, tc * 512:(tc + 1) * 512], PSB[0:1, :]),
                    waits=[("pe", pev)], inc="dve")
            p.add("scalar", lambda s: s.activation(sqq[:, :], kf[:, :], AF.Square),
                  waits=[("al", al_k), ("pe", ctx.cnt["pe"])])
            a_k = p.add("scalar", lambda s: s.drain(), inc="act")
            for tc in range(NTC):
                w = [("act", a_k), ("dve", nev[tc - 1] if tc else nev[NTC - 1])]
                p.add("tensor", lambda t, tc=tc: t.matmul(
                    PSB[0:1, :], ctx.ones_col[0:64, :], sqq[:, tc * 512:(tc + 1) * 512],
                    start=True, stop=True), waits=w, inc="pe")
                pev = ctx.cnt["pe"]
                nev[tc] = p.add("vector", lambda v, tc=tc: v.tensor_copy(
                    nk2[:, tc * 512:(tc + 1) * 512], PSB[0:1, :]),
                    waits=[("pe", pev)], inc="dve")
            p.add("vector", lambda v: v.reduce_max(k2m[:, 0:1], nk2[:, :], AX.X))
            dk = p.add("vector", lambda v: v.drain(), inc="dve")
            p.add("scalar", lambda s: s.activation(k2m[:, 1:2], k2m[:, 0:1], AF.Ln,
                                                   bias=ctx.cb(1e-20)[0:1, :]),
                  waits=[("dve", dk)])
            lk = p.add("scalar", lambda s: s.drain(), inc="act")
            p.add("scalar", lambda s: s.activation(m_row[:, :], nq2[:, :], AF.Ln,
                                                   bias=ctx.cb(1e-20)[0:1, :]))
            lq = p.add("scalar", lambda s: s.drain(), inc="act")
            p.add("vector", lambda v: v.tensor_scalar_add(m_row[:, :], m_row[:, :], k2m[:, 1:2]),
                  waits=[("act", lq), ("act", lk)])
            d2 = p.add("vector", lambda v: v.drain(), inc="dve")
            p.add("scalar", lambda s: s.activation(m_row[:, :], m_row[:, :], AF.Exp, scale=0.5),
                  waits=[("dve", d2)])
            me = p.add("scalar", lambda s: s.drain(), inc="act")
            bev = {}
            for qs in range(NQS):
                w = [("act", me)]
                if qs > 0:
                    w.append(("dve", bev[qs - 1]))
                p.add("vector", lambda v, qs=qs: v.reduce_max(
                    msr[:, qs:qs + 1], m_row[:, qs * 512:(qs + 1) * 512], AX.X), waits=w)
                dm = p.add("vector", lambda v: v.drain(), inc="dve")
                p.add("tensor", lambda t, qs=qs: t.matmul(
                    PSB[:, 0:1], ctx.negones_row_f[:, :], msr[:, qs:qs + 1],
                    start=True, stop=True), waits=[("dve", dm)], inc="pe")
                pev = ctx.cnt["pe"]
                bev[qs] = p.add("vector", lambda v, qs=qs: v.tensor_copy(
                    negm[:, qs:qs + 1], PSB[:, 0:1]), waits=[("pe", pev)], inc="dve")
            prep = p.add("vector", lambda v: v.drain(), inc="dve")
            tr_all = p.add("vector", lambda v: v.drain(), inc="dve")
            for qs in range(NQS):
                so = qs % 2
                for pair in range(16):
                    pb = pair % 2
                    for j in range(2):
                        kt = pair * 2 + j
                        w = [("dve", tr_all), ("dve", prep)]
                        key = (h, qs, pair - 2) if pair >= 2 else                               ((h, qs - 1, 14 + pair) if qs > 0 else
                               ((h - 1, NQS - 1, 14 + pair) if h > 0 else None))
                        if key in exp_done:
                            w.append(("act", exp_done[key]))
                        p.add("tensor", lambda t, kt=kt, qs=qs, pb=pb, j=j: t.matmul(
                            PSS[pb][:, j * 512:(j + 1) * 512],
                            kf[:, kt * 128:(kt + 1) * 128], qf[:, qs * 512:(qs + 1) * 512],
                            start=True, stop=True),
                            waits=w, inc="pe" if j == 1 else None)
                    pev = ctx.cnt["pe"]
                    w = [("pe", pev)]
                    if pair == 0:
                        okey = (h, qs - 2) if qs >= 2 else ((h - 1, NQS - 2 + qs) if h > 0 else None)
                        if okey in ev_prev:
                            w.append(("dve", ev_prev[okey]))
                    p.add("scalar", lambda s, pb=pb, qs=qs: s.activation(
                        pt[pb][:, :], PSS[pb][:, :], AF.Exp, bias=negm[:, qs:qs + 1]),
                        waits=w)
                    exp_done[(h, qs, pair)] = p.add("scalar", lambda s: s.drain(), inc="act")
                    for j in range(2):
                        kt = pair * 2 + j
                        w = [("act", exp_done[(h, qs, pair)])]
                        p.add("tensor", lambda t, kt=kt, so=so, pb=pb, j=j: t.matmul(
                            PSO[so][0:65, :], vtm[:, kt * 65:(kt + 1) * 65],
                            pt[pb][:, j * 512:(j + 1) * 512],
                            start=(kt == 0), stop=(kt == 31)),
                            waits=w, inc="pe" if kt == 31 else None)
                pv_done = ctx.cnt["pe"]
                w = [("pe", pv_done)]
                if (h, qs - 1) in ev_prev:
                    w.append(("dve", ev_prev[(h, qs - 1)]))
                p.add("vector", lambda v, so=so: v.reciprocal(rbuf[:, :], PSO[so][64:65, :]),
                      waits=w)
                dr = p.add("vector", lambda v: v.drain(), inc="dve")
                p.add("tensor", lambda t: t.matmul(
                    PSB[0:64, :], ctx.ones_row_f[:, 0:64], rbuf[:, :],
                    start=True, stop=True), waits=[("dve", dr)], inc="pe")
                pbv = ctx.cnt["pe"]
                a1 = p.add("scalar", lambda s, so=so: s.activation(
                    craw[:, so * 512:(so + 1) * 512], PSO[so][0:64, :], AF.Copy),
                    waits=[("pe", pbv)])
                a1d = p.add("scalar", lambda s: s.drain(), inc="act")
                dst = ctx_sb[(h % 2) * 64:(h % 2) * 64 + 64,
                             (h // 2) * T + qs * 512:(h // 2) * T + (qs + 1) * 512]
                ev_prev[(h, qs)] = p.add("vector", lambda v, so=so, dst=dst: v.tensor_tensor(
                    dst, craw[:, so * 512:(so + 1) * 512], PSB[0:64, :], ALU.mult),
                    waits=[("act", a1d)], inc="dve")
            head_done[h] = ctx.cnt["pe"]
        p.add("vector", lambda v: v.drain())
        ctx_all = p.add("vector", lambda v: v.engine_nop(), inc="dve")

        for cc in range(NCC):
            p.dma("gpsimd", wout[:, cc * 1024:(cc + 1) * 1024],
                  ctx.rd("w_out1")[cc * 128:(cc + 1) * 128, :], "wl")
        wl_all = ctx.cnt["wl"]
        al_ga = p.dma("sync", ga_col[:, :], ctx.rd("mod_cols")[1, 0], "al")
        ep = {}
        stv = {}
        for tc in range(NTC):
            for ec in range(NCC):
                for cc in range(NCC):
                    waits = [("dve", ctx_all)]
                    if cc == 0:
                        waits.append(("wl", wl_all))
                        key = (tc, ec - 2) if ec >= 2 else (tc - 1, NCC - 2 + ec)
                        if key in ep:
                            waits.append(("dve", ep[key]))
                    p.add("tensor", lambda t, ec=ec, cc=cc, tc=tc: t.matmul(
                        PSO[ec % 2][:, :], wout[:, cc * 1024 + ec * 128: cc * 1024 + (ec + 1) * 128],
                        ctx_sb[:, cc * T + tc * 512: cc * T + (tc + 1) * 512],
                        start=(cc == 0), stop=(cc == NCC - 1)),
                        waits=waits, inc="pe" if cc == NCC - 1 else None)
                pev = ctx.cnt["pe"]
                slot = ec % 2
                pkey = (tc, ec - 2) if ec >= 2 else (tc - 1, NCC - 2 + ec)
                alx = p.dma("sync", xr[:, slot * 512:(slot + 1) * 512],
                            ctx.rd("xs")[ec * 128:(ec + 1) * 128, tc * 512:(tc + 1) * 512],
                            "al", waits=[("dve", ep[pkey])] if pkey in ep else [])
                aw = [("pe", pev), ("al", al_ga)]
                if pkey in ep:
                    aw.append(("dve", ep[pkey]))
                a1 = p.add("scalar", lambda s, ec=ec, slot=slot: s.activation(
                    t1[:, slot * 512:(slot + 1) * 512], PSO[ec % 2][:, :], AF.Identity,
                    scale=ga_col[:, ec:ec + 1]), waits=aw)
                a1d = p.add("scalar", lambda s: s.drain(), inc="act")
                sw = [("st", stv[pkey])] if pkey in stv else []
                p.add("vector", lambda v, slot=slot: v.tensor_tensor(
                    ob[:, slot * 512:(slot + 1) * 512], t1[:, slot * 512:(slot + 1) * 512],
                    xr[:, slot * 512:(slot + 1) * 512], ALU.add),
                    waits=[("act", a1d), ("al", alx)] + sw)
                ep[(tc, ec)] = p.add("vector", lambda v: v.drain(), inc="dve")
                stv[(tc, ec)] = p.dma(
                    "gpsimd", ctx.wr("xs")[ec * 128:(ec + 1) * 128, tc * 512:(tc + 1) * 512],
                    ob[:, slot * 512:(slot + 1) * 512], "st",
                    waits=[("dve", ep[(tc, ec)])])
        p.play("l1a_attn")
    ctx.barrier()



@stage(reads=["xs", "mod_rows"], writes=["l1f_h"])
def emit_l1f_h(ctx):
    nc = ctx.nc
    with contextlib.ExitStack() as stk:
        xt = ctx.sb(stk, "xtfh", [128, 2 * T], BF16)
        hb = ctx.sb(stk, "hbfh", [128, 2 * 512], BF16)
        sq = ctx.sb(stk, "sqfh", [128, 1024], BF16)
        srow = ctx.sb(stk, "srowfh", [1, 2048], F32)
        r_row = ctx.sb(stk, "r_rowfh", [1, T], BF16)
        negmr_row = ctx.sb(stk, "negmrfh", [1, T], BF16)
        sb_row = ctx.sb(stk, "sb_rowfh", [1, 1024], BF16)
        cb_row = ctx.sb(stk, "cb_rowfh", [1, 1024], BF16)
        t1 = ctx.sb(stk, "t1fh", [128, 512], F32)
        ps_sum = ctx.ps(stk, "ps_sumfh", [128, 512], F32)
        ps_sq = ctx.ps(stk, "ps_sqfh", [128, 512], F32)
        ps_sb = [ctx.ps(stk, f"ps_sbfh{i}", [128, 1024], F32) for i in range(2)]
        p = Prog(ctx)
        mr = ctx.rd("mod_rows")
        al_sb = p.dma("gpsimd", sb_row[:, :], mr[1][None, 3 * 1024:4 * 1024], "al")
        al_cb = p.dma("gpsimd", cb_row[:, :], mr[1][None, 4 * 1024:5 * 1024], "al")
        sdve = None
        h_done = {}
        st_done = {}
        for tc in range(NTC):
            xbase = (tc % 2) * T
            xwaits = []
            for cc in range(NCC):
                w = []
                if tc >= 2:
                    w.append(("dve", h_done[(tc - 2, NCC - 1)]))
                v = p.dma("sync", xt[:, xbase + cc * 512: xbase + (cc + 1) * 512],
                          ctx.rd("xs")[cc * 128:(cc + 1) * 128, tc * 512:(tc + 1) * 512],
                          "al", waits=w)
                xwaits.append(("al", v))
            xv = [xt[:, xbase + cc * 512: xbase + (cc + 1) * 512] for cc in range(NCC)]
            sdve = _emit_stats_tc(p, ctx, tc, xv, xwaits, sq, ps_sum, ps_sq,
                                  r_row, negmr_row, srow, 1e-5,
                                  prev_sdve=(sdve if tc > 0 else None))
            for cc in range(NCC):
                sl = cc % 2
                ew = [("al", al_sb), ("al", al_cb)]
                key = (tc, cc - 2) if cc >= 2 else (tc - 1, NCC - 2 + cc)
                if key in st_done:
                    ew.append(("st", st_done[key]))
                if key in h_done:
                    ew.append(("dve", h_done[key]))
                h_done[(tc, cc)] = _emit_apply_tc(
                    p, ctx, tc, cc, xv[cc], sdve, sb_row[:, :], cb_row[:, :],
                    r_row[:, :], negmr_row[:, :], ps_sb[cc % 2],
                    hb[:, sl * 512:(sl + 1) * 512], t1[:, :], extra_waits=ew)
                st_done[(tc, cc)] = p.dma(
                    "gpsimd", ctx.wr("l1f_h")[cc * 128:(cc + 1) * 128, tc * 512:(tc + 1) * 512],
                    hb[:, sl * 512:(sl + 1) * 512], "st", waits=[("dve", h_done[(tc, cc)])])
        p.play("l1f_h")
    ctx.barrier()


def _proj_tc(p, ctx, w_sb, in_tiles, in_ready, psq, out_cb, nec=NCC, wl_all=None,
             carry=None):
    eps = {}
    for ec in range(nec):
        for cc in range(NCC):
            w = list(in_ready)
            if cc == 0:
                if wl_all:
                    w.append(("wl", wl_all))
                if ec >= 2:
                    w.append(eps[ec - 2])
                elif carry:
                    k = max(carry)
                    w.append(carry[k - 1 + ec])
            p.add("tensor", lambda t, ec=ec, cc=cc: t.matmul(
                psq[ec % 2][:, :], w_sb(ec, cc), in_tiles[cc],
                start=(cc == 0), stop=(cc == NCC - 1)),
                waits=w, inc="pe" if cc == NCC - 1 else None)
        eps[ec] = out_cb(ec, ctx.cnt["pe"])
    return eps


def make_l1f_nP(c):
    eq, ek, ev, in_q, in_k, in_v = DAG[c]
    reads = ["l1f_h", "we_in", "wq_in", "wk_in", "wv_in", "be_in", "ew_in",
             NODE_NAME[in_q], NODE_NAME[in_k], NODE_NAME[in_v]]
    @stage(reads=sorted(set(reads)), writes=["l1f_q", "l1f_qh", "l1f_kh", "l1f_vh"])
    def fn(ctx):
        _l1f_nP_impl(ctx, c)
    fn.__name__ = f"l1f_n{c}P"
    return fn


def _l1f_nP_impl(ctx, c):
    nc = ctx.nc
    eq, ek, ev, in_q, in_k, in_v = DAG[c]
    srcs = [NODE_NAME[in_q], NODE_NAME[in_k], NODE_NAME[in_v]]
    with contextlib.ExitStack() as stk:
        we = [ctx.sb(stk, f"weP{j}", [128, NCC * 1024], BF16) for j in range(3)]
        wh = [ctx.sb(stk, f"whP{j}", [128, NCC * 1024], BF16) for j in range(3)]
        at = ctx.sb(stk, "atP", [128, 3 * 4096], BF16)
        eb = ctx.sb(stk, "ebP", [128, 3 * 4096], BF16)
        ob = ctx.sb(stk, "obP", [128, 2 * 512], BF16)
        berow = ctx.sb(stk, "berow", [1, 3 * 1024], F32)
        ewrow = ctx.sb(stk, "ewrow", [1, 18], F32)
        becol = ctx.sb(stk, "becol", [128, 24], F32)
        ewcol = ctx.sb(stk, "ewcol", [128, 3], F32)
        PSQ = [ctx.ps(stk, f"PSQP{i}", [128, 512], F32) for i in range(2)]
        PST = [ctx.ps(stk, f"PSTP{i}", [128, 512], F32) for i in range(4)]

        p = Prog(ctx)
        widx = [("we_in", eq), ("we_in", ek), ("we_in", ev)]
        hidx = [("wq_in", c), ("wk_in", c), ("wv_in", c)]
        for j in range(3):
            for cc in range(NCC):
                p.dma("gpsimd", we[j][:, cc * 1024:(cc + 1) * 1024],
                      ctx.rd(widx[j][0])[widx[j][1], cc * 128:(cc + 1) * 128, :], "wl")
                p.dma("gpsimd", wh[j][:, cc * 1024:(cc + 1) * 1024],
                      ctx.rd(hidx[j][0])[hidx[j][1], cc * 128:(cc + 1) * 128, :], "wl")
        wl_all = ctx.cnt["wl"]
        al_be = [p.dma("sync", berow[:, j * 1024:(j + 1) * 1024],
                       ctx.rd("be_in")[None, (eq, ek, ev)[j], :], "al") for j in range(3)]
        al_ew = p.dma("sync", ewrow[:, :], ctx.rd("ew_in")[None, :], "al")
        p.ops["tensor"].append((lambda t: t.nop(), [("al", al_be[-1]), ("al", al_ew)], []))
        specs = []
        for j in range(3):
            for a in range(8):
                specs.append((berow[:, j * 1024 + a * 128: j * 1024 + (a + 1) * 128],
                              becol[:, j * 8 + a: j * 8 + a + 1]))
        rows_to_cols(p, specs, PST)
        rcv = ctx.cnt["dve"]
        for j in range(3):
            scalar_bcast_col(p, ewrow[:, 3 * c + j: 3 * c + j + 1], ewcol[:, j:j + 1],
                             PST[j], waits=[("dve", rcv)])
        ewv = ctx.cnt["dve"]
        for j in range(3):
            p.add("vector", lambda v, j=j: v.tensor_tensor(
                becol[:, j * 8:(j + 1) * 8], becol[:, j * 8:(j + 1) * 8],
                ewcol[:, j:j + 1].broadcast_to([128, 8]), ALU.mult),
                waits=[("dve", ewv)] if j == 0 else [])
        colv = p.add("vector", lambda v: v.drain(), inc="dve")
        if DEBUG_NP:
            p.dma("sync", ctx.wr("dbg_a")[0:128, 0:24], becol[:, :], "st", waits=[("dve", colv)])
            p.dma("sync", ctx.wr("dbg_a")[0:128, 24:27], ewcol[:, :], "st", waits=[("dve", colv)])
            p.dma("sync", ctx.wr("dbg_a")[0:1, 32:32 + 3072], berow[:, :], "st", waits=[("dve", colv)])

        st_e = {}
        st_h = {}
        prev_pe = {}
        carry = None
        for tc in range(NTC):
            loaded = {}
            tload = {}
            reg = {}
            for j, s in enumerate(srcs):
                if s in loaded:
                    tload[j] = tload[loaded[s]]
                    reg[j] = reg[loaded[s]]
                    continue
                loaded[s] = j
                reg[j] = j
                w = []
                if tc >= 1:
                    w.append(("pe", prev_pe[tc - 1]))
                vv = [p.dma("sync", at[:, j * 4096 + cc * 512: j * 4096 + (cc + 1) * 512],
                            ctx.rd(s)[cc * 128:(cc + 1) * 128, tc * 512:(tc + 1) * 512],
                            "al", waits=w if cc == 0 else []) for cc in range(NCC)]
                tload[j] = vv[-1]
            for j, sname in enumerate(("l1f_qh", "l1f_kh", "l1f_vh")):
                rj = reg[j]
                in_tiles = [at[:, rj * 4096 + cc * 512: rj * 4096 + (cc + 1) * 512]
                            for cc in range(NCC)]

                def edge_cb(ec, pev, j=j, tc=tc):
                    w = [("pe", pev), ("dve", colv)]
                    ek_ = (tc - 1, j, ec)
                    if st_e.get(ek_) is not None:
                        w.append(("st", st_e[ek_]))
                    p.add("scalar", lambda s, ec=ec, j=j: s.activation(
                        eb[:, j * 4096 + ec * 512: j * 4096 + (ec + 1) * 512],
                        PSQ[ec % 2][:, :], AF.Identity,
                        scale=ewcol[:, j:j + 1], bias=becol[:, j * 8 + ec: j * 8 + ec + 1]),
                        waits=w)
                    av = p.add("scalar", lambda s: s.drain(), inc="act")
                    if j == 0:
                        st_e[(tc, j, ec)] = p.dma(
                            "gpsimd", ctx.wr("l1f_q")[ec * 128:(ec + 1) * 128, tc * 512:(tc + 1) * 512],
                            eb[:, j * 4096 + ec * 512: j * 4096 + (ec + 1) * 512],
                            "st", waits=[("act", av)])
                    else:
                        st_e[(tc, j, ec)] = None
                    return ("act", av)

                eps1 = _proj_tc(p, ctx,
                                lambda ec, cc, j=j: we[j][:, cc * 1024 + ec * 128: cc * 1024 + (ec + 1) * 128],
                                in_tiles, [("al", tload[j]), ("dve", colv)], PSQ, edge_cb,
                                wl_all=wl_all, carry=carry)
                edge_all = eps1[NCC - 1]
                in2 = [eb[:, j * 4096 + cc * 512: j * 4096 + (cc + 1) * 512]
                       for cc in range(NCC)]
                scale2 = RSQ128 if j == 1 else 1.0

                def head_cb(ec, pev, j=j, sname=sname, tc=tc, scale2=scale2):
                    sl = ec % 2
                    w = [("pe", pev)]
                    hk = (tc, j, ec - 2) if ec >= 2 else \
                         ((tc, j - 1, NCC - 2 + ec) if j > 0 else (tc - 1, 2, NCC - 2 + ec))
                    if hk in st_h:
                        w.append(("st", st_h[hk]))
                    p.add("scalar", lambda s, ec=ec, sl=sl, scale2=scale2: s.activation(
                        ob[:, sl * 512:(sl + 1) * 512], PSQ[ec % 2][:, :], AF.Copy,
                        scale=float(scale2)), waits=w)
                    av = p.add("scalar", lambda s: s.drain(), inc="act")
                    st_h[(tc, j, ec)] = p.dma(
                        "gpsimd", ctx.wr(sname)[ec * 128:(ec + 1) * 128, tc * 512:(tc + 1) * 512],
                        ob[:, sl * 512:(sl + 1) * 512], "st", waits=[("act", av)])
                    return ("act", av)

                carry = _proj_tc(p, ctx,
                                 lambda ec, cc, j=j: wh[j][:, cc * 1024 + ec * 128: cc * 1024 + (ec + 1) * 128],
                                 in2, [edge_all], PSQ, head_cb, wl_all=wl_all, carry=eps1)
            prev_pe[tc] = ctx.cnt["pe"]
        p.play(f"l1f_n{c}P")
    ctx.barrier()


def make_l1f_nA(c):
    @stage(reads=["l1f_qh", "l1f_kh", "l1f_vh"], writes=["l1f_ctx"])
    def fn(ctx):
        _l1f_nA_impl(ctx, c)
    fn.__name__ = f"l1f_n{c}A"
    return fn


def _l1f_nA_impl(ctx, c):
    nc = ctx.nc
    H = 8
    NQS = 8
    with contextlib.ExitStack() as stk:
        kh = ctx.sb(stk, "khA", [128, NCC * T], BF16)
        vhtm = ctx.sb(stk, "vhtmA", [128, NCC * T], BF16)
        vf = ctx.sb(stk, "vfA", [128, T], BF16)
        qh = ctx.sb(stk, "qhA", [128, T], BF16)
        sqq = ctx.sb(stk, "sqqA", [128, T], BF16)
        nq2 = ctx.sb(stk, "nq2A", [1, T], BF16)
        nk2 = ctx.sb(stk, "nk2A", [1, T], BF16)
        m_row = ctx.sb(stk, "m_rowA", [1, T], BF16)
        k2m = ctx.sb(stk, "k2mA", [1, 8], F32)
        msr = ctx.sb(stk, "msrA", [1, 8], F32)
        negm = ctx.sb(stk, "negmA", [128, 8], F32)
        pt = [ctx.sb(stk, f"ptA{i}", [128, 1024], BF16) for i in range(2)]
        rbuf = ctx.sb(stk, "rbufA", [1, 512], F32)
        craw = ctx.sb(stk, "crawA", [128, 2 * 512], BF16)
        cob = ctx.sb(stk, "cobA", [128, 2 * 512], BF16)
        PSS = [ctx.ps(stk, f"PSSA{i}", [128, 1024], F32) for i in range(2)]
        PSO = [ctx.ps(stk, f"PSOA{i}", [128, 512], F32) for i in range(2)]
        PSB = ctx.ps(stk, "PSBA", [128, 512], F32)
        PSN = ctx.ps(stk, "PSNA", [128, 512], F32)
        trb = [PSS[0][:, 0:64].bitcast(BF16), PSS[0][:, 512:576].bitcast(BF16),
               PSS[1][:, 0:64].bitcast(BF16)]

        p = Prog(ctx)
        for cc in range(NCC):
            p.dma("gpsimd", kh[:, cc * T:(cc + 1) * T],
                  ctx.rd("l1f_kh")[cc * 128:(cc + 1) * 128, :], "wl")
        tr_ev = {}
        gi = 0
        trpe = {}
        for h in range(H):
            alv = p.dma("sync", vf[:, :], ctx.rd("l1f_vh")[h * 128:(h + 1) * 128, :], "al",
                        waits=[("pe", trpe[h - 1])] if h > 0 else [])
            for i in range(32):
                w = [("al", alv)]
                if gi >= 3:
                    w.append(("dve", tr_ev[gi - 3]))
                b = trb[gi % 3]
                p.add("tensor", lambda t, i=i, b=b: t.transpose(
                    b[:, 0:128], vf[:, i * 128:(i + 1) * 128], ctx.ident[:]),
                    waits=w, inc="pe")
                pev = ctx.cnt["pe"]
                tr_ev[gi] = p.add("vector", lambda v, h=h, i=i, b=b: v.tensor_copy(
                    vhtm[:, h * T + i * 128: h * T + (i + 1) * 128], b[:, 0:128]),
                    waits=[("pe", pev)], inc="dve")
                gi += 1
            trpe[h] = ctx.cnt["pe"]
        p.add("vector", lambda v: v.drain())
        tr_all = p.add("vector", lambda v: v.engine_nop(), inc="dve")
        wl_kh = ctx.cnt["wl"]

        head_done = {}
        ev_prev = {}
        exp_done = {}
        st_done = {}
        for h in range(H):
            lw = [("pe", head_done[h - 1]), ("dve", ev_prev[(h - 1, NQS - 1)])] if h > 0 \
                 else [("dve", tr_all)]
            al_q = p.dma("sync", qh[:, :], ctx.rd("l1f_qh")[h * 128:(h + 1) * 128, :],
                         "al", waits=lw)
            p.add("scalar", lambda s: s.activation(sqq[:, :], qh[:, :], AF.Square),
                  waits=[("al", al_q)])
            a_q = p.add("scalar", lambda s: s.drain(), inc="act")
            nev = {}
            for tc in range(NTC):
                w = [("act", a_q), ("dve", tr_all)]
                if tc > 0:
                    w.append(("dve", nev[tc - 1]))
                p.add("tensor", lambda t, tc=tc: t.matmul(
                    PSB[0:1, :], ctx.ones_col[:, :], sqq[:, tc * 512:(tc + 1) * 512],
                    start=True, stop=True), waits=w, inc="pe")
                pev = ctx.cnt["pe"]
                nev[tc] = p.add("vector", lambda v, tc=tc: v.tensor_copy(
                    nq2[:, tc * 512:(tc + 1) * 512], PSB[0:1, :]),
                    waits=[("pe", pev)], inc="dve")
            p.add("scalar", lambda s, h=h: s.activation(
                sqq[:, :], kh[:, h * T:(h + 1) * T], AF.Square),
                waits=[("wl", wl_kh), ("pe", ctx.cnt["pe"])])
            a_k = p.add("scalar", lambda s: s.drain(), inc="act")
            for tc in range(NTC):
                w = [("act", a_k), ("dve", nev[tc - 1] if tc else nev[NTC - 1])]
                p.add("tensor", lambda t, tc=tc: t.matmul(
                    PSB[0:1, :], ctx.ones_col[:, :], sqq[:, tc * 512:(tc + 1) * 512],
                    start=True, stop=True), waits=w, inc="pe")
                pev = ctx.cnt["pe"]
                nev[tc] = p.add("vector", lambda v, tc=tc: v.tensor_copy(
                    nk2[:, tc * 512:(tc + 1) * 512], PSB[0:1, :]),
                    waits=[("pe", pev)], inc="dve")
            p.add("vector", lambda v: v.reduce_max(k2m[:, 0:1], nk2[:, :], AX.X))
            dk = p.add("vector", lambda v: v.drain(), inc="dve")
            lk = p.add("scalar", lambda s: s.activation(k2m[:, 1:2], k2m[:, 0:1], AF.Ln,
                                                        bias=ctx.cb(1e-20)[0:1, :]),
                       waits=[("dve", dk)])
            lkd = p.add("scalar", lambda s: s.drain(), inc="act")
            lq = p.add("scalar", lambda s: s.activation(m_row[:, :], nq2[:, :], AF.Ln,
                                                        bias=ctx.cb(1e-20)[0:1, :]))
            lqd = p.add("scalar", lambda s: s.drain(), inc="act")
            p.add("vector", lambda v: v.tensor_scalar_add(m_row[:, :], m_row[:, :], k2m[:, 1:2]),
                  waits=[("act", lqd), ("act", lkd)])
            d2 = p.add("vector", lambda v: v.drain(), inc="dve")
            p.add("scalar", lambda s: s.activation(m_row[:, :], m_row[:, :], AF.Exp, scale=0.5),
                  waits=[("dve", d2)])
            me = p.add("scalar", lambda s: s.drain(), inc="act")
            bev = {}
            for qs in range(NQS):
                w = [("act", me)]
                if qs > 0:
                    w.append(("dve", bev[qs - 1]))
                p.add("vector", lambda v, qs=qs: v.reduce_max(
                    msr[:, qs:qs + 1], m_row[:, qs * 512:(qs + 1) * 512], AX.X), waits=w)
                dm = p.add("vector", lambda v: v.drain(), inc="dve")
                p.add("tensor", lambda t, qs=qs: t.matmul(
                    PSB[:, 0:1], ctx.negones_row_f[:, :], msr[:, qs:qs + 1],
                    start=True, stop=True), waits=[("dve", dm)], inc="pe")
                pev = ctx.cnt["pe"]
                bev[qs] = p.add("vector", lambda v, qs=qs: v.tensor_copy(
                    negm[:, qs:qs + 1], PSB[:, 0:1]), waits=[("pe", pev)], inc="dve")
            prep = p.add("vector", lambda v: v.drain(), inc="dve")
            for qs in range(NQS):
                so = qs % 2
                for pair in range(16):
                    pb = pair % 2
                    for j in range(2):
                        kt = pair * 2 + j
                        w = [("dve", prep)]
                        key = (h, qs, pair - 2) if pair >= 2 else \
                              ((h, qs - 1, 14 + pair) if qs > 0 else
                               ((h - 1, NQS - 1, 14 + pair) if h > 0 else None))
                        if key in exp_done:
                            w.append(("act", exp_done[key]))
                        p.add("tensor", lambda t, kt=kt, qs=qs, pb=pb, j=j, h=h: t.matmul(
                            PSS[pb][:, j * 512:(j + 1) * 512],
                            kh[:, h * T + kt * 128: h * T + (kt + 1) * 128],
                            qh[:, qs * 512:(qs + 1) * 512],
                            start=True, stop=True),
                            waits=w, inc="pe" if j == 1 else None)
                    pev = ctx.cnt["pe"]
                    w = [("pe", pev)]
                    if pair == 0:
                        okey = (h, qs - 2) if qs >= 2 else ((h - 1, NQS - 2 + qs) if h > 0 else None)
                        if okey in ev_prev:
                            w.append(("dve", ev_prev[okey]))
                    p.add("scalar", lambda s, pb=pb, qs=qs: s.activation(
                        pt[pb][:, :], PSS[pb][:, :], AF.Exp, bias=negm[:, qs:qs + 1]),
                        waits=w)
                    exp_done[(h, qs, pair)] = p.add("scalar", lambda s: s.drain(), inc="act")
                    for j in range(2):
                        kt = pair * 2 + j
                        w = [("act", exp_done[(h, qs, pair)])]
                        p.add("tensor", lambda t, kt=kt, so=so, pb=pb, j=j, h=h: t.matmul(
                            PSO[so][:, :], vhtm[:, h * T + kt * 128: h * T + (kt + 1) * 128],
                            pt[pb][:, j * 512:(j + 1) * 512],
                            start=(kt == 0), stop=(kt == 31)), waits=w)
                        p.add("tensor", lambda t, kt=kt, pb=pb, j=j: t.matmul(
                            PSN[0:1, :], ctx.ones_col[:, :],
                            pt[pb][:, j * 512:(j + 1) * 512],
                            start=(kt == 0), stop=(kt == 31)),
                            inc="pe" if kt == 31 else None)
                pv_done = ctx.cnt["pe"]
                w = [("pe", pv_done)]
                if (h, qs - 1) in ev_prev:
                    w.append(("dve", ev_prev[(h, qs - 1)]))
                p.add("vector", lambda v: v.reciprocal(rbuf[:, :], PSN[0:1, :]), waits=w)
                dr = p.add("vector", lambda v: v.drain(), inc="dve")
                p.add("tensor", lambda t: t.matmul(
                    PSB[:, :], ctx.ones_row_f[:, 0:128], rbuf[:, :],
                    start=True, stop=True), waits=[("dve", dr)], inc="pe")
                pbv = ctx.cnt["pe"]
                p.add("scalar", lambda s, so=so: s.activation(
                    craw[:, so * 512:(so + 1) * 512], PSO[so][:, :], AF.Copy),
                    waits=[("pe", pbv)])
                a1d = p.add("scalar", lambda s: s.drain(), inc="act")
                skey = (h, qs - 2) if qs >= 2 else (h - 1, NQS - 2 + qs)
                sw = [("st", st_done[skey])] if skey in st_done else []
                p.add("vector", lambda v, so=so: v.tensor_tensor(
                    cob[:, so * 512:(so + 1) * 512], craw[:, so * 512:(so + 1) * 512],
                    PSB[:, :], ALU.mult), waits=[("act", a1d)] + sw)
                ev_prev[(h, qs)] = p.add("vector", lambda v: v.drain(), inc="dve")
                st_done[(h, qs)] = p.dma(
                    "gpsimd", ctx.wr("l1f_ctx")[h * 128:(h + 1) * 128, qs * 512:(qs + 1) * 512],
                    cob[:, so * 512:(so + 1) * 512], "st",
                    waits=[("dve", ev_prev[(h, qs)])])
            head_done[h] = ctx.cnt["pe"]
        p.play(f"l1f_n{c}A")
    ctx.barrier()


def make_l1f_nO(c):
    eq, ek, ev, in_q, in_k, in_v = DAG[c]
    @stage(reads=["l1f_ctx", "l1f_q", "wo_in", "actw_in"], writes=[NODE_NAME[c]])
    def fn(ctx):
        _l1f_nO_impl(ctx, c)
    fn.__name__ = f"l1f_n{c}O"
    return fn


def _l1f_nO_impl(ctx, c):
    nc = ctx.nc
    with contextlib.ExitStack() as stk:
        wo = ctx.sb(stk, "woO", [128, NCC * 1024], BF16)
        ct = ctx.sb(stk, "ctO", [128, T], BF16)
        qt = ctx.sb(stk, "qtO", [128, 2 * 512], BF16)
        awrow = ctx.sb(stk, "awrow", [1, 8], F32)
        awcol = ctx.sb(stk, "awcol", [128, 1], F32)
        t1 = ctx.sb(stk, "t1O", [128, 2 * 512], F32)
        ob = ctx.sb(stk, "obO", [128, 2 * 512], BF16)
        PSQ = [ctx.ps(stk, f"PSQO{i}", [128, 512], F32) for i in range(2)]
        PST = ctx.ps(stk, "PSTO", [128, 512], F32)
        p = Prog(ctx)
        for cc in range(NCC):
            p.dma("gpsimd", wo[:, cc * 1024:(cc + 1) * 1024],
                  ctx.rd("wo_in")[c, cc * 128:(cc + 1) * 128, :], "wl")
        wl_all = ctx.cnt["wl"]
        al_aw = p.dma("sync", awrow[:, 0:6], ctx.rd("actw_in")[None, :], "al")
        p.ops["tensor"].append((lambda t: t.nop(), [("al", al_aw)], []))
        scalar_bcast_col(p, awrow[:, c:c + 1], awcol[:, 0:1], PST)
        awv = p.add("vector", lambda v: v.drain(), inc="dve")
        ep = {}
        st_done = {}
        prev_pe = {}
        for tc in range(NTC):
            cload = []
            for cc in range(NCC):
                w = [("pe", prev_pe[tc - 1])] if (tc >= 1 and cc == 0) else []
                cload.append(p.dma("sync", ct[:, cc * 512:(cc + 1) * 512],
                                   ctx.rd("l1f_ctx")[cc * 128:(cc + 1) * 128,
                                                     tc * 512:(tc + 1) * 512],
                                   "al", waits=w))
            in_tiles = [ct[:, cc * 512:(cc + 1) * 512] for cc in range(NCC)]
            def out_cb(ec, pev, tc=tc):
                sl = ec % 2
                pkey = (tc, ec - 2) if ec >= 2 else (tc - 1, NCC - 2 + ec)
                alq = p.dma("sync", qt[:, sl * 512:(sl + 1) * 512],
                            ctx.rd("l1f_q")[ec * 128:(ec + 1) * 128, tc * 512:(tc + 1) * 512],
                            "al", waits=[ep[pkey]] if pkey in ep else [])
                w = [("pe", pev), ("dve", awv), ("al", alq)]
                if pkey in ep:
                    w.append(ep[pkey])
                p.add("scalar", lambda s, ec=ec, sl=sl: s.activation(
                    t1[:, sl * 512:(sl + 1) * 512], PSQ[ec % 2][:, :], AF.Identity,
                    scale=awcol[:, 0:1]), waits=w)
                a1d = p.add("scalar", lambda s: s.drain(), inc="act")
                sw = [("st", st_done[pkey])] if pkey in st_done else []
                p.add("vector", lambda v, sl=sl: v.tensor_tensor(
                    ob[:, sl * 512:(sl + 1) * 512], t1[:, sl * 512:(sl + 1) * 512],
                    qt[:, sl * 512:(sl + 1) * 512], ALU.add),
                    waits=[("act", a1d)] + sw)
                epv = p.add("vector", lambda v: v.drain(), inc="dve")
                st_done[(tc, ec)] = p.dma(
                    "gpsimd", ctx.wr(NODE_NAME[c])[ec * 128:(ec + 1) * 128, tc * 512:(tc + 1) * 512],
                    ob[:, sl * 512:(sl + 1) * 512], "st", waits=[("dve", epv)])
                ep[(tc, ec)] = ("dve", epv)
                return ("dve", epv)
            _proj_tc(p, ctx, lambda ec, cc: wo[:, cc * 1024 + ec * 128: cc * 1024 + (ec + 1) * 128],
                     in_tiles, [("al", cload[-1])], PSQ, out_cb, wl_all=wl_all)
            prev_pe[tc] = ctx.cnt["pe"]
        p.play(f"l1f_n{c}O")
    ctx.barrier()


@stage(reads=["n5", "xs", "lng_in", "lnb_in", "mod_rows"], writes=["out_f", "dbg_a"])
def emit_l1f_fin(ctx):
    nc = ctx.nc
    with contextlib.ExitStack() as stk:
        nt = ctx.sb(stk, "ntF", [128, 2 * T], BF16)
        xt = ctx.sb(stk, "xtF", [128, 2 * 512], BF16)
        sq = ctx.sb(stk, "sqF", [128, 1024], BF16)
        srow = ctx.sb(stk, "srowF", [1, 2048], F32)
        r_row = ctx.sb(stk, "r_rowF", [1, T], F32)
        negmr_row = ctx.sb(stk, "negmrF", [1, T], F32)
        g_row = ctx.sb(stk, "g_rowF", [1, 1024], F32)
        b_row = ctx.sb(stk, "b_rowF", [1, 1024], F32)
        gb_row = ctx.sb(stk, "gb_rowF", [1, 1024], F32)
        gbg = ctx.sb(stk, "gbgF", [1, 1024], F32)
        gbb = ctx.sb(stk, "gbbF", [1, 1024], F32)
        t1 = ctx.sb(stk, "t1F", [128, 512], F32)
        t2 = ctx.sb(stk, "t2F", [128, 512], F32)
        of = ctx.sb(stk, "ofF", [128, 2 * 512], F32)
        snapA = ctx.sb(stk, "snapA", [128, 512], F32)
        snapB = ctx.sb(stk, "snapB", [128, 512], F32)
        snapC = ctx.sb(stk, "snapC", [128, 512], F32)
        snapD = ctx.sb(stk, "snapD", [128, 512], F32)
        ps_sum = ctx.ps(stk, "ps_sumF", [128, 512], F32)
        ps_sq = ctx.ps(stk, "ps_sqF", [128, 512], F32)
        ps_sb = [ctx.ps(stk, f"ps_sbF{i}", [128, 1024], F32) for i in range(2)]
        p = Prog(ctx)
        a1 = p.dma("sync", g_row[:, :], ctx.rd("lng_in")[None, :], "al")
        a2 = p.dma("sync", b_row[:, :], ctx.rd("lnb_in")[None, :], "al")
        a3 = p.dma("sync", gb_row[:, :], ctx.rd("mod_rows")[1][None, 5 * 1024:6 * 1024], "al")
        p.add("vector", lambda v: v.tensor_tensor(gbg[:, :], g_row[:, :], gb_row[:, :], ALU.mult),
              waits=[("al", a1), ("al", a3)])
        p.add("vector", lambda v: v.tensor_tensor(gbb[:, :], b_row[:, :], gb_row[:, :], ALU.mult),
              waits=[("al", a2)])
        rowv = p.add("vector", lambda v: v.drain(), inc="dve")
        sdve = None
        h_done = {}
        st_done = {}
        for tc in range(NTC):
            nbase = (tc % 2) * T
            nwaits = []
            for cc in range(NCC):
                w = []
                if tc >= 2:
                    w.append(("dve", h_done[(tc - 2, NCC - 1)]))
                v = p.dma("sync", nt[:, nbase + cc * 512: nbase + (cc + 1) * 512],
                          ctx.rd("n5")[cc * 128:(cc + 1) * 128, tc * 512:(tc + 1) * 512],
                          "al", waits=w)
                nwaits.append(("al", v))
            nv = [nt[:, nbase + cc * 512: nbase + (cc + 1) * 512] for cc in range(NCC)]
            sdve = _emit_stats_tc(p, ctx, tc, nv, nwaits, sq, ps_sum, ps_sq,
                                  r_row, negmr_row, srow, 1e-6,
                                  prev_sdve=(sdve if tc > 0 else None))
            for cc in range(NCC):
                sl = cc % 2
                rslab = r_row[:, tc * 512:(tc + 1) * 512]
                nmr = negmr_row[:, tc * 512:(tc + 1) * 512]
                w = [("dve", sdve), ("dve", rowv)]
                key = (tc, cc - 2) if cc >= 2 else (tc - 1, NCC - 2 + cc)
                if key in h_done:
                    w.append(("dve", h_done[key]))
                p.add("tensor", lambda t, cc=cc, rslab=rslab: t.matmul(
                    ps_sb[cc % 2][:, 0:512], gbg[:, cc * 128:(cc + 1) * 128], rslab,
                    start=True, stop=True), waits=w)
                p.add("tensor", lambda t, cc=cc: t.matmul(
                    ps_sb[cc % 2][:, 512:1024], gbb[:, cc * 128:(cc + 1) * 128],
                    ctx.ones_row_f[:, :], start=True, stop=False))
                p.add("tensor", lambda t, cc=cc, nmr=nmr: t.matmul(
                    ps_sb[cc % 2][:, 512:1024], gbg[:, cc * 128:(cc + 1) * 128], nmr,
                    start=False, stop=True), inc="pe")
                pev = ctx.cnt["pe"]
                alx = p.dma("sync", xt[:, sl * 512:(sl + 1) * 512],
                            ctx.rd("xs")[cc * 128:(cc + 1) * 128, tc * 512:(tc + 1) * 512],
                            "al", waits=[("dve", h_done[key])] if key in h_done else [])
                p.add("vector", lambda v, cc=cc, nvc=nv[cc]: v.tensor_tensor(
                    t1[:, :], nvc, ps_sb[cc % 2][:, 0:512], ALU.mult),
                    waits=[("pe", pev)])
                p.add("vector", lambda v, cc=cc: v.drain())
                p.add("vector", lambda v, cc=cc: v.tensor_tensor(
                    t2[:, :], t1[:, :], ps_sb[cc % 2][:, 512:1024], ALU.add))
                p.add("vector", lambda v: v.drain())
                if DEBUG_FIN and tc == 0 and cc == 0:
                    p.add("vector", lambda v: v.tensor_copy(snapA[:, :], ps_sb[0][:, 0:512]))
                    p.add("vector", lambda v: v.tensor_copy(snapB[:, :], ps_sb[0][:, 512:1024]))
                    p.add("vector", lambda v: v.tensor_copy(snapC[:, :], t1[:, :]))
                    p.add("vector", lambda v: v.tensor_copy(snapD[:, :], t2[:, :]))
                    sdv = p.add("vector", lambda v: v.drain(), inc="dve")
                    p.dma("gpsimd", ctx.wr("dbg_a")[2:130, 0:512], snapA[:, :], "st", waits=[("dve", sdv)])
                    p.dma("gpsimd", ctx.wr("dbg_a")[130:258, 0:512], snapB[:, :], "st")
                    p.dma("gpsimd", ctx.wr("dbg_a")[258:386, 0:512], snapC[:, :], "st")
                    p.dma("gpsimd", ctx.wr("dbg_a")[386:514, 0:512], snapD[:, :], "st")
                sw = [("st", st_done[key])] if key in st_done else []
                p.add("vector", lambda v, sl=sl: v.tensor_tensor(
                    of[:, sl * 512:(sl + 1) * 512], t2[:, :], xt[:, sl * 512:(sl + 1) * 512],
                    ALU.add), waits=[("al", alx)] + sw)
                h_done[(tc, cc)] = p.add("vector", lambda v: v.drain(), inc="dve")
                st_done[(tc, cc)] = p.dma(
                    "sync", ctx.wr("out_f")[cc * 128:(cc + 1) * 128, tc * 512:(tc + 1) * 512],
                    of[:, sl * 512:(sl + 1) * 512], "st", waits=[("dve", h_done[(tc, cc)])])
        if DEBUG_FIN:
            p.dma("gpsimd", ctx.wr("dbg_a")[0:1, 0:T], r_row[:, :], "st",
                  waits=[("dve", h_done[(NTC - 1, NCC - 1)])])
            p.dma("gpsimd", ctx.wr("dbg_a")[1:2, 0:T], negmr_row[:, :], "st")
        p.add("sync", lambda hh: hh.nop(), waits=[("st", st_done[(NTC - 1, NCC - 1)])])
        p.play("l1f_fin")
    ctx.barrier()


STAGES_V1 = None


def make_stages():
    return [
        emit_xcast,
        emit_mod0,
        emit_mod1,
        make_ln_qkv_stage(0, True, None, "w_qkv0"),
        emit_l0a_attn,
        emit_l0f_wq,
        emit_l0f_main,
        make_ln_qkv_stage(1, False, 0.125, "w_qkv1"),
        emit_l1a_attn,
        emit_l1f_h,
    ] + [s for cN in range(6) for s in (make_l1f_nP(cN), make_l1f_nA(cN), make_l1f_nO(cN))] + [
        emit_l1f_fin,
    ]


def build_launch(stage_fns, force_out=()):
    nc = bass.Bass("TRN2", target_bir_lowering=False)
    reads, writes = set(), set()
    first = {}
    for fn in stage_fns:
        for n in getattr(fn, "reads", []):
            reads.add(n)
            first.setdefault(n, "r")
        for n in getattr(fn, "writes", []):
            writes.add(n)
            first.setdefault(n, "w")
    reads.add("ident_in")
    rd_aps, wr_aps = {}, {}
    for name in sorted(reads | writes):
        shape, dt = DRAM_REG[name]
        if name in EXTERNAL_IN:
            rd_aps[name] = nc.dram_tensor(name, shape, dt, kind="ExternalInput").ap()
            continue
        incoming = first[name] == "r"
        written = name in writes
        outgoing = written and (name == "out_f" or name in force_out)
        if incoming and outgoing:
            rd_aps[name] = nc.dram_tensor(name, shape, dt, kind="ExternalInput").ap()
            wr_aps[name] = nc.dram_tensor(name + "__o", shape, dt, kind="ExternalOutput").ap()
        elif incoming:
            ap = nc.dram_tensor(name, shape, dt, kind="ExternalInput").ap()
            rd_aps[name] = ap
            wr_aps[name] = ap
        elif outgoing:
            ap = nc.dram_tensor(name + "__o", shape, dt, kind="ExternalOutput").ap()
            rd_aps[name] = ap
            wr_aps[name] = ap
        else:
            ap = nc.dram_tensor(name, shape, dt, kind="Internal").ap()
            rd_aps[name] = ap
            wr_aps[name] = ap
    with contextlib.ExitStack() as stack:
        ctx = Ctx(nc, stack, rd_aps, wr_aps)
        emit_consts(ctx)
        for fn in stage_fns:
            fn(ctx)
    return nc


def host_prep(inputs, b):
    d = {}
    d["x_in"] = np.ascontiguousarray(inputs["L0_attn_x"][b].T)
    d["w_qkv0"] = np.ascontiguousarray(inputs["L0_attn_Wqkv"].T)
    d["w_proj0"] = np.ascontiguousarray(inputs["L0_attn_Wproj"].T)
    d["bproj0"] = inputs["L0_attn_bproj"]
    d["wg_in"] = inputs["L0_ffn_w_gate"]
    d["wu_in"] = inputs["L0_ffn_w_up"]
    d["wd_in"] = inputs["L0_ffn_w_down"]
    d["c0_in"] = np.ascontiguousarray(inputs["L0_c"][b])
    d["wmod0"] = inputs["L0_wmod"]
    wq1 = inputs["L1_attn_W_qkv"].reshape(C, 16, 64, 3).transpose(0, 3, 1, 2).reshape(C, 3072)
    d["w_qkv1"] = np.ascontiguousarray(wq1)
    d["w_out1"] = inputs["L1_attn_W_out"]
    d["c1_in"] = np.ascontiguousarray(inputs["L1_c"][b])
    d["wmod1"] = inputs["L1_wmod"]
    d["we_in"] = inputs["L1_ffn_We"]
    d["be_in"] = inputs["L1_ffn_be"]
    d["wq_in"] = inputs["L1_ffn_Wq"]
    d["wk_in"] = inputs["L1_ffn_Wk"]
    d["wv_in"] = inputs["L1_ffn_Wv"]
    d["wo_in"] = inputs["L1_ffn_Wo"]
    d["ew_in"] = np.ascontiguousarray(inputs["L1_ffn_ew"].reshape(-1))
    d["actw_in"] = inputs["L1_ffn_act_w"]
    d["lng_in"] = inputs["L1_ffn_ln_g"]
    d["lnb_in"] = inputs["L1_ffn_ln_b"]
    d["ident_in"] = np.eye(128, dtype=np.float32)
    return d


def kernel(**inputs):
    nc = build_launch(make_stages())
    in_maps = [host_prep(inputs, b) for b in range(8)]
    res = run_bass_kernel_spmd(nc, in_maps, core_ids=list(range(8)))
    out = np.stack([np.asarray(res.results[b]["out_f__o"]).T for b in range(8)])
    return np.ascontiguousarray(out.astype(np.float32))


# version 05 (final)
# speedup vs baseline: 1.4001x; 1.4001x over previous
import contextlib
import numpy as np

import concourse.bass as bass
import concourse.mybir as mybir
from concourse.bass_utils import run_bass_kernel_spmd

F32 = mybir.dt.float32
BF16 = mybir.dt.bfloat16
I32 = mybir.dt.int32
AF = mybir.ActivationFunctionType
ALU = mybir.AluOpType
AX = mybir.AxisListType

T = 4096
C = 1024
NCC = 8
TCH = 512
NTC = T // TCH
ENGINES = ("tensor", "scalar", "vector", "gpsimd", "sync")
BAR = {"tensor": "b_t", "scalar": "b_s", "vector": "b_v", "gpsimd": "b_g", "sync": "b_y"}
RSQ128 = float(1.0 / np.sqrt(128.0))

DRAM_REG = {
    "x_in":      ([C, T], F32),
    "out_f":     ([C, T], F32),
    "xs":        ([C, T], BF16),
    "mod_rows":  ([2, 6144], F32),
    "mod_cols":  ([2, 2, 128, 8], F32),
    "aq":        ([3 * C, T], BF16),
    "l1f_h":     ([C, T], BF16),
    "l1f_q":     ([C, T], BF16),
    "l1f_qh":    ([C, T], BF16),
    "l1f_kh":    ([C, T], BF16),
    "l1f_vh":    ([C, T], BF16),
    "l1f_ctx":   ([C, T], BF16),
    "n0": ([C, T], BF16), "n1": ([C, T], BF16), "n2": ([C, T], BF16),
    "n3": ([C, T], BF16), "n4": ([C, T], BF16), "n5": ([C, T], BF16),
    "dbg_a":     ([C, T], F32),
    "dbg_b":     ([2816, T], BF16),
    "wgq":       ([C, 2816], BF16),
    "wuq":       ([C, 2816], BF16),
    "wdq":       ([2816, C], BF16),
    "w_qkv0":    ([C, 3072], F32),
    "w_proj0":   ([C, C], F32),
    "bproj0":    ([C], F32),
    "wg_in":     ([2816, C], F32),
    "wu_in":     ([2816, C], F32),
    "wd_in":     ([C, 2816], F32),
    "c0_in":     ([C], F32),
    "wmod0":     ([C, 6144], F32),
    "w_qkv1":    ([C, 3072], F32),
    "w_out1":    ([C, C], F32),
    "c1_in":     ([C], F32),
    "wmod1":     ([C, 6144], F32),
    "we_in":     ([20, C, C], F32),
    "be_in":     ([20, C], F32),
    "wq_in":     ([6, C, C], F32),
    "wk_in":     ([6, C, C], F32),
    "wv_in":     ([6, C, C], F32),
    "wo_in":     ([6, C, C], F32),
    "ew_in":     ([18], F32),
    "actw_in":   ([6], F32),
    "lng_in":    ([C], F32),
    "lnb_in":    ([C], F32),
    "ident_in":  ([128, 128], F32),
}
EXTERNAL_IN = {"x_in", "w_qkv0", "w_proj0", "bproj0", "wg_in", "wu_in", "wd_in",
               "c0_in", "wmod0", "w_qkv1", "w_out1", "c1_in", "wmod1", "we_in",
               "be_in", "wq_in", "wk_in", "wv_in", "wo_in", "ew_in", "actw_in",
               "lng_in", "lnb_in", "ident_in"}

DAG = []
_lind = 0
for _c in range(6):
    _nsrc = min(_c + 1, 5)
    _snode = _c - _nsrc
    DAG.append((_lind + _nsrc - 1, _lind, _lind + _nsrc // 2,
                _c - 1, _snode, _snode + _nsrc // 2))
    _lind += _nsrc
NODE_NAME = {-1: "l1f_h", 0: "n0", 1: "n1", 2: "n2", 3: "n3", 4: "n4", 5: "n5"}


def stage(reads, writes):
    def deco(fn):
        fn.reads = reads
        fn.writes = writes
        return fn
    return deco


class Ctx:
    def __init__(self, nc, stack, rd_aps, wr_aps):
        self.nc = nc
        self.stack = stack
        self._rd = rd_aps
        self._wr = wr_aps
        self.sems = {}
        self.cnt = {}
        for name in ("wl", "al", "st", "pe", "act", "dve", "g2",
                     "b_t", "b_s", "b_v", "b_g", "b_y"):
            self.sems[name] = stack.enter_context(nc.semaphore(name))
            self.cnt[name] = 0
        self.bar_i = 0
        self.ones_row = self.sb(stack, "ones_row", [1, 512], BF16)
        self.ones_row_f = self.sb(stack, "ones_row_f", [1, 512], F32)
        self.negones_row_f = self.sb(stack, "negones_row_f", [1, 128], F32)
        self.ones_col = self.sb(stack, "ones_col", [128, 1], BF16)
        self.ones_col_f = self.sb(stack, "ones_col_f", [128, 1], F32)
        self.ident = self.sb(stack, "ident", [128, 128], BF16)
        self.ident_f = self.sb(stack, "ident_f", [128, 128], F32)
        self.cvals = {}
        for v in (1e-5, 1e-6, 1e-20):
            self.cvals[v] = self.sb(stack, "cval", [128, 1], F32)

    def cb(self, val):
        return self.cvals[val][:, :]

    _uid = 0

    def sb(self, stack, name, shape, dt):
        Ctx._uid += 1
        return stack.enter_context(self.nc.sbuf_tensor(f"{name}_{Ctx._uid}", shape, dt))

    def ps(self, stack, name, shape, dt):
        Ctx._uid += 1
        return stack.enter_context(self.nc.psum_tensor(f"{name}_{Ctx._uid}", shape, dt))

    def rd(self, name):
        return self._rd[name]

    def wr(self, name):
        return self._wr[name]

    def s(self, name):
        return self.sems[name]

    def bump(self, name, n=1):
        self.cnt[name] += n
        return self.cnt[name]

    def barrier(self):
        nc = self.nc
        self.bar_i += 1
        i = self.bar_i
        dma_targets = [(self.s(n), self.cnt[n]) for n in ("wl", "al", "st", "g2")]
        with nc.Block(name=f"bar{i}") as block:
            def mk(engname):
                def body(eng):
                    if engname == "sync":
                        for sem, v in dma_targets:
                            if v:
                                eng.wait_ge(sem, v)
                    eng.nop().then_inc(self.s(BAR[engname]), 1)
                    for other in ENGINES:
                        if other != engname:
                            eng.wait_ge(self.s(BAR[other]), i)
                return body
            block.tensor(mk("tensor"))
            block.scalar(mk("scalar"))
            block.vector(mk("vector"))
            block.gpsimd(mk("gpsimd"))
            block.sync(mk("sync"))
        for n in BAR.values():
            self.cnt[n] += 1


class Prog:

    def __init__(self, ctx):
        self.ctx = ctx
        self.ops = {e: [] for e in ENGINES}

    def add(self, eng, fn, waits=(), inc=None, inc_n=1):
        v = None
        incs = []
        if inc is not None:
            v = self.ctx.bump(inc, inc_n)
            incs = [(inc, inc_n)]
        self.ops[eng].append((fn, list(waits), incs))
        return v

    def dma(self, eng, out, in_, sem, waits=()):
        return self.add(eng, lambda h, o=out, i=in_: h.dma_start(out=o, in_=i),
                        waits=waits, inc=sem, inc_n=16)

    def play(self, name):
        ctx = self.ctx
        with ctx.nc.Block(name=name) as block:
            for eng in ENGINES:
                ops = self.ops[eng]
                if not ops:
                    continue
                def body(h, ops=ops):
                    for (fn, waits, incs) in ops:
                        for (s, v) in waits:
                            h.wait_ge(ctx.s(s), v)
                        inst = fn(h)
                        for (s, n) in incs:
                            inst.then_inc(ctx.s(s), n)
                getattr(block, eng)(body)
        self.ops = {e: [] for e in ENGINES}



def emit_consts(ctx):
    p = Prog(ctx)
    wl = p.dma("gpsimd", ctx.ident[:], ctx.rd("ident_in")[:, :], "wl")
    al = p.dma("sync", ctx.ident_f[:], ctx.rd("ident_in")[:, :], "al")
    p.add("vector", lambda v: v.memset(ctx.ones_row[:], 1.0))
    p.add("vector", lambda v: v.memset(ctx.ones_row_f[:], 1.0))
    p.add("vector", lambda v: v.memset(ctx.negones_row_f[:], -1.0))
    p.add("vector", lambda v: v.memset(ctx.ones_col[:], 1.0))
    p.add("vector", lambda v: v.memset(ctx.ones_col_f[:], 1.0))
    for val, tile in ctx.cvals.items():
        p.add("vector", lambda v, tile=tile, val=val: v.memset(tile[:], float(val)))
    p.play("consts")
    ctx.barrier()


def rows_to_cols(p, specs, ps_t, extra_waits=()):
    ctx = p.ctx
    evac_v = {}
    for i, (row, col) in enumerate(specs):
        waits = list(extra_waits) if i == 0 else []
        if i >= 4:
            waits.append(("dve", evac_v[i - 4]))
        p.add("tensor", lambda t, r=row, b=ps_t[i % 4]: t.matmul(
            b[:, 0:1], r, ctx.ones_row_f[:, 0:1], start=True, stop=True),
            waits=waits, inc="pe")
        pev = ctx.cnt["pe"]
        evac_v[i] = p.add("vector", lambda v, c=col, b=ps_t[i % 4]: v.tensor_copy(c, b[:, 0:1]),
                          waits=[("pe", pev)], inc="dve")


def scalar_bcast_col(p, src11, dst_col, ps, scale=1.0, negate=False, waits=()):
    ctx = p.ctx
    lhs = ctx.negones_row_f if negate else ctx.ones_row_f
    p.add("tensor", lambda t: t.matmul(ps[:, 0:1], lhs[:, 0:128], src11, start=True, stop=True),
          waits=list(waits), inc="pe")
    pev = ctx.cnt["pe"]
    if scale == 1.0:
        return p.add("vector", lambda v: v.tensor_copy(dst_col, ps[:, 0:1]),
                     waits=[("pe", pev)], inc="dve")
    return p.add("vector", lambda v: v.tensor_scalar_mul(dst_col, ps[:, 0:1], float(scale)),
                 waits=[("pe", pev)], inc="dve")



@stage(reads=["x_in"], writes=["xs"])
def emit_xcast(ctx):
    nc = ctx.nc
    with contextlib.ExitStack() as st:
        buf = ctx.sb(st, "xc", [128, 2 * T], BF16)
        p = Prog(ctx)
        stv = {}
        for cc in range(NCC):
            waits = []
            if cc >= 2:
                waits.append(("st", stv[cc - 2]))
            alv = p.dma("gpsimd", buf[:, (cc % 2) * T:(cc % 2) * T + T],
                        ctx.rd("x_in")[cc * 128:(cc + 1) * 128, :], "al", waits=waits)
            stv[cc] = p.dma("sync", ctx.wr("xs")[cc * 128:(cc + 1) * 128, :],
                            buf[:, (cc % 2) * T:(cc % 2) * T + T], "st",
                            waits=[("al", alv)])
        p.play("xcast")
    ctx.barrier()


def emit_mod(ctx, layer):
    nc = ctx.nc
    c_in = ctx.rd("c0_in" if layer == 0 else "c1_in")
    wmod = ctx.rd("wmod0" if layer == 0 else "wmod1")
    with contextlib.ExitStack() as st:
        c_row = ctx.sb(st, "c_row", [1, C], F32)
        c_col = ctx.sb(st, "c_col", [128, 8], F32)
        sc_col = ctx.sb(st, "sc_col", [128, 8], BF16)
        wm = ctx.sb(st, "wm", [128, 8 * 6144], BF16)
        mod = ctx.sb(st, "mod", [1, 6144], F32)
        rows = ctx.sb(st, "rows", [1, 6144], F32)
        cols = ctx.sb(st, "cols", [128, 16], F32)
        tmp = ctx.sb(st, "tmpm", [1, 1024], F32)
        ps_t = [ctx.ps(st, f"ps_t{i}", [128, 512], F32) for i in range(4)]
        ps_m = [ctx.ps(st, f"ps_m{i}", [128, 512], F32) for i in range(2)]

        p = Prog(ctx)
        al_c = p.dma("sync", c_row[:, :], c_in[None, :], "al")
        wlv = {}
        for cc in range(NCC):
            wlv[cc] = p.dma("gpsimd", wm[:, cc * 6144:(cc + 1) * 6144],
                            wmod[cc * 128:(cc + 1) * 128, :], "wl")
        p.ops["tensor"].append((lambda t: t.nop(), [("al", al_c)], []))
        rows_to_cols(p, [(c_row[:, a * 128:(a + 1) * 128], c_col[:, a:a + 1]) for a in range(8)],
                     ps_t)
        dve_c = p.add("vector", lambda v: v.drain(), inc="dve")
        p.add("scalar", lambda s: s.activation(sc_col[:, :], c_col[:, :], AF.Silu),
              waits=[("dve", dve_c)])
        actv = p.add("scalar", lambda s: s.drain(), inc="act")
        evac = {}
        for e in range(12):
            for cc in range(NCC):
                waits = [("act", actv), ("wl", wlv[cc])]
                if cc == 0 and e >= 2:
                    waits.append(("dve", evac[e - 2]))
                p.add("tensor",
                      lambda t, e=e, cc=cc: t.matmul(
                          ps_m[e % 2][0:1, :], sc_col[:, cc:cc + 1],
                          wm[:, cc * 6144 + e * 512: cc * 6144 + (e + 1) * 512],
                          start=(cc == 0), stop=(cc == NCC - 1)),
                      waits=waits, inc="pe" if cc == NCC - 1 else None)
            pev = ctx.cnt["pe"]
            evac[e] = p.add("vector",
                            lambda v, e=e: v.tensor_copy(mod[:, e * 512:(e + 1) * 512], ps_m[e % 2][0:1, :]),
                            waits=[("pe", pev)], inc="dve")
        p.play(f"mod{layer}_mm")

        p = Prog(ctx)
        dve_pp_w = [("dve", ctx.cnt["dve"])]
        p.add("vector", lambda v: v.tensor_copy(rows[:, 0:1024], mod[:, 0:1024]), waits=dve_pp_w)
        p.add("vector", lambda v: v.tensor_scalar_add(rows[:, 1024:2048], mod[:, 1024:2048], 1.0))
        p.add("vector", lambda v: v.tensor_copy(rows[:, 3 * 1024:4 * 1024], mod[:, 3 * 1024:4 * 1024]))
        p.add("vector", lambda v: v.tensor_scalar_add(rows[:, 4 * 1024:5 * 1024], mod[:, 4 * 1024:5 * 1024], 1.0))
        for src in (2, 5):
            g_ = mod[:, src * 1024:(src + 1) * 1024]
            r_ = rows[:, src * 1024:(src + 1) * 1024]
            p.add("vector", lambda v, g_=g_: v.tensor_scalar(tmp[:, :], g_, 0.0, 2.0, ALU.is_ge, ALU.mult))
            p.add("vector", lambda v: v.drain())
            p.add("vector", lambda v: v.tensor_scalar_sub(tmp[:, :], tmp[:, :], 1.0))
            p.add("vector", lambda v, g_=g_, r_=r_: v.tensor_scalar_mul(r_, g_, -1.0))
            p.add("vector", lambda v: v.drain())
            p.add("vector", lambda v, g_=g_, r_=r_: v.tensor_tensor(r_, r_, g_, ALU.max))
            p.add("vector", lambda v, r_=r_: v.tensor_scalar_max(r_, r_, 0.25))
            p.add("vector", lambda v: v.drain())
            p.add("vector", lambda v, r_=r_: v.tensor_tensor(r_, r_, tmp[:, :], ALU.mult))
        dve_rows = p.add("vector", lambda v: v.drain(), inc="dve")
        p.ops["tensor"].append((lambda t: t.nop(), [("dve", dve_rows)], []))
        specs = []
        for j, src in enumerate((2, 5)):
            for a in range(8):
                specs.append((rows[:, src * 1024 + a * 128: src * 1024 + (a + 1) * 128],
                              cols[:, j * 8 + a: j * 8 + a + 1]))
        rows_to_cols(p, specs, ps_t)
        dve_fin = p.add("vector", lambda v: v.drain(), inc="dve")
        p.dma("sync", ctx.wr("mod_rows")[layer][None, :], rows[:, :], "st",
              waits=[("dve", dve_fin)])
        p.dma("sync", ctx.wr("mod_cols")[layer, 0], cols[:, 0:8], "st",
              waits=[("dve", dve_fin)])
        p.dma("sync", ctx.wr("mod_cols")[layer, 1], cols[:, 8:16], "st",
              waits=[("dve", dve_fin)])
        p.play(f"mod{layer}_pp")
    ctx.barrier()


@stage(reads=["c0_in", "wmod0"], writes=["mod_rows", "mod_cols"])
def emit_mod0(ctx):
    emit_mod(ctx, 0)


@stage(reads=["c1_in", "wmod1"], writes=["mod_rows", "mod_cols"])
def emit_mod1(ctx):
    emit_mod(ctx, 1)


def _emit_stats_tc(p, ctx, tc, xv, xwaits, sq, ps_sum, ps_sq, r_row, negmr_row, srow,
                   eps, prev_sdve=None):
    sqmm = {}
    actv = {}
    for cc in range(NCC):
        w = [xwaits[cc]]
        if cc == 0 and prev_sdve is not None:
            w.append(("dve", prev_sdve))
        if cc >= 2:
            w.append(("pe", sqmm[cc - 2]))
        p.add("scalar", lambda s, cc=cc: s.activation(
            sq[:, (cc % 2) * 512:(cc % 2) * 512 + 512], xv[cc], AF.Square),
            waits=w)
        actv[cc] = p.add("scalar", lambda s: s.drain(), inc="act")
        p.add("tensor", lambda t, cc=cc: t.matmul(
            ps_sum[0:1, 0:512], ctx.ones_col[:, :], xv[cc],
            start=(cc == 0), stop=(cc == NCC - 1)), waits=[xwaits[cc]])
        sqmm[cc] = p.add("tensor", lambda t, cc=cc: t.matmul(
            ps_sq[0:1, 0:512], ctx.ones_col[:, :], sq[:, (cc % 2) * 512:(cc % 2) * 512 + 512],
            start=(cc == 0), stop=(cc == NCC - 1)),
            waits=[("act", actv[cc])], inc="pe")
    pev = sqmm[NCC - 1]
    m_ = srow[:, 0:512]
    es = srow[:, 512:1024]
    var = srow[:, 1024:1536]
    mr = srow[:, 1536:2048]
    p.add("vector", lambda v: v.tensor_scalar_mul(m_, ps_sum[0:1, 0:512], 1.0 / C),
          waits=[("pe", pev)])
    p.add("vector", lambda v: v.tensor_scalar_mul(es, ps_sq[0:1, 0:512], 1.0 / C))
    p.add("vector", lambda v: v.drain())
    p.add("vector", lambda v: v.tensor_tensor(var, m_, m_, ALU.mult))
    p.add("vector", lambda v: v.drain())
    p.add("vector", lambda v: v.tensor_tensor(var, es, var, ALU.subtract))
    dvev = p.add("vector", lambda v: v.drain(), inc="dve")
    p.add("scalar", lambda s: s.activation(var, var, AF.Ln, bias=ctx.cb(eps)[0:1, :]),
          waits=[("dve", dvev)])
    p.add("scalar", lambda s: s.drain())
    p.add("scalar", lambda s: s.activation(
        r_row[:, tc * 512:(tc + 1) * 512], var, AF.Exp, scale=-0.5))
    actv2 = p.add("scalar", lambda s: s.drain(), inc="act")
    p.add("vector", lambda v: v.tensor_tensor(mr, m_, r_row[:, tc * 512:(tc + 1) * 512], ALU.mult),
          waits=[("act", actv2)])
    p.add("vector", lambda v: v.drain())
    p.add("vector", lambda v: v.tensor_scalar_mul(
        negmr_row[:, tc * 512:(tc + 1) * 512], mr, -1.0))
    p.add("vector", lambda v: v.drain())
    return p.add("vector", lambda v: v.engine_nop(), inc="dve")


def _emit_apply_tc(p, ctx, tc, cc, xv_cc, stats_dve, sa_row, capa_row, r_row, negmr_row,
                   ps_pair, h_dst, t1, extra_waits=(), out_f32=False):
    rslab = r_row[:, tc * 512:(tc + 1) * 512]
    nmrslab = negmr_row[:, tc * 512:(tc + 1) * 512]
    waits = [("dve", stats_dve)] + list(extra_waits)
    p.add("tensor", lambda t: t.matmul(ps_pair[:, 0:512], capa_row[:, cc * 128:(cc + 1) * 128],
                                       rslab, start=True, stop=True), waits=waits)
    p.add("tensor", lambda t: t.matmul(ps_pair[:, 512:1024], sa_row[:, cc * 128:(cc + 1) * 128],
                                       ctx.ones_row[:, :], start=True, stop=False))
    p.add("tensor", lambda t: t.matmul(ps_pair[:, 512:1024], capa_row[:, cc * 128:(cc + 1) * 128],
                                       nmrslab, start=False, stop=True), inc="pe")
    pev = ctx.cnt["pe"]
    p.add("vector", lambda v: v.tensor_tensor(t1, xv_cc, ps_pair[:, 0:512], ALU.mult),
          waits=[("pe", pev)])
    p.add("vector", lambda v: v.tensor_tensor(h_dst, t1, ps_pair[:, 512:1024], ALU.add))
    p.add("vector", lambda v: v.drain())
    return p.add("vector", lambda v: v.engine_nop(), inc="dve")


def make_ln_qkv_stage(layer, phi, qscale, wname):
    @stage(reads=["xs", wname, "mod_rows"], writes=["aq"])
    def fn(ctx):
        _ln_qkv_impl(ctx, layer, phi, qscale, wname)
    fn.__name__ = f"l{layer}_qkv"
    return fn


def _ln_qkv_impl(ctx, layer, phi, qscale, wname):
    nc = ctx.nc
    eps = 1e-5
    NE = 24
    with contextlib.ExitStack() as stk:
        w = ctx.sb(stk, "wqkv", [128, NCC * 3072], BF16)
        xt = ctx.sb(stk, "xt", [128, 2 * T], BF16)
        hb = ctx.sb(stk, "hb", [128, T], BF16)
        sq = ctx.sb(stk, "sqt", [128, 1024], BF16)
        srow = ctx.sb(stk, "srow", [1, 2048], F32)
        r_row = ctx.sb(stk, "r_row", [1, T], BF16)
        negmr_row = ctx.sb(stk, "negmr_row", [1, T], BF16)
        sa_row = ctx.sb(stk, "sa_row", [1, 1024], BF16)
        capa_row = ctx.sb(stk, "capa_row", [1, 1024], BF16)
        t1 = ctx.sb(stk, "t1q", [128, 512], F32)
        e1 = ctx.sb(stk, "e1", [128, 512], F32)
        e2 = ctx.sb(stk, "e2", [128, 512], F32)
        ob = ctx.sb(stk, "ob", [128, 2 * 512], BF16)
        ps_sum = ctx.ps(stk, "ps_sum", [128, 512], F32)
        ps_sq = ctx.ps(stk, "ps_sq", [128, 512], F32)
        ps_sb = [ctx.ps(stk, f"ps_sb{i}", [128, 1024], F32) for i in range(2)]
        ps_q = [ctx.ps(stk, f"ps_q{i}", [128, 512], F32) for i in range(2)]

        p = Prog(ctx)
        for cc in range(NCC):
            p.dma("gpsimd", w[:, cc * 3072:(cc + 1) * 3072],
                  ctx.rd(wname)[cc * 128:(cc + 1) * 128, :], "wl")
        wl_all = ctx.cnt["wl"]
        mr = ctx.rd("mod_rows")
        al_sa = p.dma("gpsimd", sa_row[:, :], mr[layer][None, 0:1024], "al")
        al_ca = p.dma("gpsimd", capa_row[:, :], mr[layer][None, 1024:2048], "al")

        sdve = None
        h_done = {}
        ep_done = {}
        st_done = {}
        pe_of_e = {}
        for tc in range(NTC):
            xbase = (tc % 2) * T
            xwaits = []
            for cc in range(NCC):
                waits = []
                if tc >= 2 and (tc - 2, NE - 1) in ep_done:
                    waits.append(ep_done[(tc - 2, NE - 1)])
                    waits.append(("dve", h_done[(tc - 2, NCC - 1)]))
                v = p.dma("sync", xt[:, xbase + cc * 512: xbase + (cc + 1) * 512],
                          ctx.rd("xs")[cc * 128:(cc + 1) * 128, tc * 512:(tc + 1) * 512],
                          "al", waits=waits)
                xwaits.append(("al", v))
            xv = [xt[:, xbase + cc * 512: xbase + (cc + 1) * 512] for cc in range(NCC)]
            sdve = _emit_stats_tc(p, ctx, tc, xv, xwaits, sq, ps_sum, ps_sq,
                                  r_row, negmr_row, srow, eps,
                                  prev_sdve=(sdve if tc > 0 else None))
            for cc in range(NCC):
                ew = [("al", al_sa), ("al", al_ca)]
                if (tc * NCC + cc) >= 2:
                    pi = tc * NCC + cc - 2
                    ew.append(("dve", h_done[(pi // NCC, pi % NCC)]))
                h_done[(tc, cc)] = _emit_apply_tc(
                    p, ctx, tc, cc, xv[cc], sdve, sa_row[:, :], capa_row[:, :],
                    r_row[:, :], negmr_row[:, :],
                    ps_sb[cc % 2], hb[:, cc * 512:(cc + 1) * 512], t1[:, :], extra_waits=ew)
            h_all = h_done[(tc, NCC - 1)]
            for e in range(NE):
                for cc in range(NCC):
                    waits = [("dve", h_all)]
                    if cc == 0:
                        waits.append(("wl", wl_all))
                        key = (tc, e - 2) if e >= 2 else (tc - 1, NE - 2 + e)
                        if key in ep_done:
                            waits.append(ep_done[key])
                    p.add("tensor", lambda t, e=e, cc=cc, xb=xbase: t.matmul(
                        ps_q[e % 2][:, :],
                        w[:, cc * 3072 + e * 128: cc * 3072 + (e + 1) * 128],
                        hb[:, cc * 512:(cc + 1) * 512],
                        start=(cc == 0), stop=(cc == NCC - 1)),
                        waits=waits, inc="pe" if cc == NCC - 1 else None)
                pe_of_e[(tc, e)] = ctx.cnt["pe"]
                pw = [("pe", pe_of_e[(tc, e)])]
                slot = e % 2
                o_ap = ob[:, slot * 512:(slot + 1) * 512]
                sw = st_done.get((tc, e - 2)) or st_done.get((tc - 1, NE - 2 + e))
                sw = [("st", sw)] if sw else []
                if phi and e < 16:
                    p.add("vector", lambda v, e=e: v.tensor_scalar_min(e1[:, :], ps_q[e % 2][:, :], 0.0),
                          waits=pw + sw)
                    d1 = p.add("vector", lambda v: v.drain(), inc="dve")
                    a1 = p.add("scalar", lambda s: s.activation(e2[:, :], e1[:, :], AF.Exp),
                               waits=[("dve", d1)])
                    a1d = p.add("scalar", lambda s: s.drain(), inc="act")
                    p.add("vector", lambda v, e=e, o=o_ap: v.scalar_tensor_tensor(
                        o, ps_q[e % 2][:, :], 0.0, e2[:, :], ALU.max, ALU.add),
                        waits=[("act", a1d)])
                    epv = p.add("vector", lambda v: v.drain(), inc="dve")
                    ep_done[(tc, e)] = ("dve", epv)
                else:
                    sc_ = float(qscale) if (qscale is not None and e < 8) else 1.0
                    p.add("scalar", lambda s, e=e, o=o_ap, sc_=sc_: s.activation(
                        o, ps_q[e % 2][:, :], AF.Copy, scale=sc_),
                        waits=pw + sw)
                    epv = p.add("scalar", lambda s: s.drain(), inc="act")
                    ep_done[(tc, e)] = ("act", epv)
                st_done[(tc, e)] = p.dma(
                    "gpsimd", ctx.wr("aq")[e * 128:(e + 1) * 128, tc * 512:(tc + 1) * 512],
                    o_ap, "st", waits=[ep_done[(tc, e)]])
        p.play(f"l{layer}a_qkv" if phi else f"l{layer}_qkv")
    ctx.barrier()


@stage(reads=["aq", "w_proj0", "bproj0", "mod_cols", "xs"], writes=["xs"])
def emit_l0a_attn(ctx):
    nc = ctx.nc
    H = 16
    with contextlib.ExitStack() as stk:
        kf = ctx.sb(stk, "kf", [64, T], BF16)
        vf = ctx.sb(stk, "vf", [64, T], BF16)
        qf = ctx.sb(stk, "qf", [64, T], BF16)
        ktm = ctx.sb(stk, "ktm", [128, 32 * 64], BF16)
        vtm = ctx.sb(stk, "vtm", [128, 32 * 65], BF16)
        kvb = ctx.sb(stk, "kvb", [64, 16 * 65], BF16)
        ctx_sb = ctx.sb(stk, "ctx_sb", [128, NCC * T], BF16)
        rbuf = ctx.sb(stk, "rbuf", [1, 512], F32)
        craw = ctx.sb(stk, "craw", [64, 2 * 512], BF16)
        wpr = ctx.sb(stk, "wpr", [128, NCC * 1024], BF16)
        bp_row = ctx.sb(stk, "bp_row", [1, 1024], F32)
        ga_col = ctx.sb(stk, "ga_col", [128, 8], F32)
        bias_col = ctx.sb(stk, "bias_col", [128, 8], F32)
        xt = ctx.sb(stk, "xtp", [128, 2 * 512], BF16)
        t1 = ctx.sb(stk, "t1p", [128, 2 * 512], F32)
        ob = ctx.sb(stk, "obp", [128, 2 * 512], BF16)
        ps_tr = [ctx.ps(stk, f"ptr{i}", [128, 1024], BF16) for i in range(3)]
        ps_kv = ctx.ps(stk, "ps_kv", [128, 512], F32)
        ps_o = [ctx.ps(stk, f"ps_o{i}", [128, 512], F32) for i in range(2)]
        ps_b = [ctx.ps(stk, f"ps_b{i}", [128, 512], F32) for i in range(2)]

        p = Prog(ctx)
        p.add("vector", lambda v: v.memset(vtm[:, :], 1.0))
        vtm_init = p.add("vector", lambda v: v.engine_nop(), inc="dve")

        kv_done = {}
        kv_evac = {}
        tr_evac = {}
        tr_idx = 0
        aq = ctx.rd("aq")
        for h in range(H):
            al_k = p.dma("sync", kf[:, :], aq[C + h * 64: C + (h + 1) * 64, :], "al",
                         waits=([("pe", kv_done[h - 1])] if h > 0 else []))
            al_v = p.dma("sync", vf[:, :], aq[2 * C + h * 64: 2 * C + (h + 1) * 64, :], "al")
            ev_k, ev_v = {}, {}
            for i in range(32):
                for (src, dst_evmap, dst) in ((kf, ev_k, ktm[:, i * 64:(i + 1) * 64]),
                                              (vf, ev_v, vtm[:, i * 65:i * 65 + 64])):
                    waits = [("al", al_k if src is kf else al_v)]
                    if src is vf:
                        waits.append(("dve", vtm_init))
                    if tr_idx >= 3:
                        waits.append(("dve", tr_evac[tr_idx - 3]))
                    b = ps_tr[tr_idx % 3]
                    p.add("tensor", lambda t, src=src, i=i, b=b: t.transpose(
                        b[:, 0:64], src[:, i * 128:(i + 1) * 128], ctx.ident[0:64, 0:64]),
                        waits=waits, inc="pe")
                    pev = ctx.cnt["pe"]
                    tr_evac[tr_idx] = p.add("vector", lambda v, d=dst, b=b: v.tensor_copy(d, b[:, 0:64]),
                                            waits=[("pe", pev)], inc="dve")
                    dst_evmap[i] = tr_evac[tr_idx]
                    tr_idx += 1
            p.add("vector", lambda v: v.drain())
            trm = p.add("vector", lambda v: v.engine_nop(), inc="dve")
            for i in range(32):
                waits = [("dve", trm)]
                if i == 0 and h > 0:
                    waits.append(("dve", kv_evac[h - 1]))
                p.add("tensor", lambda t, i=i: t.matmul(
                    ps_kv[0:64, 0:65], ktm[:, i * 64:(i + 1) * 64], vtm[:, i * 65:(i + 1) * 65],
                    start=(i == 0), stop=(i == 31)),
                    waits=waits, inc="pe" if i == 31 else None)
            kv_done[h] = ctx.cnt["pe"]
            p.add("vector", lambda v, h=h: v.tensor_copy(
                kvb[:, h * 65:(h + 1) * 65], ps_kv[0:64, 0:65]),
                waits=[("pe", kv_done[h])])
            p.add("vector", lambda v: v.drain())
            kv_evac[h] = p.add("vector", lambda v: v.engine_nop(), inc="dve")

        ev_prev = {}
        for h in range(H):
            al_q = p.dma("sync", qf[:, :], aq[h * 64:(h + 1) * 64, :], "al",
                         waits=[("dve", kv_evac[h])])
            for tc in range(NTC):
                sl = tc % 2
                waits = [("al", al_q), ("dve", kv_evac[h])]
                key = (h, tc - 2) if tc >= 2 else (h - 1, NTC - 2 + tc)
                if key in ev_prev:
                    waits.append(("dve", ev_prev[key]))
                p.add("tensor", lambda t, h=h, tc=tc, sl=sl: t.matmul(
                    ps_o[sl][0:65, :], kvb[:, h * 65:(h + 1) * 65],
                    qf[:, tc * 512:(tc + 1) * 512], start=True, stop=True),
                    waits=waits, inc="pe")
                pev = ctx.cnt["pe"]
                p.add("vector", lambda v, sl=sl: v.tensor_scalar_add(rbuf[:, :], ps_o[sl][64:65, :], 1e-6),
                      waits=[("pe", pev)])
                d1 = p.add("vector", lambda v: v.reciprocal(rbuf[:, :], rbuf[:, :]), inc="dve")
                p.add("tensor", lambda t, sl=sl: t.matmul(
                    ps_b[sl][0:64, :], ctx.ones_row_f[:, 0:64], rbuf[:, :], start=True, stop=True),
                    waits=[("dve", d1)], inc="pe")
                pbv = ctx.cnt["pe"]
                cw = [("pe", pbv)]
                if key in ev_prev:
                    cw.append(("dve", ev_prev[key]))
                p.add("scalar", lambda s, sl=sl: s.activation(
                    craw[:, sl * 512:(sl + 1) * 512], ps_o[sl][0:64, :], AF.Copy),
                    waits=cw)
                a1 = p.add("scalar", lambda s: s.drain(), inc="act")
                dst = ctx_sb[(h % 2) * 64:(h % 2) * 64 + 64,
                             (h // 2) * T + tc * 512:(h // 2) * T + (tc + 1) * 512]
                ev_prev[(h, tc)] = p.add("vector", lambda v, sl=sl, dst=dst: v.tensor_tensor(
                    dst, craw[:, sl * 512:(sl + 1) * 512], ps_b[sl][0:64, :], ALU.mult),
                    waits=[("act", a1)], inc="dve")
        p.add("vector", lambda v: v.drain())
        ctx_all = p.add("vector", lambda v: v.engine_nop(), inc="dve")

        for cc in range(NCC):
            p.dma("gpsimd", wpr[:, cc * 1024:(cc + 1) * 1024],
                  ctx.rd("w_proj0")[cc * 128:(cc + 1) * 128, :], "wl")
        wl_all = ctx.cnt["wl"]
        al_bp = p.dma("sync", bp_row[:, :], ctx.rd("bproj0")[None, :], "al")
        al_ga = p.dma("sync", ga_col[:, :], ctx.rd("mod_cols")[0, 0], "al")
        rows_to_cols(p, [(bp_row[:, a * 128:(a + 1) * 128], bias_col[:, a:a + 1]) for a in range(8)],
                     [ps_o[0], ps_o[1], ps_b[0], ps_b[1]],
                     extra_waits=[("al", al_bp), ("dve", ctx_all)])
        bias_dve = ctx.cnt["dve"]
        p.add("vector", lambda v: v.tensor_tensor(bias_col[:, :], bias_col[:, :], ga_col[:, :], ALU.mult),
              waits=[("dve", bias_dve), ("al", al_ga)])
        p.add("vector", lambda v: v.drain())
        bdv = p.add("vector", lambda v: v.engine_nop(), inc="dve")
        ep = {}
        stv = {}
        for tc in range(NTC):
            for ec in range(NCC):
                for cc in range(NCC):
                    waits = [("dve", ctx_all), ("dve", bdv)]
                    if cc == 0:
                        waits.append(("wl", wl_all))
                        key = (tc, ec - 2) if ec >= 2 else (tc - 1, NCC - 2 + ec)
                        if key in ep:
                            waits.append(("dve", ep[key]))
                    p.add("tensor", lambda t, ec=ec, cc=cc: t.matmul(
                        ps_o[ec % 2][:, :], wpr[:, cc * 1024 + ec * 128: cc * 1024 + (ec + 1) * 128],
                        ctx_sb[:, cc * T + tc * 512: cc * T + (tc + 1) * 512],
                        start=(cc == 0), stop=(cc == NCC - 1)),
                        waits=waits, inc="pe" if cc == NCC - 1 else None)
                pev = ctx.cnt["pe"]
                slot = ec % 2
                pkey = (tc, ec - 2) if ec >= 2 else (tc - 1, NCC - 2 + ec)
                alx = p.dma("sync", xt[:, slot * 512:(slot + 1) * 512],
                            ctx.rd("xs")[ec * 128:(ec + 1) * 128, tc * 512:(tc + 1) * 512],
                            "al", waits=[("dve", ep[pkey])] if pkey in ep else [])
                aw = [("pe", pev), ("al", al_ga)]
                if pkey in ep:
                    aw.append(("dve", ep[pkey]))
                p.add("scalar", lambda s, ec=ec, slot=slot: s.activation(
                    t1[:, slot * 512:(slot + 1) * 512], ps_o[ec % 2][:, :], AF.Identity,
                    bias=bias_col[:, ec:ec + 1], scale=ga_col[:, ec:ec + 1]),
                    waits=aw)
                a1 = p.add("scalar", lambda s: s.drain(), inc="act")
                o_ap = ob[:, slot * 512:(slot + 1) * 512]
                sw = []
                if pkey in stv:
                    sw.append(("st", stv[pkey]))
                p.add("vector", lambda v, o=o_ap, slot=slot: v.tensor_tensor(
                    o, t1[:, slot * 512:(slot + 1) * 512], xt[:, slot * 512:(slot + 1) * 512], ALU.add),
                    waits=[("act", a1), ("al", alx)] + sw)
                ep[(tc, ec)] = p.add("vector", lambda v: v.drain(), inc="dve")
                stv[(tc, ec)] = p.dma(
                    "gpsimd", ctx.wr("xs")[ec * 128:(ec + 1) * 128, tc * 512:(tc + 1) * 512],
                    o_ap, "st", waits=[("dve", ep[(tc, ec)])])
        p.play("l0a_attn")
    ctx.barrier()



QUANT_DRAIN = "some"


def emit_quant_tile(p, ctx, x_ap, out_ap, F, scr, waits=()):
    nb = F // 16
    A = scr["A"][:, 0:F]
    B = scr["B"][:, 0:F]
    Cc = scr["C"][:, 0:F]
    D = scr["D"][:, 0:F]
    I = scr["I"][:, 0:F]
    SC = scr["SC"][:, 0:nb]
    RS = scr["RS"][:, 0:nb]
    SM = scr["SM"]
    x3 = x_ap.rearrange("p (b k) -> p b k", k=16)
    a3 = A.rearrange("p (b k) -> p b k", k=16)
    v3 = lambda ap, k: ap.rearrange("p (b j) -> p b j", j=k)
    _add = p.add

    def padd(eng, fn, drain=None, **kw):
        r = _add(eng, fn, **kw)
        if eng == "vector" and (QUANT_DRAIN == "all" or (QUANT_DRAIN == "some" and drain)):
            _add("vector", lambda v: v.drain())
        return r
    p = type("P", (), {"add": staticmethod(padd), "dma": p.dma, "ctx": ctx})()
    _add("vector", lambda v: v.drain())
    p.add("vector", lambda v: v.reduce_max(
        B[:, 0:nb].unsqueeze(2), x3, AX.X, apply_absolute_value=True),
        waits=list(waits), drain=True)
    p.add("vector", lambda v: v.tensor_scalar(SC, B[:, 0:nb], 1.0 / 6.0, 1e-12, ALU.mult, ALU.max),
          drain=True)
    p.add("vector", lambda v: v.reciprocal(RS, SC), drain=True)
    rs3 = RS.unsqueeze(2).broadcast_to([128, nb, 16])
    sc3 = SC.unsqueeze(2).broadcast_to([128, nb, 16])
    c3v = v3(Cc, 16)
    p.add("vector", lambda v: v.tensor_tensor(c3v, x3, rs3, ALU.mult), drain=True)
    p.add("vector", lambda v: v.tensor_scalar_mul(A, Cc, -1.0))
    p.add("vector", lambda v: v.tensor_tensor(A, Cc, A, ALU.max))
    p.add("vector", lambda v: v.tensor_scalar_mul(B, A, 2.0))
    p.add("vector", lambda v: v.tensor_copy(I, B), drain=True)
    p.add("vector", lambda v: v.tensor_scalar(B, I, 0.5, 2.0, ALU.mult, ALU.min))
    p.add("vector", lambda v: v.tensor_copy(I, A), drain=True)
    p.add("vector", lambda v: v.tensor_scalar(D, I, 2.0, 0.0, ALU.subtract, ALU.max))
    p.add("vector", lambda v: v.tensor_scalar_min(D, D, 2.0))
    p.add("vector", lambda v: v.tensor_tensor(B, B, D, ALU.add))
    p.add("vector", lambda v: v.tensor_scalar_mul(D, A, 0.5))
    p.add("vector", lambda v: v.tensor_copy(I, D), drain=True)
    p.add("vector", lambda v: v.tensor_scalar(D, I, 2.0, 4.0, ALU.mult, ALU.subtract))
    p.add("vector", lambda v: v.scalar_tensor_tensor(B, D, 0.0, B, ALU.max, ALU.add))
    smb = SM.broadcast_to([128, F])
    p.add("vector", lambda v: v.tensor_tensor(A.bitcast(I32), x_ap.bitcast(I32), smb, ALU.bitwise_and))
    p.add("vector", lambda v: v.tensor_tensor(Cc.bitcast(I32), B.bitcast(I32), A.bitcast(I32), ALU.bitwise_or), drain=True)
    o3 = out_ap.rearrange("p (b k) -> p b k", k=16)
    p.add("vector", lambda v: v.tensor_tensor(o3, v3(Cc, 16), sc3, ALU.mult))
    p.add("vector", lambda v: v.drain())
    return p.add("vector", lambda v: v.engine_nop(), inc="dve")


def make_quant_scratch(ctx, stk, F):
    return {
        "A": ctx.sb(stk, "qA", [128, F], F32)[:, :],
        "B": ctx.sb(stk, "qB", [128, F], F32)[:, :],
        "C": ctx.sb(stk, "qC", [128, F], F32)[:, :],
        "D": ctx.sb(stk, "qD", [128, F], F32)[:, :],
        "I": ctx.sb(stk, "qI", [128, F], I32)[:, :],
        "SC": ctx.sb(stk, "qSC", [128, F // 16], F32)[:, :],
        "RS": ctx.sb(stk, "qRS", [128, F // 16], F32)[:, :],
        "SM": ctx.sb(stk, "qSM", [128, 1], I32)[:, :],
    }


@stage(reads=["wg_in", "wu_in", "wd_in"], writes=["wgq", "wuq", "wdq"])
def emit_l0f_wq(ctx):
    nc = ctx.nc
    with contextlib.ExitStack() as stk:
        xin = ctx.sb(stk, "wqx", [128, 2 * 2816], F32)
        qt = ctx.sb(stk, "wqq", [128, 2 * 2816], BF16)
        tb = ctx.sb(stk, "wqt", [128, 4 * 128], BF16)
        scr = make_quant_scratch(ctx, stk, 2816)
        ps_tr = [ctx.ps(stk, f"wqp{i}", [128, 1024], BF16) for i in range(4)]
        p = Prog(ctx)
        p.add("vector", lambda v: v.memset(scr["SM"], -2147483648))
        jobs = [("wg_in", "wgq", 22, 1024), ("wu_in", "wuq", 22, 1024),
                ("wd_in", "wdq", 8, 2816)]
        tr_done = {}
        ev_done = {}
        st_done = {}
        gidx = 0
        ridx = 0
        for (src_n, dst_n, NR, F) in jobs:
            nt = F // 128
            for r in range(NR):
                sl = ridx % 2
                waits = []
                if ridx >= 2:
                    waits.append(("dve", q_done[ridx - 2]))
                alv = p.dma("gpsimd", xin[:, sl * 2816: sl * 2816 + F],
                            ctx.rd(src_n)[r * 128:(r + 1) * 128, :], "wl", waits=waits)
                qw = [("wl", alv)]
                if ridx >= 2:
                    qw.append(("pe", tr_done[ridx - 2]))
                if ridx == 0:
                    qw.append(("dve", ctx.cnt["dve"]))
                if 'q_done' not in dir():
                    q_done = {}
                q_done[ridx] = emit_quant_tile(
                    p, ctx, xin[:, sl * 2816: sl * 2816 + F],
                    qt[:, sl * 2816: sl * 2816 + F], F, scr, waits=qw)
                for j in range(nt):
                    w2 = [("dve", q_done[ridx])]
                    if gidx >= 4:
                        w2.append(("dve", ev_done[gidx - 4]))
                    p.add("tensor", lambda t, sl=sl, j=j, b=ps_tr[gidx % 4]: t.transpose(
                        b[:, 0:128], qt[:, sl * 2816 + j * 128: sl * 2816 + (j + 1) * 128],
                        ctx.ident[:]), waits=w2, inc="pe")
                    pev = ctx.cnt["pe"]
                    tsl = gidx % 4
                    w3 = [("pe", pev)]
                    if gidx >= 4 and (gidx - 4) in st_done:
                        w3.append(("st", st_done[gidx - 4]))
                    ev_done[gidx] = p.add("vector", lambda v, tsl=tsl, b=ps_tr[gidx % 4]: v.tensor_copy(
                        tb[:, tsl * 128:(tsl + 1) * 128], b[:, 0:128]),
                        waits=w3, inc="dve")
                    st_done[gidx] = p.dma(
                        "sync", ctx.wr(dst_n)[j * 128:(j + 1) * 128, r * 128:(r + 1) * 128],
                        tb[:, tsl * 128:(tsl + 1) * 128], "st",
                        waits=[("dve", ev_done[gidx])])
                    gidx += 1
                tr_done[ridx] = ctx.cnt["pe"]
                ridx += 1
        p.play("l0f_wq")
    ctx.barrier()


DEBUG_L0F = False
DEBUG_L0F_H = False
DEBUG_L0F_G = False
DEBUG_NP = False
DEBUG_L0F_S = False
DEBUG_FIN = False


@stage(reads=["xs", "wgq", "wuq", "wdq", "mod_rows", "mod_cols"], writes=["xs", "dbg_a", "dbg_b"])
def emit_l0f_main(ctx):
    nc = ctx.nc
    NI = 22
    with contextlib.ExitStack() as stk:
        wg = ctx.sb(stk, "wg", [128, NCC * 2816], BF16)
        wu = ctx.sb(stk, "wu", [128, NCC * 2816], BF16)
        wd = ctx.sb(stk, "wd", [128, 1024], BF16)
        xt = ctx.sb(stk, "xt0f", [128, T], BF16)
        hf = ctx.sb(stk, "hf", [128, NCC * 512], F32)
        uni = ctx.sb(stk, "uni0f", [128, 6656], F32)
        htm = uni[:, 0:4096]
        hqtm = ctx.sb(stk, "hqtm0f", [128, 4096], BF16)[:, :]
        h2qf = uni[:, 0:5632].bitcast(BF16)
        hqf = ctx.sb(stk, "hqf", [128, NCC * 512], BF16)
        h2 = ctx.sb(stk, "h2", [128, 512], F32)
        h2tm = ctx.sb(stk, "h2tm", [128, 4 * 128], F32)
        h2qtm = ctx.sb(stk, "h2qtm", [128, 4 * 128], BF16)
        gt = ctx.sb(stk, "gt", [128, 512], F32)
        ut = ctx.sb(stk, "ut", [128, 512], F32)
        sq = ctx.sb(stk, "sq0f", [128, 1024], BF16)
        srow = ctx.sb(stk, "srow0f", [1, 2048], F32)
        r_row = ctx.sb(stk, "r_row0f", [1, T], BF16)
        negmr_row = ctx.sb(stk, "negmr0f", [1, T], BF16)
        sb_row = ctx.sb(stk, "sb_row", [1, 1024], BF16)
        cbpa_row = ctx.sb(stk, "cbpa_row", [1, 1024], BF16)
        gb_col = ctx.sb(stk, "gb_col", [128, 8], F32)
        t1 = gt
        t1e = ctx.sb(stk, "t1e", [128, 2 * 512], F32)
        xr = ctx.sb(stk, "xr0f", [128, 2 * 512], BF16)
        ob = ctx.sb(stk, "ob0f", [128, 2 * 512], BF16)
        scr = {
            "A": hf[:, 0:1024], "B": hf[:, 1024:2048],
            "C": hf[:, 2048:3072], "D": hf[:, 3072:4096],
            "I": uni[:, 5632:6656].bitcast(I32),
            "SC": h2[:, 0:64],
            "RS": h2[:, 64:128],
            "SM": ctx.sb(stk, "qSM0f", [128, 1], I32)[:, :],
        }
        PA = ctx.ps(stk, "PA", [128, 1024], F32)
        PB = ctx.ps(stk, "PB", [128, 1024], F32)
        PG = ctx.ps(stk, "PG", [128, 512], F32)
        PU = ctx.ps(stk, "PU", [128, 512], F32)
        PS1 = ctx.ps(stk, "PS1", [128, 512], F32)
        PS2 = ctx.ps(stk, "PS2", [128, 512], F32)
        trf = [PB[:, 0:128], PB[:, 512:640], PG[:, 0:128]]
        trf2 = [PS1[:, 0:128], PS2[:, 0:128], PA[:, 0:128]]
        trb = [PB[:, 0:64].bitcast(BF16), PB[:, 512:576].bitcast(BF16),
               PG[:, 0:64].bitcast(BF16)]
        trb2 = [PA[:, 512:576].bitcast(BF16), PB[:, 0:64].bitcast(BF16),
                PB[:, 512:576].bitcast(BF16)]
        down_banks = [PA[:, 0:512], PA[:, 512:1024], PB[:, 0:512], PB[:, 512:1024]]

        p = Prog(ctx)
        p.add("vector", lambda v: v.memset(scr["SM"], -2147483648))
        for cc in range(NCC):
            p.dma("gpsimd", wg[:, cc * 2816:(cc + 1) * 2816],
                  ctx.rd("wgq")[cc * 128:(cc + 1) * 128, :], "wl")
            p.dma("gpsimd", wu[:, cc * 2816:(cc + 1) * 2816],
                  ctx.rd("wuq")[cc * 128:(cc + 1) * 128, :], "wl")
        wl_all = ctx.cnt["wl"]
        mr = ctx.rd("mod_rows")
        al_sb = p.dma("gpsimd", sb_row[:, :], mr[0][None, 3 * 1024:4 * 1024], "al")
        al_cb = p.dma("gpsimd", cbpa_row[:, :], mr[0][None, 4 * 1024:5 * 1024], "al")
        al_gb = p.dma("sync", gb_col[:, :], ctx.rd("mod_cols")[0, 1], "al")

        sdve = None
        h_done = {}
        res_done = {}
        st_done = {}
        act_guard = {}
        for tc in range(NTC):
            xbase = 0
            xwaits = []
            for cc in range(NCC):
                waits = []
                if tc >= 1 and (tc - 1) in act_guard:
                    waits.append(("act", act_guard[tc - 1]))
                    waits.append(("dve", h_done[(tc - 1, NCC - 1)]))
                if tc >= 2:
                    waits.append(("dve", res_done[(tc - 2, NCC - 1)]))
                v = p.dma("sync", xt[:, xbase + cc * 512: xbase + (cc + 1) * 512],
                          ctx.rd("xs")[cc * 128:(cc + 1) * 128, tc * 512:(tc + 1) * 512],
                          "al", waits=waits)
                xwaits.append(("al", v))
            xv = [xt[:, xbase + cc * 512: xbase + (cc + 1) * 512] for cc in range(NCC)]
            sdve = _emit_stats_tc(p, ctx, tc, xv, xwaits, sq, PS1, PS2,
                                  r_row, negmr_row, srow, 1e-5,
                                  prev_sdve=(sdve if tc > 0 else None))
            for cc in range(NCC):
                ew = [("al", al_sb), ("al", al_cb)]
                if (tc, cc - 1) in h_done:
                    ew.append(("dve", h_done[(tc, cc - 1)]))
                elif (tc - 1, NCC - 1) in h_done:
                    ew.append(("dve", h_done[(tc - 1, NCC - 1)]))
                h_done[(tc, cc)] = _emit_apply_tc(
                    p, ctx, tc, cc, xv[cc], sdve, sb_row[:, :], cbpa_row[:, :],
                    r_row[:, :], negmr_row[:, :], PA,
                    hf[:, cc * 512:(cc + 1) * 512], t1[:, :], extra_waits=ew)
            h_all = h_done[(tc, NCC - 1)]
            p.play(f"l0f_m{tc}pre")
            ctx.barrier()
            p = Prog(ctx)
            dump_st = None
            if DEBUG_L0F_H:
                for cc in range(NCC):
                    dump_st = p.dma("gpsimd",
                                    ctx.wr("dbg_a")[cc * 128:(cc + 1) * 128,
                                                    tc * 512:(tc + 1) * 512],
                                    hf[:, cc * 512:(cc + 1) * 512], "st",
                                    waits=[("dve", h_all)])
            gi = 0
            ev = {}
            for tt in range(4):
                for cc in range(NCC):
                    w2 = [("dve", h_all)]
                    if gi >= 3:
                        w2.append(("dve", ev[gi - 3]))
                    b = trf[gi % 3]
                    p.add("tensor", lambda t, tt=tt, cc=cc, b=b: t.transpose(
                        b, hf[:, cc * 512 + tt * 128: cc * 512 + (tt + 1) * 128],
                        ctx.ident_f[:]), waits=w2, inc="pe")
                    pev = ctx.cnt["pe"]
                    ev[gi] = p.add("vector", lambda v, tt=tt, cc=cc, b=b: v.tensor_copy(
                        htm[:, tt * 1024 + cc * 128: tt * 1024 + (cc + 1) * 128], b),
                        waits=[("pe", pev)], inc="dve")
                    gi += 1
            if DEBUG_L0F_H:
                for tt in range(4):
                    dump_st = p.dma("gpsimd",
                                    ctx.wr("dbg_b")[tt * 128:(tt + 1) * 128, tc * 512:(tc + 1) * 512],
                                    htm[:, tt * 1024: tt * 1024 + 512], "st",
                                    waits=[("dve", ev[31])])
                    dump_st = p.dma("gpsimd",
                                    ctx.wr("dbg_b")[512 + tt * 128: 512 + (tt + 1) * 128, tc * 512:(tc + 1) * 512],
                                    htm[:, tt * 1024 + 512: tt * 1024 + 1024], "st",
                                    waits=[("dve", ev[31])])
            qdv = {}
            for tt in range(4):
                qw = [("dve", ev[31])]
                if DEBUG_L0F_H and tt == 0:
                    qw.append(("st", dump_st))
                qdv[tt] = emit_quant_tile(p, ctx, htm[:, tt * 1024:(tt + 1) * 1024],
                                          hqtm[:, tt * 1024:(tt + 1) * 1024], 1024, scr,
                                          waits=qw)
            gi = 0
            ev2 = {}
            for cc in range(NCC):
                for tt in range(4):
                    w2 = [("dve", qdv[tt])]
                    if gi >= 3:
                        w2.append(("dve", ev2[gi - 3]))
                    else:
                        w2.append(("dve", ev[31]))
                    b = trb[gi % 3]
                    p.add("tensor", lambda t, tt=tt, cc=cc, b=b: t.transpose(
                        b, hqtm[:, tt * 1024 + cc * 128: tt * 1024 + (cc + 1) * 128],
                        ctx.ident[:]), waits=w2, inc="pe")
                    pev = ctx.cnt["pe"]
                    ev2[gi] = p.add("vector", lambda v, tt=tt, cc=cc, b=b: v.tensor_copy(
                        hqf[:, cc * 512 + tt * 128: cc * 512 + (tt + 1) * 128], b),
                        waits=[("pe", pev)], inc="dve")
                    gi += 1
            p.add("vector", lambda v: v.drain())
            hq_all = p.add("vector", lambda v: v.engine_nop(), inc="dve")
            p.play(f"l0f_m{tc}a")
            ctx.barrier()
            p = Prog(ctx)
            if DEBUG_L0F_S == "hq" and tc == 0:
                for cc in range(NCC):
                    p.dma("gpsimd", ctx.wr("dbg_a")[cc * 128:(cc + 1) * 128, 0:512],
                          hqf[:, cc * 512:(cc + 1) * 512], "st", waits=[("dve", hq_all)])
            if DEBUG_L0F_H:
                for tt in range(4):
                    p.dma("sync", ctx.wr("dbg_b")[1024 + tt * 128: 1024 + (tt + 1) * 128,
                                                  tc * 512:(tc + 1) * 512],
                          hqtm[:, tt * 1024: tt * 1024 + 512], "st", waits=[("dve", hq_all)])
                    p.dma("sync", ctx.wr("dbg_b")[1536 + tt * 128: 1536 + (tt + 1) * 128,
                                                  tc * 512:(tc + 1) * 512],
                          hqtm[:, tt * 1024 + 512: tt * 1024 + 1024], "st", waits=[("dve", hq_all)])
                for cc in range(4):
                    p.dma("sync", ctx.wr("dbg_b")[2048 + cc * 128: 2048 + (cc + 1) * 128,
                                                  tc * 512:(tc + 1) * 512],
                          hqf[:, cc * 512:(cc + 1) * 512], "st", waits=[("dve", hq_all)])
            if DEBUG_L0F and not DEBUG_L0F_H:
                for cc in range(NCC):
                    p.dma("sync", ctx.wr("dbg_a")[cc * 128:(cc + 1) * 128, tc * 512:(tc + 1) * 512],
                          hqf[:, cc * 512:(cc + 1) * 512], "st", waits=[("dve", hq_all)])
            h2q_done = {}
            gu_ev = {}
            for ec in range(NI):
                for cc in range(NCC):
                    w = [("dve", hq_all), ("wl", wl_all)]
                    if cc == 0 and ec >= 1:
                        w.append(("dve", gu_ev[ec - 1]))
                    p.add("tensor", lambda t, ec=ec, cc=cc: t.matmul(
                        PG[:, :], wg[:, cc * 2816 + ec * 128: cc * 2816 + (ec + 1) * 128],
                        hqf[:, cc * 512:(cc + 1) * 512],
                        start=(cc == 0), stop=(cc == NCC - 1)), waits=w)
                    p.add("tensor", lambda t, ec=ec, cc=cc: t.matmul(
                        PU[:, :], wu[:, cc * 2816 + ec * 128: cc * 2816 + (ec + 1) * 128],
                        hqf[:, cc * 512:(cc + 1) * 512],
                        start=(cc == 0), stop=(cc == NCC - 1)),
                        inc="pe" if cc == NCC - 1 else None)
                pev = ctx.cnt["pe"]
                aw = [("pe", pev)]
                if ec >= 1:
                    aw.append(("dve", gu_ev[ec - 1]))
                p.add("scalar", lambda s: s.activation(gt[:, :], PG[:, :], AF.Silu),
                      waits=aw)
                a1 = p.add("scalar", lambda s: s.drain(), inc="act")
                p.add("vector", lambda v: v.tensor_copy(ut[:, :], PU[:, :]),
                      waits=[("pe", pev), ("act", a1)])
                p.add("vector", lambda v: v.tensor_tensor(
                    h2[:, :], gt[:, :], ut[:, :], ALU.mult))
                p.add("vector", lambda v: v.drain())
                gu_ev[ec] = p.add("vector", lambda v: v.engine_nop(), inc="dve")
                if DEBUG_L0F_G:
                    p.dma("gpsimd", ctx.wr("dbg_b")[ec * 128:(ec + 1) * 128, tc * 512:(tc + 1) * 512],
                          h2[:, :], "st", waits=[("dve", gu_ev[ec])])
                    gu_ev[ec] = p.add("vector", lambda v: v.engine_nop(),
                                      waits=[("st", ctx.cnt["st"])], inc="dve")
                ev3 = {}
                for tt in range(4):
                    w2 = [("dve", gu_ev[ec])]
                    if tt >= 3:
                        w2.append(("dve", ev3[tt - 3]))
                    b = trf2[tt % 3]
                    p.add("tensor", lambda t, tt=tt, b=b: t.transpose(
                        b, h2[:, tt * 128:(tt + 1) * 128], ctx.ident_f[:]),
                        waits=w2, inc="pe")
                    pev = ctx.cnt["pe"]
                    ev3[tt] = p.add("vector", lambda v, tt=tt, b=b: v.tensor_copy(
                        h2tm[:, tt * 128:(tt + 1) * 128], b),
                        waits=[("pe", pev)], inc="dve")
                q2 = emit_quant_tile(p, ctx, h2tm[:, :], h2qtm[:, :], 512, scr,
                                     waits=[("dve", ev3[3])])
                ev4 = {}
                for tt in range(4):
                    w2 = [("dve", q2)]
                    if tt >= 3:
                        w2.append(("dve", ev4[tt - 3]))
                    b = trb2[tt % 3]
                    p.add("tensor", lambda t, tt=tt, b=b: t.transpose(
                        b, h2qtm[:, tt * 128:(tt + 1) * 128], ctx.ident[:]),
                        waits=w2, inc="pe")
                    pev = ctx.cnt["pe"]
                    ev4[tt] = p.add("vector", lambda v, tt=tt, ec=ec, b=b: v.tensor_copy(
                        h2qf[:, ec * 512 + tt * 128: ec * 512 + (tt + 1) * 128], b),
                        waits=[("pe", pev)], inc="dve")
                p.add("vector", lambda v: v.drain())
                h2q_done[ec] = p.add("vector", lambda v: v.engine_nop(), inc="dve")
                if DEBUG_L0F_S == "h2q" and tc == 0:
                    p.dma("sync", ctx.wr("dbg_b")[ec * 128:(ec + 1) * 128, 0:512],
                          h2qf[:, ec * 512:(ec + 1) * 512], "st", waits=[("dve", h2q_done[ec])])
                if DEBUG_L0F:
                    p.dma("sync", ctx.wr("dbg_b")[ec * 128:(ec + 1) * 128, tc * 512:(tc + 1) * 512],
                          h2qf[:, ec * 512:(ec + 1) * 512], "st", waits=[("dve", h2q_done[ec])])
            p.play(f"l0f_m{tc}b")
            ctx.barrier()
            p = Prog(ctx)
            eplist = []
            prev_dpe = None
            for half in range(2):
                wd_pe = {}
                wdl = {}
                for ec in range(NI):
                    w = [("dve", h2q_done[NI - 1])]
                    if ec >= 1:
                        w.append(("pe", wd_pe[ec - 1]))
                    elif prev_dpe is not None:
                        w.append(("pe", prev_dpe))
                    wdl[ec] = p.dma("gpsimd", wd[:, :],
                                    ctx.rd("wdq")[ec * 128:(ec + 1) * 128, :], "wl",
                                    waits=w)
                    for hj in range(4):
                        hc = half * 4 + hj
                        w2 = [("wl", wdl[ec])]
                        if ec == 0:
                            w2.append(("dve", h2q_done[NI - 1]))
                            if half == 1:
                                w2.append(("act", eplist[-1]))
                        p.add("tensor", lambda t, ec=ec, hj=hj, hc=hc: t.matmul(
                            down_banks[hj], wd[:, hc * 128:(hc + 1) * 128],
                            h2qf[:, ec * 512:(ec + 1) * 512],
                            start=(ec == 0), stop=(ec == NI - 1)),
                            inc="pe" if hj == 3 else None, waits=w2)
                    wd_pe[ec] = ctx.cnt["pe"]
                dpe = ctx.cnt["pe"]
                prev_dpe = dpe
                for hj in range(4):
                    hc = half * 4 + hj
                    sl = hc % 2
                    pkey = (tc, hc - 2) if hc >= 2 else (tc - 1, NCC - 2 + hc)
                    alx = p.dma("sync", xr[:, sl * 512:(sl + 1) * 512],
                                ctx.rd("xs")[hc * 128:(hc + 1) * 128, tc * 512:(tc + 1) * 512],
                                "al", waits=[("dve", res_done[pkey])] if pkey in res_done else [])
                    aw = [("pe", dpe), ("al", al_gb)]
                    if pkey in res_done:
                        aw.append(("dve", res_done[pkey]))
                    p.add("scalar", lambda s, hj=hj, hc=hc, sl=sl: s.activation(
                        t1e[:, sl * 512:(sl + 1) * 512], down_banks[hj], AF.Identity,
                        scale=gb_col[:, hc:hc + 1]), waits=aw)
                    a1 = p.add("scalar", lambda s: s.drain(), inc="act")
                    eplist.append(a1)
                    sw = [("st", st_done[pkey])] if pkey in st_done else []
                    p.add("vector", lambda v, sl=sl: v.tensor_tensor(
                        ob[:, sl * 512:(sl + 1) * 512], t1e[:, sl * 512:(sl + 1) * 512],
                        xr[:, sl * 512:(sl + 1) * 512], ALU.add),
                        waits=[("act", a1), ("al", alx)] + sw)
                    res_done[(tc, hc)] = p.add("vector", lambda v: v.drain(), inc="dve")
                    st_done[(tc, hc)] = p.dma(
                        "gpsimd", ctx.wr("xs")[hc * 128:(hc + 1) * 128, tc * 512:(tc + 1) * 512],
                        ob[:, sl * 512:(sl + 1) * 512], "st",
                        waits=[("dve", res_done[(tc, hc)])])
            act_guard[tc] = eplist[-1]
            p.play(f"l0f_m{tc}c")
            ctx.barrier()
            p = Prog(ctx)
        if DEBUG_L0F_S == "rows":
            p.dma("gpsimd", ctx.wr("dbg_a")[0:1, 0:T], r_row[:, :], "st",
                  waits=[("act", act_guard[NTC - 1])])
            p.dma("gpsimd", ctx.wr("dbg_a")[1:2, 0:T], negmr_row[:, :], "st")
        p.play("l0f_main")
    ctx.barrier()


@stage(reads=["aq", "w_out1", "mod_cols", "xs"], writes=["xs"])
def emit_l1a_attn(ctx):
    nc = ctx.nc
    H = 16
    NQS = 8
    with contextlib.ExitStack() as stk:
        qf = ctx.sb(stk, "qf1", [64, T], BF16)
        kf = ctx.sb(stk, "kf1", [64, T], BF16)
        vf = ctx.sb(stk, "vf1", [64, T], BF16)
        vtm = ctx.sb(stk, "vtm1", [128, 32 * 65], BF16)
        sqq = ctx.sb(stk, "sqq1", [64, T], BF16)
        nq2 = ctx.sb(stk, "nq2r", [1, T], F32)
        nk2 = ctx.sb(stk, "nk2r", [1, T], F32)
        m_row = ctx.sb(stk, "m_row1", [1, T], F32)
        k2m = ctx.sb(stk, "k2m", [1, 8], F32)
        msr = ctx.sb(stk, "msr", [1, 8], F32)
        negm = ctx.sb(stk, "negm", [128, 8], F32)
        pt = [ctx.sb(stk, f"pt1{i}", [128, 1024], BF16) for i in range(2)]
        rbuf = ctx.sb(stk, "rbuf1", [1, 512], F32)
        craw = ctx.sb(stk, "craw1", [64, 2 * 512], BF16)
        ctx_sb = ctx.sb(stk, "ctx1", [128, NCC * T], BF16)
        wout = ctx.sb(stk, "wout", [128, NCC * 1024], BF16)
        ga_col = ctx.sb(stk, "ga_col1", [128, 8], F32)
        xr = ctx.sb(stk, "xr1", [128, 2 * 512], BF16)
        t1 = ctx.sb(stk, "t1x", [128, 2 * 512], F32)
        ob = ctx.sb(stk, "ob1", [128, 2 * 512], BF16)
        PSS = [ctx.ps(stk, f"PSS{i}", [128, 1024], F32) for i in range(2)]
        PSO = [ctx.ps(stk, f"PSO{i}", [128, 512], F32) for i in range(2)]
        PSB = ctx.ps(stk, "PSB", [128, 512], F32)
        trb = [PSS[0][:, 0:64].bitcast(BF16), PSS[0][:, 512:576].bitcast(BF16),
               PSS[1][:, 0:64].bitcast(BF16)]

        p = Prog(ctx)
        p.add("vector", lambda v: v.memset(vtm[:, :], 1.0))
        vtm_init = p.add("vector", lambda v: v.drain(), inc="dve")
        aq = ctx.rd("aq")
        head_done = {}
        ev_prev = {}
        exp_done = {}
        for h in range(H):
            lw = [("pe", head_done[h - 1]), ("dve", ev_prev[(h - 1, NQS - 1)])] if h > 0 else []
            al_q = p.dma("sync", qf[:, :], aq[h * 64:(h + 1) * 64, :], "al", waits=lw)
            al_k = p.dma("sync", kf[:, :], aq[C + h * 64: C + (h + 1) * 64, :], "al")
            al_v = p.dma("sync", vf[:, :], aq[2 * C + h * 64: 2 * C + (h + 1) * 64, :], "al")
            tr_ev = {}
            for i in range(32):
                w = [("al", al_v), ("dve", vtm_init)]
                if i >= 3:
                    w.append(("dve", tr_ev[i - 3]))
                if i == 0 and h > 0:
                    w.append(("dve", ev_prev[(h - 1, NQS - 1)]))
                b = trb[i % 3][:, 0:64]
                p.add("tensor", lambda t, i=i, b=b: t.transpose(
                    b, vf[:, i * 128:(i + 1) * 128], ctx.ident[0:64, 0:64]),
                    waits=w, inc="pe")
                pev = ctx.cnt["pe"]
                tr_ev[i] = p.add("vector", lambda v, i=i, b=b: v.tensor_copy(
                    vtm[:, i * 65:i * 65 + 64], b), waits=[("pe", pev)], inc="dve")
            p.add("scalar", lambda s: s.activation(sqq[:, :], qf[:, :], AF.Square),
                  waits=[("al", al_q)])
            a_q = p.add("scalar", lambda s: s.drain(), inc="act")
            nev = {}
            for tc in range(NTC):
                w = [("act", a_q)]
                if tc > 0:
                    w.append(("dve", nev[tc - 1]))
                p.add("tensor", lambda t, tc=tc: t.matmul(
                    PSB[0:1, :], ctx.ones_col[0:64, :], sqq[:, tc * 512:(tc + 1) * 512],
                    start=True, stop=True), waits=w, inc="pe")
                pev = ctx.cnt["pe"]
                nev[tc] = p.add("vector", lambda v, tc=tc: v.tensor_copy(
                    nq2[:, tc * 512:(tc + 1) * 512], PSB[0:1, :]),
                    waits=[("pe", pev)], inc="dve")
            p.add("scalar", lambda s: s.activation(sqq[:, :], kf[:, :], AF.Square),
                  waits=[("al", al_k), ("pe", ctx.cnt["pe"])])
            a_k = p.add("scalar", lambda s: s.drain(), inc="act")
            for tc in range(NTC):
                w = [("act", a_k), ("dve", nev[tc - 1] if tc else nev[NTC - 1])]
                p.add("tensor", lambda t, tc=tc: t.matmul(
                    PSB[0:1, :], ctx.ones_col[0:64, :], sqq[:, tc * 512:(tc + 1) * 512],
                    start=True, stop=True), waits=w, inc="pe")
                pev = ctx.cnt["pe"]
                nev[tc] = p.add("vector", lambda v, tc=tc: v.tensor_copy(
                    nk2[:, tc * 512:(tc + 1) * 512], PSB[0:1, :]),
                    waits=[("pe", pev)], inc="dve")
            p.add("vector", lambda v: v.reduce_max(k2m[:, 0:1], nk2[:, :], AX.X))
            dk = p.add("vector", lambda v: v.drain(), inc="dve")
            p.add("scalar", lambda s: s.activation(k2m[:, 1:2], k2m[:, 0:1], AF.Ln,
                                                   bias=ctx.cb(1e-20)[0:1, :]),
                  waits=[("dve", dk)])
            lk = p.add("scalar", lambda s: s.drain(), inc="act")
            p.add("scalar", lambda s: s.activation(m_row[:, :], nq2[:, :], AF.Ln,
                                                   bias=ctx.cb(1e-20)[0:1, :]))
            lq = p.add("scalar", lambda s: s.drain(), inc="act")
            p.add("vector", lambda v: v.tensor_scalar_add(m_row[:, :], m_row[:, :], k2m[:, 1:2]),
                  waits=[("act", lq), ("act", lk)])
            d2 = p.add("vector", lambda v: v.drain(), inc="dve")
            p.add("scalar", lambda s: s.activation(m_row[:, :], m_row[:, :], AF.Exp, scale=0.5),
                  waits=[("dve", d2)])
            me = p.add("scalar", lambda s: s.drain(), inc="act")
            bev = {}
            for qs in range(NQS):
                w = [("act", me)]
                if qs > 0:
                    w.append(("dve", bev[qs - 1]))
                p.add("vector", lambda v, qs=qs: v.reduce_max(
                    msr[:, qs:qs + 1], m_row[:, qs * 512:(qs + 1) * 512], AX.X), waits=w)
                dm = p.add("vector", lambda v: v.drain(), inc="dve")
                p.add("tensor", lambda t, qs=qs: t.matmul(
                    PSB[:, 0:1], ctx.negones_row_f[:, :], msr[:, qs:qs + 1],
                    start=True, stop=True), waits=[("dve", dm)], inc="pe")
                pev = ctx.cnt["pe"]
                bev[qs] = p.add("vector", lambda v, qs=qs: v.tensor_copy(
                    negm[:, qs:qs + 1], PSB[:, 0:1]), waits=[("pe", pev)], inc="dve")
            prep = p.add("vector", lambda v: v.drain(), inc="dve")
            tr_all = p.add("vector", lambda v: v.drain(), inc="dve")
            def emit_evac(qs, so, pv_done, h=h):
                w = [("pe", pv_done)]
                if (h, qs - 1) in ev_prev:
                    w.append(("dve", ev_prev[(h, qs - 1)]))
                p.add("vector", lambda v, so=so: v.reciprocal(rbuf[:, :], PSO[so][64:65, :]),
                      waits=w)
                dr = p.add("vector", lambda v: v.drain(), inc="dve")
                p.add("tensor", lambda t: t.matmul(
                    PSB[0:64, :], ctx.ones_row_f[:, 0:64], rbuf[:, :],
                    start=True, stop=True), waits=[("dve", dr)], inc="pe")
                pbv = ctx.cnt["pe"]
                p.add("scalar", lambda s, so=so: s.activation(
                    craw[:, so * 512:(so + 1) * 512], PSO[so][0:64, :], AF.Copy),
                    waits=[("pe", pbv)])
                a1d = p.add("scalar", lambda s: s.drain(), inc="act")
                dst = ctx_sb[(h % 2) * 64:(h % 2) * 64 + 64,
                             (h // 2) * T + qs * 512:(h // 2) * T + (qs + 1) * 512]
                p.add("vector", lambda v, so=so, dst=dst: v.tensor_tensor(
                    dst, craw[:, so * 512:(so + 1) * 512], PSB[0:64, :], ALU.mult),
                    waits=[("act", a1d)])
                ev_prev[(h, qs)] = p.add("vector", lambda v: v.drain(), inc="dve")

            pend = None
            for qs in range(NQS):
                so = qs % 2

                def emit_S(pair, qs=qs, h=h):
                    pb = pair % 2
                    for j in range(2):
                        kt = pair * 2 + j
                        w = [("dve", tr_all), ("dve", prep)]
                        key = (h, qs, pair - 2) if pair >= 2 else \
                              ((h, qs - 1, 14 + pair) if qs > 0 else
                               ((h - 1, NQS - 1, 14 + pair) if h > 0 else None))
                        if key in exp_done:
                            w.append(("act", exp_done[key]))
                        p.add("tensor", lambda t, kt=kt, qs=qs, pb=pb, j=j: t.matmul(
                            PSS[pb][:, j * 512:(j + 1) * 512],
                            kf[:, kt * 128:(kt + 1) * 128], qf[:, qs * 512:(qs + 1) * 512],
                            start=True, stop=True),
                            waits=w, inc="pe" if j == 1 else None)
                    return ctx.cnt["pe"]

                pevs = {0: emit_S(0)}
                for pair in range(16):
                    pb = pair % 2
                    w = [("pe", pevs[pair])]
                    if pair == 0:
                        okey = (h, qs - 2) if qs >= 2 else ((h - 1, NQS - 2 + qs) if h > 0 else None)
                        if okey in ev_prev:
                            w.append(("dve", ev_prev[okey]))
                    p.add("scalar", lambda s, pb=pb, qs=qs: s.activation(
                        pt[pb][:, :], PSS[pb][:, :], AF.Exp, bias=negm[:, qs:qs + 1]),
                        waits=w)
                    exp_done[(h, qs, pair)] = p.add("scalar", lambda s: s.drain(), inc="act")
                    if pair < 15:
                        pevs[pair + 1] = emit_S(pair + 1)
                    if pair == 2 and pend is not None:
                        emit_evac(*pend)
                        pend = None
                    for j in range(2):
                        kt = pair * 2 + j
                        w = [("act", exp_done[(h, qs, pair)])]
                        p.add("tensor", lambda t, kt=kt, so=so, pb=pb, j=j: t.matmul(
                            PSO[so][0:65, :], vtm[:, kt * 65:(kt + 1) * 65],
                            pt[pb][:, j * 512:(j + 1) * 512],
                            start=(kt == 0), stop=(kt == 31)),
                            waits=w, inc="pe" if kt == 31 else None)
                pend = (qs, so, ctx.cnt["pe"])
            emit_evac(*pend)
            pend = None
            head_done[h] = ctx.cnt["pe"]
        p.add("vector", lambda v: v.drain())
        ctx_all = p.add("vector", lambda v: v.engine_nop(), inc="dve")

        for cc in range(NCC):
            p.dma("gpsimd", wout[:, cc * 1024:(cc + 1) * 1024],
                  ctx.rd("w_out1")[cc * 128:(cc + 1) * 128, :], "wl")
        wl_all = ctx.cnt["wl"]
        al_ga = p.dma("sync", ga_col[:, :], ctx.rd("mod_cols")[1, 0], "al")
        ep = {}
        stv = {}
        for tc in range(NTC):
            for ec in range(NCC):
                for cc in range(NCC):
                    waits = [("dve", ctx_all)]
                    if cc == 0:
                        waits.append(("wl", wl_all))
                        key = (tc, ec - 2) if ec >= 2 else (tc - 1, NCC - 2 + ec)
                        if key in ep:
                            waits.append(("dve", ep[key]))
                    p.add("tensor", lambda t, ec=ec, cc=cc, tc=tc: t.matmul(
                        PSO[ec % 2][:, :], wout[:, cc * 1024 + ec * 128: cc * 1024 + (ec + 1) * 128],
                        ctx_sb[:, cc * T + tc * 512: cc * T + (tc + 1) * 512],
                        start=(cc == 0), stop=(cc == NCC - 1)),
                        waits=waits, inc="pe" if cc == NCC - 1 else None)
                pev = ctx.cnt["pe"]
                slot = ec % 2
                pkey = (tc, ec - 2) if ec >= 2 else (tc - 1, NCC - 2 + ec)
                alx = p.dma("sync", xr[:, slot * 512:(slot + 1) * 512],
                            ctx.rd("xs")[ec * 128:(ec + 1) * 128, tc * 512:(tc + 1) * 512],
                            "al", waits=[("dve", ep[pkey])] if pkey in ep else [])
                aw = [("pe", pev), ("al", al_ga)]
                if pkey in ep:
                    aw.append(("dve", ep[pkey]))
                a1 = p.add("scalar", lambda s, ec=ec, slot=slot: s.activation(
                    t1[:, slot * 512:(slot + 1) * 512], PSO[ec % 2][:, :], AF.Identity,
                    scale=ga_col[:, ec:ec + 1]), waits=aw)
                a1d = p.add("scalar", lambda s: s.drain(), inc="act")
                sw = [("st", stv[pkey])] if pkey in stv else []
                p.add("vector", lambda v, slot=slot: v.tensor_tensor(
                    ob[:, slot * 512:(slot + 1) * 512], t1[:, slot * 512:(slot + 1) * 512],
                    xr[:, slot * 512:(slot + 1) * 512], ALU.add),
                    waits=[("act", a1d), ("al", alx)] + sw)
                ep[(tc, ec)] = p.add("vector", lambda v: v.drain(), inc="dve")
                stv[(tc, ec)] = p.dma(
                    "gpsimd", ctx.wr("xs")[ec * 128:(ec + 1) * 128, tc * 512:(tc + 1) * 512],
                    ob[:, slot * 512:(slot + 1) * 512], "st",
                    waits=[("dve", ep[(tc, ec)])])
        p.play("l1a_attn")
    ctx.barrier()



@stage(reads=["xs", "mod_rows"], writes=["l1f_h"])
def emit_l1f_h(ctx):
    nc = ctx.nc
    with contextlib.ExitStack() as stk:
        xt = ctx.sb(stk, "xtfh", [128, 2 * T], BF16)
        hb = ctx.sb(stk, "hbfh", [128, 2 * 512], BF16)
        sq = ctx.sb(stk, "sqfh", [128, 1024], BF16)
        srow = ctx.sb(stk, "srowfh", [1, 2048], F32)
        r_row = ctx.sb(stk, "r_rowfh", [1, T], BF16)
        negmr_row = ctx.sb(stk, "negmrfh", [1, T], BF16)
        sb_row = ctx.sb(stk, "sb_rowfh", [1, 1024], BF16)
        cb_row = ctx.sb(stk, "cb_rowfh", [1, 1024], BF16)
        t1 = ctx.sb(stk, "t1fh", [128, 512], F32)
        ps_sum = ctx.ps(stk, "ps_sumfh", [128, 512], F32)
        ps_sq = ctx.ps(stk, "ps_sqfh", [128, 512], F32)
        ps_sb = [ctx.ps(stk, f"ps_sbfh{i}", [128, 1024], F32) for i in range(2)]
        p = Prog(ctx)
        mr = ctx.rd("mod_rows")
        al_sb = p.dma("gpsimd", sb_row[:, :], mr[1][None, 3 * 1024:4 * 1024], "al")
        al_cb = p.dma("gpsimd", cb_row[:, :], mr[1][None, 4 * 1024:5 * 1024], "al")
        sdve = None
        h_done = {}
        st_done = {}
        for tc in range(NTC):
            xbase = (tc % 2) * T
            xwaits = []
            for cc in range(NCC):
                w = []
                if tc >= 2:
                    w.append(("dve", h_done[(tc - 2, NCC - 1)]))
                v = p.dma("sync", xt[:, xbase + cc * 512: xbase + (cc + 1) * 512],
                          ctx.rd("xs")[cc * 128:(cc + 1) * 128, tc * 512:(tc + 1) * 512],
                          "al", waits=w)
                xwaits.append(("al", v))
            xv = [xt[:, xbase + cc * 512: xbase + (cc + 1) * 512] for cc in range(NCC)]
            sdve = _emit_stats_tc(p, ctx, tc, xv, xwaits, sq, ps_sum, ps_sq,
                                  r_row, negmr_row, srow, 1e-5,
                                  prev_sdve=(sdve if tc > 0 else None))
            for cc in range(NCC):
                sl = cc % 2
                ew = [("al", al_sb), ("al", al_cb)]
                key = (tc, cc - 2) if cc >= 2 else (tc - 1, NCC - 2 + cc)
                if key in st_done:
                    ew.append(("st", st_done[key]))
                if key in h_done:
                    ew.append(("dve", h_done[key]))
                h_done[(tc, cc)] = _emit_apply_tc(
                    p, ctx, tc, cc, xv[cc], sdve, sb_row[:, :], cb_row[:, :],
                    r_row[:, :], negmr_row[:, :], ps_sb[cc % 2],
                    hb[:, sl * 512:(sl + 1) * 512], t1[:, :], extra_waits=ew)
                st_done[(tc, cc)] = p.dma(
                    "gpsimd", ctx.wr("l1f_h")[cc * 128:(cc + 1) * 128, tc * 512:(tc + 1) * 512],
                    hb[:, sl * 512:(sl + 1) * 512], "st", waits=[("dve", h_done[(tc, cc)])])
        p.play("l1f_h")
    ctx.barrier()


def _proj_tc(p, ctx, w_sb, in_tiles, in_ready, psq, out_cb, nec=NCC, wl_all=None,
             carry=None):
    eps = {}
    for ec in range(nec):
        for cc in range(NCC):
            w = list(in_ready)
            if cc == 0:
                if wl_all:
                    w.append(("wl", wl_all))
                if ec >= 2:
                    w.append(eps[ec - 2])
                elif carry:
                    k = max(carry)
                    w.append(carry[k - 1 + ec])
            p.add("tensor", lambda t, ec=ec, cc=cc: t.matmul(
                psq[ec % 2][:, :], w_sb(ec, cc), in_tiles[cc],
                start=(cc == 0), stop=(cc == NCC - 1)),
                waits=w, inc="pe" if cc == NCC - 1 else None)
        eps[ec] = out_cb(ec, ctx.cnt["pe"])
    return eps


def make_l1f_nP(c):
    eq, ek, ev, in_q, in_k, in_v = DAG[c]
    reads = ["l1f_h", "we_in", "wq_in", "wk_in", "wv_in", "be_in", "ew_in",
             NODE_NAME[in_q], NODE_NAME[in_k], NODE_NAME[in_v]]
    @stage(reads=sorted(set(reads)), writes=["l1f_q", "l1f_qh", "l1f_kh", "l1f_vh"])
    def fn(ctx):
        _l1f_nP_impl(ctx, c)
    fn.__name__ = f"l1f_n{c}P"
    return fn


def _l1f_nP_impl(ctx, c):
    nc = ctx.nc
    eq, ek, ev, in_q, in_k, in_v = DAG[c]
    srcs = [NODE_NAME[in_q], NODE_NAME[in_k], NODE_NAME[in_v]]
    with contextlib.ExitStack() as stk:
        we = [ctx.sb(stk, f"weP{j}", [128, NCC * 1024], BF16) for j in range(3)]
        wh = [ctx.sb(stk, f"whP{j}", [128, NCC * 1024], BF16) for j in range(3)]
        at = ctx.sb(stk, "atP", [128, 3 * 4096], BF16)
        eb = ctx.sb(stk, "ebP", [128, 3 * 4096], BF16)
        ob = ctx.sb(stk, "obP", [128, 2 * 512], BF16)
        berow = ctx.sb(stk, "berow", [1, 3 * 1024], F32)
        ewrow = ctx.sb(stk, "ewrow", [1, 18], F32)
        becol = ctx.sb(stk, "becol", [128, 24], F32)
        ewcol = ctx.sb(stk, "ewcol", [128, 3], F32)
        PSQ = [ctx.ps(stk, f"PSQP{i}", [128, 512], F32) for i in range(2)]
        PST = [ctx.ps(stk, f"PSTP{i}", [128, 512], F32) for i in range(4)]

        p = Prog(ctx)
        widx = [("we_in", eq), ("we_in", ek), ("we_in", ev)]
        hidx = [("wq_in", c), ("wk_in", c), ("wv_in", c)]
        for j in range(3):
            for cc in range(NCC):
                p.dma("gpsimd", we[j][:, cc * 1024:(cc + 1) * 1024],
                      ctx.rd(widx[j][0])[widx[j][1], cc * 128:(cc + 1) * 128, :], "wl")
                p.dma("gpsimd", wh[j][:, cc * 1024:(cc + 1) * 1024],
                      ctx.rd(hidx[j][0])[hidx[j][1], cc * 128:(cc + 1) * 128, :], "wl")
        wl_all = ctx.cnt["wl"]
        al_be = [p.dma("sync", berow[:, j * 1024:(j + 1) * 1024],
                       ctx.rd("be_in")[None, (eq, ek, ev)[j], :], "al") for j in range(3)]
        al_ew = p.dma("sync", ewrow[:, :], ctx.rd("ew_in")[None, :], "al")
        p.ops["tensor"].append((lambda t: t.nop(), [("al", al_be[-1]), ("al", al_ew)], []))
        specs = []
        for j in range(3):
            for a in range(8):
                specs.append((berow[:, j * 1024 + a * 128: j * 1024 + (a + 1) * 128],
                              becol[:, j * 8 + a: j * 8 + a + 1]))
        rows_to_cols(p, specs, PST)
        rcv = ctx.cnt["dve"]
        for j in range(3):
            scalar_bcast_col(p, ewrow[:, 3 * c + j: 3 * c + j + 1], ewcol[:, j:j + 1],
                             PST[j], waits=[("dve", rcv)])
        ewv = ctx.cnt["dve"]
        for j in range(3):
            p.add("vector", lambda v, j=j: v.tensor_tensor(
                becol[:, j * 8:(j + 1) * 8], becol[:, j * 8:(j + 1) * 8],
                ewcol[:, j:j + 1].broadcast_to([128, 8]), ALU.mult),
                waits=[("dve", ewv)] if j == 0 else [])
        colv = p.add("vector", lambda v: v.drain(), inc="dve")
        if DEBUG_NP:
            p.dma("sync", ctx.wr("dbg_a")[0:128, 0:24], becol[:, :], "st", waits=[("dve", colv)])
            p.dma("sync", ctx.wr("dbg_a")[0:128, 24:27], ewcol[:, :], "st", waits=[("dve", colv)])
            p.dma("sync", ctx.wr("dbg_a")[0:1, 32:32 + 3072], berow[:, :], "st", waits=[("dve", colv)])

        st_e = {}
        st_h = {}
        prev_pe = {}
        carry = None
        for tc in range(NTC):
            loaded = {}
            tload = {}
            reg = {}
            for j, s in enumerate(srcs):
                if s in loaded:
                    tload[j] = tload[loaded[s]]
                    reg[j] = reg[loaded[s]]
                    continue
                loaded[s] = j
                reg[j] = j
                w = []
                if tc >= 1:
                    w.append(("pe", prev_pe[tc - 1]))
                vv = [p.dma("sync", at[:, j * 4096 + cc * 512: j * 4096 + (cc + 1) * 512],
                            ctx.rd(s)[cc * 128:(cc + 1) * 128, tc * 512:(tc + 1) * 512],
                            "al", waits=w if cc == 0 else []) for cc in range(NCC)]
                tload[j] = vv[-1]
            for j, sname in enumerate(("l1f_qh", "l1f_kh", "l1f_vh")):
                rj = reg[j]
                in_tiles = [at[:, rj * 4096 + cc * 512: rj * 4096 + (cc + 1) * 512]
                            for cc in range(NCC)]

                def edge_cb(ec, pev, j=j, tc=tc):
                    w = [("pe", pev), ("dve", colv)]
                    ek_ = (tc - 1, j, ec)
                    if st_e.get(ek_) is not None:
                        w.append(("st", st_e[ek_]))
                    p.add("scalar", lambda s, ec=ec, j=j: s.activation(
                        eb[:, j * 4096 + ec * 512: j * 4096 + (ec + 1) * 512],
                        PSQ[ec % 2][:, :], AF.Identity,
                        scale=ewcol[:, j:j + 1], bias=becol[:, j * 8 + ec: j * 8 + ec + 1]),
                        waits=w)
                    av = p.add("scalar", lambda s: s.drain(), inc="act")
                    if j == 0:
                        st_e[(tc, j, ec)] = p.dma(
                            "gpsimd", ctx.wr("l1f_q")[ec * 128:(ec + 1) * 128, tc * 512:(tc + 1) * 512],
                            eb[:, j * 4096 + ec * 512: j * 4096 + (ec + 1) * 512],
                            "st", waits=[("act", av)])
                    else:
                        st_e[(tc, j, ec)] = None
                    return ("act", av)

                eps1 = _proj_tc(p, ctx,
                                lambda ec, cc, j=j: we[j][:, cc * 1024 + ec * 128: cc * 1024 + (ec + 1) * 128],
                                in_tiles, [("al", tload[j]), ("dve", colv)], PSQ, edge_cb,
                                wl_all=wl_all, carry=carry)
                edge_all = eps1[NCC - 1]
                in2 = [eb[:, j * 4096 + cc * 512: j * 4096 + (cc + 1) * 512]
                       for cc in range(NCC)]
                scale2 = RSQ128 if j == 1 else 1.0

                def head_cb(ec, pev, j=j, sname=sname, tc=tc, scale2=scale2):
                    sl = ec % 2
                    w = [("pe", pev)]
                    hk = (tc, j, ec - 2) if ec >= 2 else \
                         ((tc, j - 1, NCC - 2 + ec) if j > 0 else (tc - 1, 2, NCC - 2 + ec))
                    if hk in st_h:
                        w.append(("st", st_h[hk]))
                    p.add("scalar", lambda s, ec=ec, sl=sl, scale2=scale2: s.activation(
                        ob[:, sl * 512:(sl + 1) * 512], PSQ[ec % 2][:, :], AF.Copy,
                        scale=float(scale2)), waits=w)
                    av = p.add("scalar", lambda s: s.drain(), inc="act")
                    st_h[(tc, j, ec)] = p.dma(
                        "gpsimd", ctx.wr(sname)[ec * 128:(ec + 1) * 128, tc * 512:(tc + 1) * 512],
                        ob[:, sl * 512:(sl + 1) * 512], "st", waits=[("act", av)])
                    return ("act", av)

                carry = _proj_tc(p, ctx,
                                 lambda ec, cc, j=j: wh[j][:, cc * 1024 + ec * 128: cc * 1024 + (ec + 1) * 128],
                                 in2, [edge_all], PSQ, head_cb, wl_all=wl_all, carry=eps1)
            prev_pe[tc] = ctx.cnt["pe"]
        p.play(f"l1f_n{c}P")
    ctx.barrier()


def make_l1f_nA(c):
    @stage(reads=["l1f_qh", "l1f_kh", "l1f_vh"], writes=["l1f_ctx"])
    def fn(ctx):
        _l1f_nA_impl(ctx, c)
    fn.__name__ = f"l1f_n{c}A"
    return fn


def _l1f_nA_impl(ctx, c):
    nc = ctx.nc
    H = 8
    NQS = 8
    with contextlib.ExitStack() as stk:
        kh = ctx.sb(stk, "khA", [128, NCC * T], BF16)
        vhtm = ctx.sb(stk, "vhtmA", [128, NCC * T], BF16)
        vf = ctx.sb(stk, "vfA", [128, T], BF16)
        qh = ctx.sb(stk, "qhA", [128, T], BF16)
        sqq = ctx.sb(stk, "sqqA", [128, T], BF16)
        nq2 = ctx.sb(stk, "nq2A", [1, T], BF16)
        nk2 = ctx.sb(stk, "nk2A", [1, T], BF16)
        m_row = ctx.sb(stk, "m_rowA", [1, T], BF16)
        k2m = ctx.sb(stk, "k2mA", [1, 8], F32)
        msr = ctx.sb(stk, "msrA", [1, 8], F32)
        negm = ctx.sb(stk, "negmA", [128, 8], F32)
        pt = [ctx.sb(stk, f"ptA{i}", [128, 1024], BF16) for i in range(2)]
        rbuf = ctx.sb(stk, "rbufA", [1, 512], F32)
        craw = ctx.sb(stk, "crawA", [128, 2 * 512], BF16)
        cob = ctx.sb(stk, "cobA", [128, 2 * 512], BF16)
        PSS = [ctx.ps(stk, f"PSSA{i}", [128, 1024], F32) for i in range(2)]
        PSO = [ctx.ps(stk, f"PSOA{i}", [128, 512], F32) for i in range(2)]
        PSB = ctx.ps(stk, "PSBA", [128, 512], F32)
        PSN = ctx.ps(stk, "PSNA", [128, 512], F32)
        naccb = [ctx.sb(stk, f"naccA{i}", [128, 512], F32) for i in range(2)]
        trb = [PSS[0][:, 0:64].bitcast(BF16), PSS[0][:, 512:576].bitcast(BF16),
               PSS[1][:, 0:64].bitcast(BF16)]

        p = Prog(ctx)
        for cc in range(NCC):
            p.dma("gpsimd", kh[:, cc * T:(cc + 1) * T],
                  ctx.rd("l1f_kh")[cc * 128:(cc + 1) * 128, :], "wl")
        tr_ev = {}
        gi = 0
        trpe = {}
        for h in range(H):
            alv = p.dma("sync", vf[:, :], ctx.rd("l1f_vh")[h * 128:(h + 1) * 128, :], "al",
                        waits=[("pe", trpe[h - 1])] if h > 0 else [])
            for i in range(32):
                w = [("al", alv)]
                if gi >= 3:
                    w.append(("dve", tr_ev[gi - 3]))
                b = trb[gi % 3]
                p.add("tensor", lambda t, i=i, b=b: t.transpose(
                    b[:, 0:128], vf[:, i * 128:(i + 1) * 128], ctx.ident[:]),
                    waits=w, inc="pe")
                pev = ctx.cnt["pe"]
                tr_ev[gi] = p.add("vector", lambda v, h=h, i=i, b=b: v.tensor_copy(
                    vhtm[:, h * T + i * 128: h * T + (i + 1) * 128], b[:, 0:128]),
                    waits=[("pe", pev)], inc="dve")
                gi += 1
            trpe[h] = ctx.cnt["pe"]
        p.add("vector", lambda v: v.drain())
        tr_all = p.add("vector", lambda v: v.engine_nop(), inc="dve")
        wl_kh = ctx.cnt["wl"]

        head_done = {}
        ev_prev = {}
        exp_done = {}
        st_done = {}
        for h in range(H):
            lw = [("pe", head_done[h - 1]), ("dve", ev_prev[(h - 1, NQS - 1)])] if h > 0 \
                 else [("dve", tr_all)]
            al_q = p.dma("sync", qh[:, :], ctx.rd("l1f_qh")[h * 128:(h + 1) * 128, :],
                         "al", waits=lw)
            p.add("scalar", lambda s: s.activation(sqq[:, :], qh[:, :], AF.Square),
                  waits=[("al", al_q)])
            a_q = p.add("scalar", lambda s: s.drain(), inc="act")
            nev = {}
            for tc in range(NTC):
                w = [("act", a_q), ("dve", tr_all)]
                if tc > 0:
                    w.append(("dve", nev[tc - 1]))
                p.add("tensor", lambda t, tc=tc: t.matmul(
                    PSB[0:1, :], ctx.ones_col[:, :], sqq[:, tc * 512:(tc + 1) * 512],
                    start=True, stop=True), waits=w, inc="pe")
                pev = ctx.cnt["pe"]
                nev[tc] = p.add("vector", lambda v, tc=tc: v.tensor_copy(
                    nq2[:, tc * 512:(tc + 1) * 512], PSB[0:1, :]),
                    waits=[("pe", pev)], inc="dve")
            p.add("scalar", lambda s, h=h: s.activation(
                sqq[:, :], kh[:, h * T:(h + 1) * T], AF.Square),
                waits=[("wl", wl_kh), ("pe", ctx.cnt["pe"])])
            a_k = p.add("scalar", lambda s: s.drain(), inc="act")
            for tc in range(NTC):
                w = [("act", a_k), ("dve", nev[tc - 1] if tc else nev[NTC - 1])]
                p.add("tensor", lambda t, tc=tc: t.matmul(
                    PSB[0:1, :], ctx.ones_col[:, :], sqq[:, tc * 512:(tc + 1) * 512],
                    start=True, stop=True), waits=w, inc="pe")
                pev = ctx.cnt["pe"]
                nev[tc] = p.add("vector", lambda v, tc=tc: v.tensor_copy(
                    nk2[:, tc * 512:(tc + 1) * 512], PSB[0:1, :]),
                    waits=[("pe", pev)], inc="dve")
            p.add("vector", lambda v: v.reduce_max(k2m[:, 0:1], nk2[:, :], AX.X))
            dk = p.add("vector", lambda v: v.drain(), inc="dve")
            lk = p.add("scalar", lambda s: s.activation(k2m[:, 1:2], k2m[:, 0:1], AF.Ln,
                                                        bias=ctx.cb(1e-20)[0:1, :]),
                       waits=[("dve", dk)])
            lkd = p.add("scalar", lambda s: s.drain(), inc="act")
            lq = p.add("scalar", lambda s: s.activation(m_row[:, :], nq2[:, :], AF.Ln,
                                                        bias=ctx.cb(1e-20)[0:1, :]))
            lqd = p.add("scalar", lambda s: s.drain(), inc="act")
            p.add("vector", lambda v: v.tensor_scalar_add(m_row[:, :], m_row[:, :], k2m[:, 1:2]),
                  waits=[("act", lqd), ("act", lkd)])
            d2 = p.add("vector", lambda v: v.drain(), inc="dve")
            p.add("scalar", lambda s: s.activation(m_row[:, :], m_row[:, :], AF.Exp, scale=0.5),
                  waits=[("dve", d2)])
            me = p.add("scalar", lambda s: s.drain(), inc="act")
            bev = {}
            for qs in range(NQS):
                w = [("act", me)]
                if qs > 0:
                    w.append(("dve", bev[qs - 1]))
                p.add("vector", lambda v, qs=qs: v.reduce_max(
                    msr[:, qs:qs + 1], m_row[:, qs * 512:(qs + 1) * 512], AX.X), waits=w)
                dm = p.add("vector", lambda v: v.drain(), inc="dve")
                p.add("tensor", lambda t, qs=qs: t.matmul(
                    PSB[:, 0:1], ctx.negones_row_f[:, :], msr[:, qs:qs + 1],
                    start=True, stop=True), waits=[("dve", dm)], inc="pe")
                pev = ctx.cnt["pe"]
                bev[qs] = p.add("vector", lambda v, qs=qs: v.tensor_copy(
                    negm[:, qs:qs + 1], PSB[:, 0:1]), waits=[("pe", pev)], inc="dve")
            prep = p.add("vector", lambda v: v.drain(), inc="dve")

            def emit_evacA(qs, so, nsum_pe, h=h):
                w = [("pe", nsum_pe)]
                if (h, qs - 1) in ev_prev:
                    w.append(("dve", ev_prev[(h, qs - 1)]))
                p.add("vector", lambda v: v.reciprocal(rbuf[:, :], PSN[0:1, :]), waits=w)
                dr = p.add("vector", lambda v: v.drain(), inc="dve")
                p.add("tensor", lambda t: t.matmul(
                    PSB[:, :], ctx.ones_row_f[:, 0:128], rbuf[:, :],
                    start=True, stop=True), waits=[("dve", dr)], inc="pe")
                pbv = ctx.cnt["pe"]
                p.add("scalar", lambda s, so=so: s.activation(
                    craw[:, so * 512:(so + 1) * 512], PSO[so][:, :], AF.Copy),
                    waits=[("pe", pbv)])
                a1d = p.add("scalar", lambda s: s.drain(), inc="act")
                skey = (h, qs - 2) if qs >= 2 else (h - 1, NQS - 2 + qs)
                sw = [("st", st_done[skey])] if skey in st_done else []
                p.add("vector", lambda v, so=so: v.tensor_tensor(
                    cob[:, so * 512:(so + 1) * 512], craw[:, so * 512:(so + 1) * 512],
                    PSB[:, :], ALU.mult), waits=[("act", a1d)] + sw)
                ev_prev[(h, qs)] = p.add("vector", lambda v: v.drain(), inc="dve")
                st_done[(h, qs)] = p.dma(
                    "gpsimd", ctx.wr("l1f_ctx")[h * 128:(h + 1) * 128, qs * 512:(qs + 1) * 512],
                    cob[:, so * 512:(so + 1) * 512], "st",
                    waits=[("dve", ev_prev[(h, qs)])])

            def emit_fold(fq, fdve):
                p.add("tensor", lambda t, fq=fq: t.matmul(
                    PSN[0:1, :], ctx.ones_col_f[:, :], naccb[fq % 2][:, :],
                    start=True, stop=True), waits=[("dve", fdve)], inc="pe")
                fold_pe[fq] = ctx.cnt["pe"]
                return (fq, fq % 2, fold_pe[fq])

            pend = None
            pend_fold = None
            fold_pe = {}
            for qs in range(NQS):
                so = qs % 2
                na = naccb[so]

                def emit_S(pair, qs=qs, h=h):
                    pb = pair % 2
                    for j in range(2):
                        kt = pair * 2 + j
                        w = [("dve", prep)]
                        key = (h, qs, pair - 2) if pair >= 2 else \
                              ((h, qs - 1, 14 + pair) if qs > 0 else
                               ((h - 1, NQS - 1, 14 + pair) if h > 0 else None))
                        if key in exp_done:
                            w.append(("act", exp_done[key]))
                        p.add("tensor", lambda t, kt=kt, qs=qs, pb=pb, j=j, h=h: t.matmul(
                            PSS[pb][:, j * 512:(j + 1) * 512],
                            kh[:, h * T + kt * 128: h * T + (kt + 1) * 128],
                            qh[:, qs * 512:(qs + 1) * 512],
                            start=True, stop=True),
                            waits=w, inc="pe" if j == 1 else None)
                    return ctx.cnt["pe"]

                pevs = {0: emit_S(0)}
                addv = {}
                for pair in range(16):
                    pb = pair % 2
                    w = [("pe", pevs[pair])]
                    if pair >= 2:
                        w.append(("dve", addv[pair - 2]))
                    if pair == 0:
                        okey = (h, qs - 2) if qs >= 2 else ((h - 1, NQS - 2 + qs) if h > 0 else None)
                        if okey in ev_prev:
                            w.append(("dve", ev_prev[okey]))
                    p.add("scalar", lambda s, pb=pb, qs=qs: s.activation(
                        pt[pb][:, :], PSS[pb][:, :], AF.Exp, bias=negm[:, qs:qs + 1]),
                        waits=w)
                    exp_done[(h, qs, pair)] = p.add("scalar", lambda s: s.drain(), inc="act")
                    if pair < 15:
                        pevs[pair + 1] = emit_S(pair + 1)
                    if pair == 1 and pend_fold is not None:
                        pend = emit_fold(*pend_fold)
                        pend_fold = None
                    if pair == 2 and pend is not None:
                        emit_evacA(*pend)
                        pend = None
                    if pair == 0:
                        nw = [("act", exp_done[(h, qs, pair)])]
                        if qs - 2 in fold_pe:
                            nw.append(("pe", fold_pe[qs - 2]))
                        p.add("vector", lambda v, pb=pb, na=na: v.tensor_copy(
                            na[:, :], pt[pb][:, 0:512]), waits=nw)
                    else:
                        p.add("vector", lambda v, pb=pb, na=na: v.tensor_tensor(
                            na[:, :], na[:, :], pt[pb][:, 0:512], ALU.add),
                            waits=[("act", exp_done[(h, qs, pair)])])
                    addv[pair] = p.add("vector", lambda v, pb=pb, na=na: v.tensor_tensor(
                        na[:, :], na[:, :], pt[pb][:, 512:1024], ALU.add), inc="dve")
                    for j in range(2):
                        kt = pair * 2 + j
                        w = [("act", exp_done[(h, qs, pair)])]
                        p.add("tensor", lambda t, kt=kt, so=so, pb=pb, j=j, h=h: t.matmul(
                            PSO[so][:, :], vhtm[:, h * T + kt * 128: h * T + (kt + 1) * 128],
                            pt[pb][:, j * 512:(j + 1) * 512],
                            start=(kt == 0), stop=(kt == 31)),
                            waits=w, inc="pe" if kt == 31 else None)
                p.add("vector", lambda v: v.drain())
                fold = p.add("vector", lambda v: v.engine_nop(), inc="dve")
                pend_fold = (qs, fold)
            pend = emit_fold(*pend_fold)
            pend_fold = None
            emit_evacA(*pend)
            pend = None
            head_done[h] = ctx.cnt["pe"]
        p.play(f"l1f_n{c}A")
    ctx.barrier()


def make_l1f_nO(c):
    eq, ek, ev, in_q, in_k, in_v = DAG[c]
    @stage(reads=["l1f_ctx", "l1f_q", "wo_in", "actw_in"], writes=[NODE_NAME[c]])
    def fn(ctx):
        _l1f_nO_impl(ctx, c)
    fn.__name__ = f"l1f_n{c}O"
    return fn


def _l1f_nO_impl(ctx, c):
    nc = ctx.nc
    with contextlib.ExitStack() as stk:
        wo = ctx.sb(stk, "woO", [128, NCC * 1024], BF16)
        ct = ctx.sb(stk, "ctO", [128, T], BF16)
        qt = ctx.sb(stk, "qtO", [128, 2 * 512], BF16)
        awrow = ctx.sb(stk, "awrow", [1, 8], F32)
        awcol = ctx.sb(stk, "awcol", [128, 1], F32)
        t1 = ctx.sb(stk, "t1O", [128, 2 * 512], F32)
        ob = ctx.sb(stk, "obO", [128, 2 * 512], BF16)
        PSQ = [ctx.ps(stk, f"PSQO{i}", [128, 512], F32) for i in range(2)]
        PST = ctx.ps(stk, "PSTO", [128, 512], F32)
        p = Prog(ctx)
        for cc in range(NCC):
            p.dma("gpsimd", wo[:, cc * 1024:(cc + 1) * 1024],
                  ctx.rd("wo_in")[c, cc * 128:(cc + 1) * 128, :], "wl")
        wl_all = ctx.cnt["wl"]
        al_aw = p.dma("sync", awrow[:, 0:6], ctx.rd("actw_in")[None, :], "al")
        p.ops["tensor"].append((lambda t: t.nop(), [("al", al_aw)], []))
        scalar_bcast_col(p, awrow[:, c:c + 1], awcol[:, 0:1], PST)
        awv = p.add("vector", lambda v: v.drain(), inc="dve")
        ep = {}
        st_done = {}
        prev_pe = {}
        for tc in range(NTC):
            cload = []
            for cc in range(NCC):
                w = [("pe", prev_pe[tc - 1])] if (tc >= 1 and cc == 0) else []
                cload.append(p.dma("sync", ct[:, cc * 512:(cc + 1) * 512],
                                   ctx.rd("l1f_ctx")[cc * 128:(cc + 1) * 128,
                                                     tc * 512:(tc + 1) * 512],
                                   "al", waits=w))
            in_tiles = [ct[:, cc * 512:(cc + 1) * 512] for cc in range(NCC)]
            def out_cb(ec, pev, tc=tc):
                sl = ec % 2
                pkey = (tc, ec - 2) if ec >= 2 else (tc - 1, NCC - 2 + ec)
                alq = p.dma("sync", qt[:, sl * 512:(sl + 1) * 512],
                            ctx.rd("l1f_q")[ec * 128:(ec + 1) * 128, tc * 512:(tc + 1) * 512],
                            "al", waits=[ep[pkey]] if pkey in ep else [])
                w = [("pe", pev), ("dve", awv), ("al", alq)]
                if pkey in ep:
                    w.append(ep[pkey])
                p.add("scalar", lambda s, ec=ec, sl=sl: s.activation(
                    t1[:, sl * 512:(sl + 1) * 512], PSQ[ec % 2][:, :], AF.Identity,
                    scale=awcol[:, 0:1]), waits=w)
                a1d = p.add("scalar", lambda s: s.drain(), inc="act")
                sw = [("st", st_done[pkey])] if pkey in st_done else []
                p.add("vector", lambda v, sl=sl: v.tensor_tensor(
                    ob[:, sl * 512:(sl + 1) * 512], t1[:, sl * 512:(sl + 1) * 512],
                    qt[:, sl * 512:(sl + 1) * 512], ALU.add),
                    waits=[("act", a1d)] + sw)
                epv = p.add("vector", lambda v: v.drain(), inc="dve")
                st_done[(tc, ec)] = p.dma(
                    "gpsimd", ctx.wr(NODE_NAME[c])[ec * 128:(ec + 1) * 128, tc * 512:(tc + 1) * 512],
                    ob[:, sl * 512:(sl + 1) * 512], "st", waits=[("dve", epv)])
                ep[(tc, ec)] = ("dve", epv)
                return ("dve", epv)
            _proj_tc(p, ctx, lambda ec, cc: wo[:, cc * 1024 + ec * 128: cc * 1024 + (ec + 1) * 128],
                     in_tiles, [("al", cload[-1])], PSQ, out_cb, wl_all=wl_all)
            prev_pe[tc] = ctx.cnt["pe"]
        p.play(f"l1f_n{c}O")
    ctx.barrier()


@stage(reads=["n5", "xs", "lng_in", "lnb_in", "mod_rows"], writes=["out_f", "dbg_a"])
def emit_l1f_fin(ctx):
    nc = ctx.nc
    with contextlib.ExitStack() as stk:
        nt = ctx.sb(stk, "ntF", [128, 2 * T], BF16)
        xt = ctx.sb(stk, "xtF", [128, 2 * 512], BF16)
        sq = ctx.sb(stk, "sqF", [128, 1024], BF16)
        srow = ctx.sb(stk, "srowF", [1, 2048], F32)
        r_row = ctx.sb(stk, "r_rowF", [1, T], F32)
        negmr_row = ctx.sb(stk, "negmrF", [1, T], F32)
        g_row = ctx.sb(stk, "g_rowF", [1, 1024], F32)
        b_row = ctx.sb(stk, "b_rowF", [1, 1024], F32)
        gb_row = ctx.sb(stk, "gb_rowF", [1, 1024], F32)
        gbg = ctx.sb(stk, "gbgF", [1, 1024], F32)
        gbb = ctx.sb(stk, "gbbF", [1, 1024], F32)
        t1 = ctx.sb(stk, "t1F", [128, 512], F32)
        t2 = ctx.sb(stk, "t2F", [128, 512], F32)
        of = ctx.sb(stk, "ofF", [128, 2 * 512], F32)
        snapA = ctx.sb(stk, "snapA", [128, 512], F32)
        snapB = ctx.sb(stk, "snapB", [128, 512], F32)
        snapC = ctx.sb(stk, "snapC", [128, 512], F32)
        snapD = ctx.sb(stk, "snapD", [128, 512], F32)
        ps_sum = ctx.ps(stk, "ps_sumF", [128, 512], F32)
        ps_sq = ctx.ps(stk, "ps_sqF", [128, 512], F32)
        ps_sb = [ctx.ps(stk, f"ps_sbF{i}", [128, 1024], F32) for i in range(2)]
        p = Prog(ctx)
        a1 = p.dma("sync", g_row[:, :], ctx.rd("lng_in")[None, :], "al")
        a2 = p.dma("sync", b_row[:, :], ctx.rd("lnb_in")[None, :], "al")
        a3 = p.dma("sync", gb_row[:, :], ctx.rd("mod_rows")[1][None, 5 * 1024:6 * 1024], "al")
        p.add("vector", lambda v: v.tensor_tensor(gbg[:, :], g_row[:, :], gb_row[:, :], ALU.mult),
              waits=[("al", a1), ("al", a3)])
        p.add("vector", lambda v: v.tensor_tensor(gbb[:, :], b_row[:, :], gb_row[:, :], ALU.mult),
              waits=[("al", a2)])
        rowv = p.add("vector", lambda v: v.drain(), inc="dve")
        sdve = None
        h_done = {}
        st_done = {}
        for tc in range(NTC):
            nbase = (tc % 2) * T
            nwaits = []
            for cc in range(NCC):
                w = []
                if tc >= 2:
                    w.append(("dve", h_done[(tc - 2, NCC - 1)]))
                v = p.dma("sync", nt[:, nbase + cc * 512: nbase + (cc + 1) * 512],
                          ctx.rd("n5")[cc * 128:(cc + 1) * 128, tc * 512:(tc + 1) * 512],
                          "al", waits=w)
                nwaits.append(("al", v))
            nv = [nt[:, nbase + cc * 512: nbase + (cc + 1) * 512] for cc in range(NCC)]
            sdve = _emit_stats_tc(p, ctx, tc, nv, nwaits, sq, ps_sum, ps_sq,
                                  r_row, negmr_row, srow, 1e-6,
                                  prev_sdve=(sdve if tc > 0 else None))
            for cc in range(NCC):
                sl = cc % 2
                rslab = r_row[:, tc * 512:(tc + 1) * 512]
                nmr = negmr_row[:, tc * 512:(tc + 1) * 512]
                w = [("dve", sdve), ("dve", rowv)]
                key = (tc, cc - 2) if cc >= 2 else (tc - 1, NCC - 2 + cc)
                if key in h_done:
                    w.append(("dve", h_done[key]))
                p.add("tensor", lambda t, cc=cc, rslab=rslab: t.matmul(
                    ps_sb[cc % 2][:, 0:512], gbg[:, cc * 128:(cc + 1) * 128], rslab,
                    start=True, stop=True), waits=w)
                p.add("tensor", lambda t, cc=cc: t.matmul(
                    ps_sb[cc % 2][:, 512:1024], gbb[:, cc * 128:(cc + 1) * 128],
                    ctx.ones_row_f[:, :], start=True, stop=False))
                p.add("tensor", lambda t, cc=cc, nmr=nmr: t.matmul(
                    ps_sb[cc % 2][:, 512:1024], gbg[:, cc * 128:(cc + 1) * 128], nmr,
                    start=False, stop=True), inc="pe")
                pev = ctx.cnt["pe"]
                alx = p.dma("sync", xt[:, sl * 512:(sl + 1) * 512],
                            ctx.rd("xs")[cc * 128:(cc + 1) * 128, tc * 512:(tc + 1) * 512],
                            "al", waits=[("dve", h_done[key])] if key in h_done else [])
                p.add("vector", lambda v, cc=cc, nvc=nv[cc]: v.tensor_tensor(
                    t1[:, :], nvc, ps_sb[cc % 2][:, 0:512], ALU.mult),
                    waits=[("pe", pev)])
                p.add("vector", lambda v, cc=cc: v.drain())
                p.add("vector", lambda v, cc=cc: v.tensor_tensor(
                    t2[:, :], t1[:, :], ps_sb[cc % 2][:, 512:1024], ALU.add))
                p.add("vector", lambda v: v.drain())
                if DEBUG_FIN and tc == 0 and cc == 0:
                    p.add("vector", lambda v: v.tensor_copy(snapA[:, :], ps_sb[0][:, 0:512]))
                    p.add("vector", lambda v: v.tensor_copy(snapB[:, :], ps_sb[0][:, 512:1024]))
                    p.add("vector", lambda v: v.tensor_copy(snapC[:, :], t1[:, :]))
                    p.add("vector", lambda v: v.tensor_copy(snapD[:, :], t2[:, :]))
                    sdv = p.add("vector", lambda v: v.drain(), inc="dve")
                    p.dma("gpsimd", ctx.wr("dbg_a")[2:130, 0:512], snapA[:, :], "st", waits=[("dve", sdv)])
                    p.dma("gpsimd", ctx.wr("dbg_a")[130:258, 0:512], snapB[:, :], "st")
                    p.dma("gpsimd", ctx.wr("dbg_a")[258:386, 0:512], snapC[:, :], "st")
                    p.dma("gpsimd", ctx.wr("dbg_a")[386:514, 0:512], snapD[:, :], "st")
                sw = [("st", st_done[key])] if key in st_done else []
                p.add("vector", lambda v, sl=sl: v.tensor_tensor(
                    of[:, sl * 512:(sl + 1) * 512], t2[:, :], xt[:, sl * 512:(sl + 1) * 512],
                    ALU.add), waits=[("al", alx)] + sw)
                h_done[(tc, cc)] = p.add("vector", lambda v: v.drain(), inc="dve")
                st_done[(tc, cc)] = p.dma(
                    "sync", ctx.wr("out_f")[cc * 128:(cc + 1) * 128, tc * 512:(tc + 1) * 512],
                    of[:, sl * 512:(sl + 1) * 512], "st", waits=[("dve", h_done[(tc, cc)])])
        if DEBUG_FIN:
            p.dma("gpsimd", ctx.wr("dbg_a")[0:1, 0:T], r_row[:, :], "st",
                  waits=[("dve", h_done[(NTC - 1, NCC - 1)])])
            p.dma("gpsimd", ctx.wr("dbg_a")[1:2, 0:T], negmr_row[:, :], "st")
        p.add("sync", lambda hh: hh.nop(), waits=[("st", st_done[(NTC - 1, NCC - 1)])])
        p.play("l1f_fin")
    ctx.barrier()


STAGES_V1 = None


def make_stages():
    return [
        emit_xcast,
        emit_mod0,
        emit_mod1,
        make_ln_qkv_stage(0, True, None, "w_qkv0"),
        emit_l0a_attn,
        emit_l0f_wq,
        emit_l0f_main,
        make_ln_qkv_stage(1, False, 0.125, "w_qkv1"),
        emit_l1a_attn,
        emit_l1f_h,
    ] + [s for cN in range(6) for s in (make_l1f_nP(cN), make_l1f_nA(cN), make_l1f_nO(cN))] + [
        emit_l1f_fin,
    ]


def build_launch(stage_fns, force_out=()):
    nc = bass.Bass("TRN2", target_bir_lowering=False)
    reads, writes = set(), set()
    first = {}
    for fn in stage_fns:
        for n in getattr(fn, "reads", []):
            reads.add(n)
            first.setdefault(n, "r")
        for n in getattr(fn, "writes", []):
            writes.add(n)
            first.setdefault(n, "w")
    reads.add("ident_in")
    rd_aps, wr_aps = {}, {}
    for name in sorted(reads | writes):
        shape, dt = DRAM_REG[name]
        if name in EXTERNAL_IN:
            rd_aps[name] = nc.dram_tensor(name, shape, dt, kind="ExternalInput").ap()
            continue
        incoming = first[name] == "r"
        written = name in writes
        outgoing = written and (name == "out_f" or name in force_out)
        if incoming and outgoing:
            rd_aps[name] = nc.dram_tensor(name, shape, dt, kind="ExternalInput").ap()
            wr_aps[name] = nc.dram_tensor(name + "__o", shape, dt, kind="ExternalOutput").ap()
        elif incoming:
            ap = nc.dram_tensor(name, shape, dt, kind="ExternalInput").ap()
            rd_aps[name] = ap
            wr_aps[name] = ap
        elif outgoing:
            ap = nc.dram_tensor(name + "__o", shape, dt, kind="ExternalOutput").ap()
            rd_aps[name] = ap
            wr_aps[name] = ap
        else:
            ap = nc.dram_tensor(name, shape, dt, kind="Internal").ap()
            rd_aps[name] = ap
            wr_aps[name] = ap
    with contextlib.ExitStack() as stack:
        ctx = Ctx(nc, stack, rd_aps, wr_aps)
        emit_consts(ctx)
        for fn in stage_fns:
            fn(ctx)
    return nc


def host_prep(inputs, b):
    d = {}
    d["x_in"] = np.ascontiguousarray(inputs["L0_attn_x"][b].T)
    d["w_qkv0"] = np.ascontiguousarray(inputs["L0_attn_Wqkv"].T)
    d["w_proj0"] = np.ascontiguousarray(inputs["L0_attn_Wproj"].T)
    d["bproj0"] = inputs["L0_attn_bproj"]
    d["wg_in"] = inputs["L0_ffn_w_gate"]
    d["wu_in"] = inputs["L0_ffn_w_up"]
    d["wd_in"] = inputs["L0_ffn_w_down"]
    d["c0_in"] = np.ascontiguousarray(inputs["L0_c"][b])
    d["wmod0"] = inputs["L0_wmod"]
    wq1 = inputs["L1_attn_W_qkv"].reshape(C, 16, 64, 3).transpose(0, 3, 1, 2).reshape(C, 3072)
    d["w_qkv1"] = np.ascontiguousarray(wq1)
    d["w_out1"] = inputs["L1_attn_W_out"]
    d["c1_in"] = np.ascontiguousarray(inputs["L1_c"][b])
    d["wmod1"] = inputs["L1_wmod"]
    d["we_in"] = inputs["L1_ffn_We"]
    d["be_in"] = inputs["L1_ffn_be"]
    d["wq_in"] = inputs["L1_ffn_Wq"]
    d["wk_in"] = inputs["L1_ffn_Wk"]
    d["wv_in"] = inputs["L1_ffn_Wv"]
    d["wo_in"] = inputs["L1_ffn_Wo"]
    d["ew_in"] = np.ascontiguousarray(inputs["L1_ffn_ew"].reshape(-1))
    d["actw_in"] = inputs["L1_ffn_act_w"]
    d["lng_in"] = inputs["L1_ffn_ln_g"]
    d["lnb_in"] = inputs["L1_ffn_ln_b"]
    d["ident_in"] = np.eye(128, dtype=np.float32)
    return d


def kernel(**inputs):
    nc = build_launch(make_stages())
    in_maps = [host_prep(inputs, b) for b in range(8)]
    res = run_bass_kernel_spmd(nc, in_maps, core_ids=list(range(8)))
    out = np.stack([np.asarray(res.results[b]["out_f__o"]).T for b in range(8)])
    return np.ascontiguousarray(out.astype(np.float32))
